# version 09
# baseline (speedup 1.0000x reference)
import contextlib
import numpy as np
import ml_dtypes
import concourse.bass as bass
import concourse.mybir as mybir
from concourse.bass_utils import run_bass_kernel_spmd

F32 = mybir.dt.float32
BF16 = mybir.dt.bfloat16
AF = mybir.ActivationFunctionType
ALU = mybir.AluOpType
AX = mybir.AxisListType

NCORES = 8
T = 2048
D = 1024
DC = D // 128
NSL = 4
SL = 512

ENGINES = ("sync", "tensor", "vector", "scalar", "gpsimd")


class Prog:

    def __init__(self, name="prog", detect_races=True):
        self.nc = bass.Bass("TRN2", target_bir_lowering=False,
                            detect_race_conditions=detect_races)
        self.streams = {e: [] for e in ENGINES}
        self.counts = {}
        self.sems = {}
        self._semnames = []
        self._es = contextlib.ExitStack()
        self.last_compute = {e: None for e in ENGINES}
        self._dummy_tiles = self._es.enter_context(
            self.nc.sbuf_tensor("dummy_sem_tile", [1, 16], F32))
        for e in ENGINES:
            self.sem("B" + e)

    def sem(self, name):
        if name not in self.counts:
            self.counts[name] = 0
            self._semnames.append(name)
        return name

    def sbuf(self, name, shape, dtype=F32):
        es = self._scopes[-1] if getattr(self, "_scopes", None) else self._es
        return es.enter_context(self.nc.sbuf_tensor(name, shape, dtype))

    def push_scope(self):
        if not hasattr(self, "_scopes"):
            self._scopes = []
        self._scopes.append(contextlib.ExitStack())

    def pop_scope(self):
        self._scopes.pop().close()

    def psum(self, name, shape, dtype=F32):
        return self._es.enter_context(self.nc.psum_tensor(name, shape, dtype))

    def dram_in(self, name, shape, dtype=F32):
        return self.nc.dram_tensor(name, shape, dtype, kind="ExternalInput").ap()

    def dram_out(self, name, shape, dtype=F32):
        return self.nc.dram_tensor(name, shape, dtype, kind="ExternalOutput").ap()

    def dram_tmp(self, name, shape, dtype=F32):
        return self.nc.dram_tensor(name, shape, dtype, kind="Internal").ap()

    def set_phase(self, name):
        self._phase = name

    def op(self, eng, fn, real=True):
        rec = {"fn": fn, "incs": [], "eng": eng,
               "ph": getattr(self, "_phase", None)}
        self.streams[eng].append(rec)
        if real:
            self.last_compute[eng] = rec
        return rec

    def _dummy_op(self, eng):
        dt = self._dummy_tiles
        if eng == "tensor":
            return self.op("tensor", lambda t: t.nop())
        if eng == "vector":
            return self.op("vector", lambda v: v.memset(dt[:, 0:4], 0.0))
        if eng == "scalar":
            return self.op("scalar", lambda s: s.memzero(dt[:, 4:8]))
        if eng == "gpsimd":
            return self.op("gpsimd", lambda g: g.memset(dt[:, 8:12], 0.0))
        raise ValueError(eng)

    def inc(self, rec, sem, n=1):
        self.sem(sem)
        if rec["incs"]:
            rec = self._dummy_op(rec["eng"])
        rec["incs"].append((sem, n))
        self.counts[sem] += n
        return self.counts[sem]

    def wait(self, eng, sem, val=None):
        self.sem(sem)
        if val is None:
            val = self.counts[sem]
        if val <= 0:
            return
        def f(e, sem=sem, val=val):
            e.wait_ge(self.sems[sem], val)
        self.op(eng, f, real=False)

    def selfsync(self, eng):
        sem = "X" + eng
        self.sem(sem)
        v = self.inc(self.last_compute[eng], sem, 1)
        self.wait(eng, sem, v)

    def dma(self, eng, out, in_, sem="DS", **kw):
        self.sem(sem)
        def f(e, out=out, in_=in_, kw=kw):
            return e.dma_start(out=out, in_=in_, **kw)
        rec = {"fn": f, "incs": [], "eng": eng,
               "ph": getattr(self, "_phase", None)}
        self.streams[eng].append(rec)
        val = self.inc(rec, sem, 16)
        return val

    def barrier(self, engines=ENGINES):
        vals = {}
        for e in engines:
            if self.last_compute[e] is not None:
                self.inc(self.last_compute[e], "B" + e, 1)
                self.last_compute[e] = None
            vals[e] = self.counts["B" + e]
        for e in engines:
            for o in engines:
                if o != e and vals[o] > 0:
                    self.wait(e, "B" + o, vals[o])

    def finalize(self):
        for name in self._semnames:
            self.sems[name] = self._es.enter_context(self.nc.semaphore(name))
        with self.nc.Block() as block:
            for eng in ENGINES:
                def runner(e, eng=eng):
                    for rec in self.streams[eng]:
                        ph = rec.get("ph")
                        if ph is None:
                            ins = rec["fn"](e)
                            for (sem, n) in rec["incs"]:
                                ins.then_inc(self.sems[sem], n)
                        else:
                            with self.nc.named_scope(ph):
                                ins = rec["fn"](e)
                                for (sem, n) in rec["incs"]:
                                    ins.then_inc(self.sems[sem], n)
                getattr(block, eng)(runner)
        self._es.close()
        return self.nc



def emit_common_consts(P):
    C = {}
    C["idn_bf"] = P.sbuf("idn_bf", [128, 128], BF16)
    C["idn_f32"] = P.sbuf("idn_f32", [128, 128], F32)
    C["onescol_bf"] = P.sbuf("onescol_bf", [128, 1], BF16)
    C["ones1_f32"] = P.sbuf("ones1_f32", [1, 128], F32)
    C["epscol"] = P.sbuf("epscol", [1, 1], F32)

    P.op("gpsimd", lambda g: g.memset(C["idn_bf"][:], 1.0))
    P.selfsync("gpsimd")
    P.op("gpsimd", lambda g: g.affine_select(
        out=C["idn_bf"][:], in_=C["idn_bf"][:],
        pattern=[[-1, 128]], compare_op=ALU.is_equal,
        fill=0.0, base=0, channel_multiplier=1))
    P.op("gpsimd", lambda g: g.memset(C["idn_f32"][:], 1.0))
    P.selfsync("gpsimd")
    P.op("gpsimd", lambda g: g.affine_select(
        out=C["idn_f32"][:], in_=C["idn_f32"][:],
        pattern=[[-1, 128]], compare_op=ALU.is_equal,
        fill=0.0, base=0, channel_multiplier=1))
    P.op("gpsimd", lambda g: g.memset(C["onescol_bf"][:], 1.0))
    P.op("gpsimd", lambda g: g.memset(C["ones1_f32"][:], 1.0))
    P.op("gpsimd", lambda g: g.memset(C["epscol"][:], 1e-5))
    return C


def emit_load_xt_from_tokmajor(P, x_ap, XT, C, pp):
    with P.nc.sbuf_tensor("xtok_stage", [128, (T // 128) * D], F32) as Xtok:
        inap = x_ap.rearrange("(tc p) d -> p tc d", p=128)
        outap = Xtok[:].rearrange("p (tc d) -> p tc d", d=D)
        P.dma("sync", outap, inap, sem="DS")
        P.wait("tensor", "DS")
        ntc = T // 128
        nps = len(pp)
        XTv = XT[:].rearrange("p (c t) -> p c t", c=DC)
        P.sem("PSF")
        psf_base = P.counts["PSF"]
        fill = 0
        for tc in range(ntc):
            for q in range(2):
                slot = fill % nps
                if fill >= nps:
                    P.wait("tensor", "PSF", psf_base + fill - nps + 1)
                for j in range(4):
                    dc = 4 * q + j
                    def f(t, tc=tc, dc=dc, slot=slot, j=j):
                        return t.transpose(out=pp[slot][:, j * 128:(j + 1) * 128],
                                           in_=Xtok[:, tc * D + dc * 128: tc * D + (dc + 1) * 128],
                                           identity=C["idn_f32"][:])
                    rec = P.op("tensor", f)
                tv = P.inc(rec, "T", 1)
                P.wait("vector", "T", tv)
                def g(v, tc=tc, q=q, slot=slot):
                    return v.tensor_copy(
                        out=XTv[:, 4 * q:4 * q + 4, tc * 128:(tc + 1) * 128],
                        in_=pp[slot][:].rearrange("p (j t) -> p j t", j=4))
                rec2 = P.op("vector", g)
                P.inc(rec2, "PSF", 1)
                fill += 1
        P.barrier()


def emit_ln(P, XT, XN, C, pb1, pb2, pst1, pst2, xsq, ta, rows):
    ps1 = pst1[0:1, :]
    ps2 = pst2[0:1, :]
    negm, rrow, mrrow, msq, var = rows["negm"], rows["r"], rows["mr"], rows["msq"], rows["var"]
    for i in range(NSL):
        t0 = i * SL
        for c in range(DC):
            def f(v, c=c, t0=t0):
                return v.tensor_copy(out=XN[:, c * T + t0: c * T + t0 + SL],
                                     in_=XT[:, c * T + t0: c * T + t0 + SL])
            rec = P.op("vector", f)
        vcast = P.inc(rec, "V", 1)
        P.wait("tensor", "V", vcast)
        for c in range(DC):
            def f(t, c=c, t0=t0):
                return t.matmul(out=ps1, lhsT=C["onescol_bf"][:],
                                rhs=XN[:, c * T + t0: c * T + t0 + SL],
                                start=(c == 0), stop=(c == DC - 1))
            rec = P.op("tensor", f)
        ts1 = P.inc(rec, "T", 1)
        P.wait("scalar", "V", vcast)
        for c in range(DC):
            def f(s, c=c, t0=t0):
                return s.activation(out=xsq[c][:], in_=XN[:, c * T + t0: c * T + t0 + SL],
                                    func=AF.Square)
            rec = P.op("scalar", f)
            sv = P.inc(rec, "S", 1)
            P.wait("tensor", "S", sv)
            def f2(t, c=c):
                return t.matmul(out=ps2, lhsT=C["onescol_bf"][:], rhs=xsq[c][:],
                                start=(c == 0), stop=(c == DC - 1))
            rec2 = P.op("tensor", f2)
        ts2 = P.inc(rec2, "T", 1)
        P.wait("scalar", "T", ts1)
        rec = P.op("scalar", lambda s: s.activation(
            out=negm[:], in_=ps1, func=AF.Copy, scale=-1.0 / D))
        P.inc(rec, "S", 1)
        rec = P.op("scalar", lambda s: s.activation(
            out=msq[:], in_=negm[:], func=AF.Square))
        smsq = P.inc(rec, "S", 1)
        P.wait("vector", "T", ts2)
        P.wait("vector", "S", smsq)
        rec = P.op("vector", lambda v: v.scalar_tensor_tensor(
            out=var[:], in0=ps2, scalar=1.0 / D,
            in1=msq[:], op0=ALU.mult, op1=ALU.subtract))
        vvar = P.inc(rec, "V", 1)
        P.wait("scalar", "V", vvar)
        rec = P.op("scalar", lambda s: s.activation(
            out=var[:], in_=var[:], func=AF.Sqrt,
            bias=C["epscol"][:]))
        sstd = P.inc(rec, "S", 1)
        P.wait("vector", "S", sstd)
        P.op("vector", lambda v: v.reciprocal(
            out=rrow[:], in_=var[:]))
        rec = P.op("vector", lambda v: v.tensor_mul(
            out=mrrow[:], in0=negm[:], in1=rrow[:]))
        vfin = P.inc(rec, "V", 1)
        P.wait("tensor", "V", vfin)
        P.op("tensor", lambda t: t.matmul(
            out=pb1[:], lhsT=C["ones1_f32"][:], rhs=rrow[:],
            start=True, stop=True))
        rec = P.op("tensor", lambda t: t.matmul(
            out=pb2[:], lhsT=C["ones1_f32"][:], rhs=mrrow[:],
            start=True, stop=True))
        tb = P.inc(rec, "T", 1)
        P.wait("vector", "T", tb)
        for c in range(DC):
            def f(v, c=c, t0=t0):
                return v.tensor_mul(out=ta[:], in0=XT[:, c * T + t0: c * T + t0 + SL],
                                    in1=pb1[:])
            P.op("vector", f)
            def f2(v, c=c, t0=t0):
                return v.tensor_add(out=XN[:, c * T + t0: c * T + t0 + SL],
                                    in0=ta[:], in1=pb2[:])
            rec = P.op("vector", f2)
        vap = P.inc(rec, "V", 1)
        P.wait("tensor", "V", vap)
    P.barrier()



def emit_block_a(P, XT, XN, C, wqkv_ap, wout_ap, dftc_ap, dftsn_ap, pp):
    pq, pk, pv, pf0, pf1, pf2 = pp
    H = 8

    dstage = P.sbuf("dft_stage", [128, 256], F32)
    Cbf = P.sbuf("dft_c_bf", [128, 128], BF16)
    Snbf = P.sbuf("dft_sn_bf", [128, 128], BF16)
    P.dma("sync", dstage[:, 0:128], dftc_ap[:, :], sem="DS")
    P.dma("sync", dstage[:, 128:256], dftsn_ap[:, :], sem="DS")
    P.wait("vector", "DS")
    P.op("vector", lambda v: v.tensor_copy(out=Cbf[:], in_=dstage[:, 0:128]))
    rec = P.op("vector", lambda v: v.tensor_copy(out=Snbf[:], in_=dstage[:, 128:256]))
    P.inc(rec, "V", 1)

    wq = [P.sbuf(f"wq{i}", [128, 1024], BF16) for i in range(2)]
    wk = [P.sbuf(f"wk{i}", [128, 1024], BF16) for i in range(2)]
    wv = [P.sbuf(f"wv{i}", [128, 1024], BF16) for i in range(2)]
    woutbf = P.sbuf("woutbf", [128, DC * 1024], BF16)

    qb = P.sbuf("a_qb", [128, SL], BF16)
    kb = P.sbuf("a_kb", [128, SL], BF16)
    vb = P.sbuf("a_vb", [128, SL], BF16)
    f6 = [P.sbuf(f"a_f6_{i}", [128, SL], F32) for i in range(6)]
    kvre = P.sbuf("a_kvre", [128, SL], F32)
    kvim = P.sbuf("a_kvim", [128, SL], F32)
    cre = P.sbuf("a_cre", [128, SL], F32)
    cim = P.sbuf("a_cim", [128, SL], F32)
    tm = [P.sbuf(f"a_tm{i}", [128, SL], F32) for i in range(2)]
    zre = P.sbuf("a_zre", [128, SL], BF16)
    zim = P.sbuf("a_zim", [128, SL], BF16)
    carry = P.sbuf("a_carry", [128, 2], F32)
    vals = P.sbuf("a_vals", [128, H * T], BF16)

    def dma_wcol(buf, f0, sem):
        inap = wqkv_ap[:, f0:f0 + 128].rearrange("(c p) j -> p c j", p=128)
        outap = buf[:].rearrange("p (c j) -> p c j", c=DC)
        return P.dma("gpsimd", outap, inap, sem=sem)

    dma_wcol(wq[0], 0, "DW0")
    dma_wcol(wk[0], D, "DW0")
    dma_wcol(wv[0], 2 * D, "DW0")
    wout_in = wout_ap.rearrange("(c p) f -> p c f", p=128)
    wout_out = woutbf[:].rearrange("p (c f) -> p c f", c=DC)
    P.dma("gpsimd", wout_out, wout_in, sem="DWO")

    for h in range(H):
        if h + 1 < H:
            sem = "DW" + str((h + 1) % 2)
            dma_wcol(wq[(h + 1) % 2], (h + 1) * 128, sem)
            dma_wcol(wk[(h + 1) % 2], D + (h + 1) * 128, sem)
            dma_wcol(wv[(h + 1) % 2], 2 * D + (h + 1) * 128, sem)
        P.wait("tensor", "DW" + str(h % 2), 48 * (h // 2 + 1))
        rec = P.op("vector", lambda v: v.memset(carry[:], 0.0))
        P.inc(rec, "V", 1)
        for i in range(NSL):
            t0 = i * SL
            wqb, wkb, wvb = wq[h % 2], wk[h % 2], wv[h % 2]
            for (psumt, wbuf) in ((pq, wqb), (pk, wkb), (pv, wvb)):
                for c in range(DC):
                    def f(t, psumt=psumt, wbuf=wbuf, c=c, t0=t0):
                        return t.matmul(out=psumt[:],
                                        lhsT=wbuf[:, c * 128:(c + 1) * 128],
                                        rhs=XN[:, c * T + t0: c * T + t0 + SL],
                                        start=(c == 0), stop=(c == DC - 1))
                    rec = P.op("tensor", f)
            tqkv = P.inc(rec, "T", 1)
            P.wait("scalar", "T", tqkv)
            P.op("scalar", lambda s: s.activation(out=qb[:], in_=pq[:], func=AF.Copy))
            P.op("scalar", lambda s: s.activation(out=kb[:], in_=pk[:], func=AF.Copy))
            rec = P.op("scalar", lambda s: s.activation(out=vb[:], in_=pv[:], func=AF.Copy))
            sev = P.inc(rec, "S", 1)
            P.wait("tensor", "S", sev)
            plan = [(pq, Cbf, qb), (pk, Snbf, qb), (pv, Cbf, kb),
                    (pf0, Snbf, kb), (pf1, Cbf, vb), (pf2, Snbf, vb)]
            for (psumt, mat, src) in plan:
                def f(t, psumt=psumt, mat=mat, src=src):
                    return t.matmul(out=psumt[:], lhsT=mat[:], rhs=src[:],
                                    start=True, stop=True)
                rec = P.op("tensor", f)
            tdft = P.inc(rec, "T", 1)
            P.wait("scalar", "T", tdft)
            P.wait("vector", "T", tdft)
            for (psumt, idx, eng) in [(pq, 0, "scalar"), (pk, 1, "vector"),
                                      (pv, 2, "scalar"), (pf0, 3, "vector"),
                                      (pf1, 4, "scalar"), (pf2, 5, "vector")]:
                if eng == "scalar":
                    def f(s, psumt=psumt, idx=idx):
                        return s.activation(out=f6[idx][:], in_=psumt[:], func=AF.Copy)
                else:
                    def f(v, psumt=psumt, idx=idx):
                        return v.tensor_copy(out=f6[idx][:], in_=psumt[:])
                P.op(eng, f)
            sdft = P.inc(P.last_compute["scalar"], "S", 1)
            vdft = P.inc(P.last_compute["vector"], "V", 1)
            P.wait("gpsimd", "S", sdft)
            P.wait("gpsimd", "V", vdft)
            P.op("gpsimd", lambda g: g.tensor_mul(out=tm[0][:], in0=f6[2][:], in1=f6[4][:]))
            P.op("gpsimd", lambda g: g.tensor_mul(out=tm[1][:], in0=f6[3][:], in1=f6[5][:]))
            P.op("gpsimd", lambda g: g.tensor_sub(out=kvre[:], in0=tm[0][:], in1=tm[1][:]))
            P.op("gpsimd", lambda g: g.tensor_mul(out=tm[0][:], in0=f6[2][:], in1=f6[5][:]))
            P.op("gpsimd", lambda g: g.tensor_mul(out=tm[1][:], in0=f6[3][:], in1=f6[4][:]))
            rec = P.op("gpsimd", lambda g: g.tensor_add(out=kvim[:], in0=tm[0][:], in1=tm[1][:]))
            gkv = P.inc(rec, "G", 1)
            P.wait("vector", "G", gkv)
            P.op("vector", lambda v: v.tensor_tensor_scan(
                out=cre[:], data0=kvre[:], data1=kvre[:], initial=0.0,
                op0=ALU.add, op1=ALU.bypass))
            P.selfsync("vector")
            P.op("vector", lambda v: v.tensor_add(
                out=cre[:], in0=cre[:], in1=carry[:, 0:1].broadcast_to([128, SL])))
            P.op("vector", lambda v: v.tensor_tensor_scan(
                out=cim[:], data0=kvim[:], data1=kvim[:], initial=0.0,
                op0=ALU.add, op1=ALU.bypass))
            P.selfsync("vector")
            P.op("vector", lambda v: v.tensor_add(
                out=cim[:], in0=cim[:], in1=carry[:, 1:2].broadcast_to([128, SL])))
            P.op("vector", lambda v: v.tensor_copy(out=carry[:, 0:1], in_=cre[:, SL - 1:SL]))
            rec = P.op("vector", lambda v: v.tensor_copy(out=carry[:, 1:2], in_=cim[:, SL - 1:SL]))
            vscan = P.inc(rec, "V", 1)
            P.wait("gpsimd", "V", vscan)
            P.op("gpsimd", lambda g: g.tensor_mul(out=tm[0][:], in0=cre[:], in1=f6[0][:]))
            P.op("gpsimd", lambda g: g.tensor_mul(out=tm[1][:], in0=cim[:], in1=f6[1][:]))
            P.op("gpsimd", lambda g: g.tensor_add(out=zre[:], in0=tm[0][:], in1=tm[1][:]))
            P.op("gpsimd", lambda g: g.tensor_mul(out=tm[0][:], in0=cim[:], in1=f6[0][:]))
            P.op("gpsimd", lambda g: g.tensor_mul(out=tm[1][:], in0=cre[:], in1=f6[1][:]))
            rec = P.op("gpsimd", lambda g: g.tensor_sub(out=zim[:], in0=tm[0][:], in1=tm[1][:]))
            gz = P.inc(rec, "G", 1)
            P.wait("tensor", "G", gz)
            P.op("tensor", lambda t: t.matmul(out=pq[:], lhsT=Cbf[:], rhs=zre[:],
                                              start=True, stop=False))
            rec = P.op("tensor", lambda t: t.matmul(out=pq[:], lhsT=Snbf[:], rhs=zim[:],
                                                    start=False, stop=True))
            tifft = P.inc(rec, "T", 1)
            P.wait("scalar", "T", tifft)
            rec = P.op("scalar", lambda s, h=h, t0=t0: s.activation(
                out=vals[:, h * T + t0: h * T + t0 + SL], in_=pq[:], func=AF.Copy,
                scale=1.0 / 128.0))
            P.inc(rec, "S", 1)
            P.barrier(("tensor", "vector", "scalar", "gpsimd"))

    P.wait("tensor", "DWO")
    for fc in range(DC):
        for i in range(NSL):
            t0 = i * SL
            for c in range(DC):
                def f(t, fc=fc, c=c, t0=t0):
                    return t.matmul(out=pk[:],
                                    lhsT=woutbf[:, c * 1024 + fc * 128: c * 1024 + (fc + 1) * 128],
                                    rhs=vals[:, c * T + t0: c * T + t0 + SL],
                                    start=(c == 0), stop=(c == DC - 1))
                rec = P.op("tensor", f)
            tw = P.inc(rec, "T", 1)
            P.wait("vector", "T", tw)
            def f2(v, fc=fc, t0=t0):
                return v.tensor_add(out=XT[:, fc * T + t0: fc * T + t0 + SL],
                                    in0=pk[:], in1=XT[:, fc * T + t0: fc * T + t0 + SL])
            rec = P.op("vector", f2)
            vres = P.inc(rec, "V", 1)
            P.wait("tensor", "V", vres)
    P.barrier()



def emit_block_b(P, XT, XN, C, rw_ap, rb_ap, w1_ap, b1_ap, w2_ap, b2_ap, pp, dbg=None):
    NE = 8
    HID = 4096
    NPAIR = 32

    w2scr = P.dram_tmp("w2scr", [D, HID], BF16)
    hgscr = P.dram_tmp("hgscr", [128, NPAIR * 1024], BF16)

    weT = P.sbuf("b_weT", [8, T], F32)
    W8 = P.sbuf("b_W8", [128, 16 * 8], F32)

    with contextlib.ExitStack() as rscope:
        sb = lambda n, s, d=F32: rscope.enter_context(P.nc.sbuf_tensor(n, s, d))
        rw = sb("b_rw", [8, D])
        rwb = sb("b_rwb", [8, D], BF16)
        rbcol = sb("b_rb", [8, 1])
        rTb = sb("b_rTb", [128, 8 * 8], BF16)
        logitsT = sb("b_logitsT", [8, T])
        m1c = sb("b_m1c", [128, 1]); m2c = sb("b_m2c", [128, 1])
        nm1c = sb("b_nm1c", [128, 1]); sc8 = sb("b_sc8", [128, 8])
        E8 = sb("b_E8", [128, 8]); G8 = sb("b_G8", [128, 8])
        ed1 = sb("b_ed1", [128, 1]); r1 = sb("b_r1", [128, 1])

        P.dma("sync", rw[:], rw_ap[:, :], sem="DS")
        P.dma("sync", rbcol[:], rb_ap.rearrange("(e o) -> e o", o=1), sem="DS")
        P.wait("vector", "DS")
        rec = P.op("vector", lambda v: v.tensor_copy(out=rwb[:], in_=rw[:]))
        vr = P.inc(rec, "V", 1)
        P.wait("tensor", "V", vr)
        for c in range(DC):
            def f(t, c=c):
                return t.transpose(out=pp[c % 2][:, 0:4].bitcast(BF16),
                                   in_=rwb[0:8, c * 128:(c + 1) * 128],
                                   identity=C["idn_bf"][0:8, 0:8])
            rec = P.op("tensor", f)
            tv = P.inc(rec, "T", 1)
            P.wait("vector", "T", tv)
            def g(v, c=c):
                return v.tensor_copy(out=rTb[:, c * 8:(c + 1) * 8],
                                     in_=pp[c % 2][:, 0:4].bitcast(BF16))
            rec = P.op("vector", g)
            vv = P.inc(rec, "V", 1)
            P.wait("tensor", "V", vv)
        for tt in range(NSL):
            t0 = tt * SL
            for c in range(DC):
                def f(t, c=c, t0=t0):
                    return t.matmul(out=pp[2][0:8, :], lhsT=rTb[:, c * 8:(c + 1) * 8],
                                    rhs=XN[:, c * T + t0: c * T + t0 + SL],
                                    start=(c == 0), stop=(c == DC - 1))
                rec = P.op("tensor", f)
            tl = P.inc(rec, "T", 1)
            P.wait("vector", "T", tl)
            def f2(v, t0=t0):
                return v.tensor_scalar(out=logitsT[:, t0:t0 + SL], in0=pp[2][0:8, :],
                                       scalar1=rbcol[:], scalar2=None, op0=ALU.add)
            rec = P.op("vector", f2)
            sl_ = P.inc(rec, "V", 1)
            P.wait("tensor", "V", sl_)
        if dbg is not None:
            P.wait("sync", "V")
            P.dma("sync", dbg["logitsT"], logitsT[:], sem="DO")
        for tc in range(16):
            def f(t, tc=tc):
                return t.transpose(out=pp[3][:, 0:8],
                                   in_=logitsT[0:8, tc * 128:(tc + 1) * 128],
                                   identity=C["idn_f32"][0:8, 0:8])
            rec = P.op("tensor", f)
            tv = P.inc(rec, "T", 1)
            P.wait("vector", "T", tv)
            L8 = pp[3][:, 0:8]
            P.op("vector", lambda v, L8=L8: v.reduce_max(out=m1c[:], in_=L8, axis=AX.X))
            P.selfsync("vector")
            P.op("vector", lambda v, L8=L8: v.tensor_sub(
                out=sc8[:], in0=L8, in1=m1c[:].broadcast_to([128, 8])))
            P.selfsync("vector")
            P.op("vector", lambda v: v.tensor_scalar(
                out=G8[:], in0=sc8[:], scalar1=0.0, scalar2=None, op0=ALU.is_ge))
            P.selfsync("vector")
            rec = P.op("vector", lambda v, L8=L8: v.scalar_tensor_tensor(
                out=E8[:], in0=G8[:], scalar=-1e30, in1=L8,
                op0=ALU.mult, op1=ALU.add))
            P.selfsync("vector")
            P.op("vector", lambda v: v.reduce_max(out=m2c[:], in_=E8[:], axis=AX.X))
            P.selfsync("vector")
            rec = P.op("vector", lambda v: v.tensor_sub(
                out=nm1c[:], in0=m2c[:], in1=m1c[:]))
            vt = P.inc(rec, "V", 1)
            P.wait("scalar", "V", vt)
            P.op("scalar", lambda s: s.activation(out=E8[:], in_=sc8[:], func=AF.Exp))
            P.selfsync("scalar")
            rec = P.op("scalar", lambda s: s.activation(
                out=ed1[:], in_=nm1c[:], func=AF.Exp))
            st_ = P.inc(rec, "S", 1)
            P.wait("vector", "S", st_)
            P.op("vector", lambda v: v.tensor_scalar_add(out=ed1[:], in0=ed1[:], scalar1=1.0))
            P.selfsync("vector")
            P.op("vector", lambda v: v.reciprocal(out=r1[:], in_=ed1[:]))
            P.selfsync("vector")
            P.op("vector", lambda v, L8=L8: v.tensor_tensor(
                out=G8[:], in0=L8, in1=m2c[:].broadcast_to([128, 8]), op=ALU.is_ge))
            P.selfsync("vector")
            P.op("vector", lambda v: v.tensor_mul(out=E8[:], in0=E8[:], in1=G8[:]))
            P.selfsync("vector")
            rec = P.op("vector", lambda v, tc=tc: v.tensor_mul(
                out=W8[:, tc * 8:(tc + 1) * 8], in0=E8[:],
                in1=r1[:].broadcast_to([128, 8])))
            vt2 = P.inc(rec, "V", 1)
            P.wait("tensor", "V", vt2)
            def f3(t, tc=tc):
                return t.transpose(out=pp[4][0:8, 0:128],
                                   in_=W8[:, tc * 8:(tc + 1) * 8],
                                   identity=C["idn_f32"][:])
            rec = P.op("tensor", f3)
            tw = P.inc(rec, "T", 1)
            P.wait("vector", "T", tw)
            def f4(v, tc=tc):
                return v.tensor_copy(out=weT[0:8, tc * 128:(tc + 1) * 128],
                                     in_=pp[4][0:8, 0:128])
            rec = P.op("vector", f4)
            vt3 = P.inc(rec, "V", 1)
            P.wait("tensor", "V", vt3)
        P.barrier()

    werow = P.sbuf("b_werow", [1, SL], F32)
    Wb = P.sbuf("b_Wb", [128, T], F32)
    b1s = P.sbuf("b_b1s", [64, 128], F32)
    b2s = P.sbuf("b_b2s", [8, 128], F32)
    b1col = P.sbuf("b_b1col", [128, 64], F32)
    b2col = P.sbuf("b_b2col", [128, 8], F32)
    w1r = [P.sbuf(f"b_w1r{i}", [128, 2 * 1024], BF16) for i in range(2)]
    w1t = [P.sbuf(f"b_w1t{i}", [128, 2 * 1024], BF16) for i in range(2)]
    w2t = [P.sbuf(f"b_w2t{i}", [128, 1024], BF16) for i in range(2)]
    sg = P.sbuf("b_sg", [128, 2 * SL], F32)
    hg = P.sbuf("b_hg", [128, NPAIR * 1024], BF16)
    hgtmp = P.sbuf("b_hgtmp", [128, 1024], BF16)

    pA, pB, ptr = pp[0], pp[1], pp[2]
    for s in ("DW1", "DC2", "DT2", "DHG", "DHGR"):
        P.sem(s)

    nw1 = [0]
    ttr = {}
    mm1evac = {}

    w1p_done = {}
    def dma_w1pair(e_, c_):
        par = nw1[0] % 2
        for (k, fc) in ((0, c_), (1, c_ + 32)):
            inap = w1_ap[e_, fc * 128:(fc + 1) * 128, :]
            v = P.dma("gpsimd", w1r[par][:, k * 1024:(k + 1) * 1024], inap,
                      sem="DWP" + str(par))
        w1p_done[nw1[0]] = ("DWP" + str(par), v)
        nw1[0] += 1

    def emit_w1pair_transpose(ldidx):
        par = ldidx % 2
        P.wait("tensor", w1p_done[ldidx][0], w1p_done[ldidx][1])
        for k in range(2):
            for j in range(8):
                def f(t, par=par, k=k, j=j):
                    return t.transpose(out=ptr[:, j * 64:(j + 1) * 64].bitcast(BF16),
                                       in_=w1r[par][:, k * 1024 + j * 128: k * 1024 + (j + 1) * 128],
                                       identity=C["idn_bf"][:])
                rec = P.op("tensor", f)
            tv = P.inc(rec, "T", 1)
            P.wait("vector", "T", tv)
            def g(v, par=par, k=k):
                return v.tensor_copy(out=w1t[par][:, k * 1024:(k + 1) * 1024],
                                     in_=ptr[:].bitcast(BF16))
            rec = P.op("vector", g)
            vv = P.inc(rec, "V", 1)
            P.wait("tensor", "V", vv)
        ttr[ldidx] = P.counts["T"]

    for e in range(NE):
        if getattr(P, "_phase", None) is not None:
            P.set_phase(f"b_e{e}")
        P.wait("gpsimd", "DT20")
        P.wait("gpsimd", "DT21")
        P.dma("gpsimd", w2scr, w2_ap[e, :, :], sem="DC2")
        P.dma("sync", b1s[:], b1_ap[e, :].rearrange("(fc j) -> fc j", j=128), sem="DS")
        P.dma("sync", b2s[:], b2_ap[e, :].rearrange("(dc j) -> dc j", j=128), sem="DS")
        P.wait("tensor", "DS")
        P.op("tensor", lambda t: t.transpose(out=ptr[:, 0:64], in_=b1s[:],
                                             identity=C["idn_f32"][0:64, 0:64]))
        P.op("tensor", lambda t: t.transpose(out=ptr[:, 64:72], in_=b2s[:],
                                             identity=C["idn_f32"][0:8, 0:8]))
        for tt in range(NSL):
            dg = P.dma("gpsimd", werow[:], weT[e:e + 1, tt * SL:(tt + 1) * SL], sem="DG")
            P.wait("tensor", "DG", dg)
            def fwb(t, tt=tt):
                return t.matmul(out=pA[:], lhsT=C["ones1_f32"][:],
                                rhs=werow[:],
                                start=True, stop=True)
            rec2 = P.op("tensor", fwb)
            tb = P.inc(rec2, "T", 1)
            P.wait("vector", "T", tb)
            def gwb(v, tt=tt):
                return v.tensor_copy(out=Wb[:, tt * SL:(tt + 1) * SL], in_=pA[:])
            rec2 = P.op("vector", gwb)
            vwb = P.inc(rec2, "V", 1)
            P.wait("gpsimd", "V", vwb)
        P.op("vector", lambda v: v.tensor_copy(out=b1col[:], in_=ptr[:, 0:64]))
        rec2 = P.op("vector", lambda v: v.tensor_copy(out=b2col[:], in_=ptr[:, 64:72]))
        vbias = P.inc(rec2, "V", 1)
        P.wait("tensor", "V", vbias)

        if e == 0:
            dma_w1pair(0, 0)
            emit_w1pair_transpose(0)
        for c in range(NPAIR):
            ldidx = e * NPAIR + c
            nxt = (e, c + 1) if c + 1 < NPAIR else (e + 1, 0)
            if nxt[0] < NE:
                if ldidx >= 1:
                    P.wait("gpsimd", "T", ttr[ldidx - 1])
                dma_w1pair(nxt[0], nxt[1])
            par = ldidx % 2
            for p2 in range(2):
                bk = pp[0:4] if p2 == 0 else pp[4:8]
                if ldidx >= 1:
                    P.wait("tensor", "V", mm1evac[(ldidx - 1, p2)])
                for dc in range(DC):
                    for k in range(2):
                        for tl in range(2):
                            tt = 2 * p2 + tl
                            def f(t, k=k, dc=dc, tl=tl, tt=tt, par=par, bk=bk):
                                return t.matmul(
                                    out=bk[2 * k + tl][:],
                                    lhsT=w1t[par][:, k * 1024 + dc * 128: k * 1024 + (dc + 1) * 128],
                                    rhs=XN[:, dc * T + tt * SL: dc * T + (tt + 1) * SL],
                                    start=(dc == 0), stop=(dc == DC - 1))
                            rec = P.op("tensor", f)
                tmm = P.inc(rec, "T", 1)
                P.wait("scalar", "T", tmm)
                for tl in range(2):
                    def fsilu(s, c=c, tl=tl, bk=bk):
                        return s.activation(out=sg[:, tl * SL:(tl + 1) * SL],
                                            in_=bk[tl][:], func=AF.Silu,
                                            bias=b1col[:, c:c + 1])
                    rec = P.op("scalar", fsilu)
                ssi = P.inc(rec, "S", 1)
                P.wait("vector", "S", ssi)
                for tl in range(2):
                    tt = 2 * p2 + tl
                    if tt < 2:
                        dst = hg[:, c * 1024 + tt * SL: c * 1024 + (tt + 1) * SL]
                    else:
                        if tt == 2:
                            P.wait("vector", "DHG", 16 * (e * NPAIR + c))
                        dst = hgtmp[:, (tt - 2) * SL: (tt - 1) * SL]
                    def fgate(v, c=c, dst=dst, tl=tl, bk=bk):
                        return v.scalar_tensor_tensor(
                            out=dst, in0=bk[2 + tl][:],
                            scalar=b1col[:, 32 + c:33 + c],
                            in1=sg[:, tl * SL:(tl + 1) * SL],
                            op0=ALU.add, op1=ALU.mult)
                    rec = P.op("vector", fgate)
                mm1evac[(ldidx, p2)] = P.inc(rec, "V", 1)
            P.wait("gpsimd", "V")
            P.dma("gpsimd", hgscr[:, c * 1024:(c + 1) * 1024], hgtmp[:], sem="DHG")
            if nxt[0] < NE:
                P.wait("tensor", "V", mm1evac[(ldidx, 0)])
                emit_w1pair_transpose(ldidx + 1)

        P.wait("tensor", "V")
        P.wait("sync", "DC2")
        tmj = {}
        for hi in range(2):
            if hi == 1:
                P.wait("gpsimd", "T")
                P.wait("gpsimd", "DHG", 16 * NPAIR * (e + 1))
                P.dma("gpsimd", hg[:], hgscr[:, :], sem="DHGR")
                P.wait("tensor", "DHGR")
            for tt in range(2):
                for c in range(NPAIR):
                    j = (hi * 2 + tt) * NPAIR + c
                    if j >= 2:
                        P.wait("sync", "T", tmj[j - 2])
                    jd = P.dma("sync", w2t[j % 2][:], w2scr[:, c * 128:(c + 1) * 128],
                               sem="DT2" + str(j % 2), transpose=True)
                    P.wait("tensor", "DT2" + str(j % 2), jd)
                    for dout in range(DC):
                        def f(t, j=j, dout=dout, c=c, tt=tt):
                            return t.matmul(
                                out=pp[dout][:],
                                lhsT=w2t[j % 2][:, dout * 128:(dout + 1) * 128],
                                rhs=hg[:, c * 1024 + tt * SL: c * 1024 + (tt + 1) * SL],
                                start=(c == 0), stop=(c == NPAIR - 1))
                        rec = P.op("tensor", f)
                    tmj[j] = P.inc(rec, "T", 1)
                P.wait("vector", "T")
                t0 = (hi * 2 + tt) * SL
                for dout in range(DC):
                    def fo(v, dout=dout, t0=t0):
                        return v.scalar_tensor_tensor(
                            out=sg[:, 0:SL], in0=pp[dout][:], scalar=b2col[:, dout:dout + 1],
                            in1=Wb[:, t0:t0 + SL], op0=ALU.add, op1=ALU.mult)
                    P.op("vector", fo)
                    def fr(v, dout=dout, t0=t0):
                        return v.tensor_add(out=XT[:, dout * T + t0: dout * T + t0 + SL],
                                            in0=sg[:, 0:SL],
                                            in1=XT[:, dout * T + t0: dout * T + t0 + SL])
                    rec = P.op("vector", fr)
                vo = P.inc(rec, "V", 1)
                P.wait("tensor", "V", vo)
        P.barrier(("tensor", "vector", "scalar", "gpsimd"))
    P.barrier()
    if dbg is not None:
        P.dma("sync", dbg["W8"], W8[:], sem="DO")
        P.dma("sync", dbg["weT"], weT[:], sem="DO")
        P.dma("sync", dbg["hg"], hg[:], sem="DO")
        P.dma("sync", dbg["Wb"], Wb[:], sem="DO")



def emit_block_c(P, XT, XN, C, wav_ap, wv_ap, wvb_ap, wo_ap, wob_ap, pp, dbg=None):
    H = 16
    HD = 64

    woT = P.sbuf("c_woT", [128, DC * 1024], BF16)
    wavb = P.sbuf("c_wavb", [128, H * 64], BF16)
    bias_col = P.sbuf("c_bias", [128, 8], F32)
    vx65 = P.sbuf("c_vx65", [128, 16 * (H * 65)], BF16)
    o_bf = P.sbuf("c_obf", [128, DC * T], BF16)
    nrmmax = P.sbuf("c_nrmmax", [1, 16], F32)
    sml = P.sbuf("c_sml", [1, 16], F32)
    sqs = P.sbuf("c_sqs", [128, SL], BF16)

    with contextlib.ExitStack() as ws:
        sb = lambda n, sh, d=F32: ws.enter_context(P.nc.sbuf_tensor(n, sh, d))
        wvT = sb("c_wvT", [128, DC * 1024], BF16)
        wstage = sb("c_wstage", [128, 1024], BF16)
        vbs = sb("c_vbs", [8, 128], F32)
        obs = sb("c_obs", [8, 128], F32)
        for (w_ap, dstT) in ((wv_ap, wvT), (wo_ap, woT)):
            for c in range(DC):
                P.wait("gpsimd", "T")
                dw = P.dma("gpsimd", wstage[:], w_ap[c * 128:(c + 1) * 128, :], sem="DG")
                P.wait("tensor", "DG", dw)
                for j in range(8):
                    def f(t, j=j):
                        return t.transpose(out=pp[0][:, j * 64:(j + 1) * 64].bitcast(BF16),
                                           in_=wstage[:, j * 128:(j + 1) * 128],
                                           identity=C["idn_bf"][:])
                    rec = P.op("tensor", f)
                tv = P.inc(rec, "T", 1)
                P.wait("vector", "T", tv)
                def g(v, c=c, dstT=dstT):
                    return v.tensor_copy(
                        out=dstT[:].rearrange("p (dc e) -> p dc e", dc=DC)[:, :, c * 128:(c + 1) * 128],
                        in_=pp[0][:].bitcast(BF16).rearrange("p (dc e) -> p dc e", dc=DC))
                rec = P.op("vector", g)
                vv = P.inc(rec, "V", 1)
                P.wait("tensor", "V", vv)
        wavs = sb("c_wavs", [128, H * 64], F32)
        P.dma("sync", wavs[0:64, :].rearrange("d (h e) -> d h e", h=H),
              wav_ap.rearrange("h d e -> d h e"), sem="DS")
        dw = P.dma("sync", wavs[64:128, :].rearrange("d (h e) -> d h e", h=H),
                   wav_ap.rearrange("h d e -> d h e"), sem="DS")
        P.wait("vector", "DS", dw)
        rec = P.op("vector", lambda v: v.tensor_copy(out=wavb[:], in_=wavs[:]))
        P.inc(rec, "V", 1)
        P.dma("sync", vbs[:], wvb_ap.rearrange("(a j) -> a j", j=128), sem="DS")
        dw = P.dma("sync", obs[:], wob_ap.rearrange("(a j) -> a j", j=128), sem="DS")
        P.wait("tensor", "DS", dw)
        P.op("tensor", lambda t: t.transpose(out=pp[1][:, 0:8], in_=vbs[:],
                                             identity=C["idn_f32"][0:8, 0:8]))
        rec = P.op("tensor", lambda t: t.transpose(out=pp[1][:, 8:16], in_=obs[:],
                                                   identity=C["idn_f32"][0:8, 0:8]))
        tv = P.inc(rec, "T", 1)
        P.wait("vector", "T", tv)
        wvbc = sb("c_wvbc", [128, 8], BF16)
        wobc = sb("c_wobc", [128, 8], F32)
        P.op("vector", lambda v: v.tensor_copy(out=wvbc[:], in_=pp[1][:, 0:8]))
        rec = P.op("vector", lambda v: v.tensor_copy(out=wobc[:], in_=pp[1][:, 8:16]))
        vv = P.inc(rec, "V", 1)
        P.wait("tensor", "V", vv)
        for fc in range(DC):
            for ec in range(DC):
                def f(t, fc=fc, ec=ec):
                    return t.matmul(out=pp[2][:, 0:1],
                                    lhsT=woT[:, ec * 1024 + fc * 128:(ec * 1024) + (fc + 1) * 128],
                                    rhs=wvbc[:, ec:ec + 1],
                                    start=(ec == 0), stop=(ec == DC - 1))
                rec = P.op("tensor", f)
            tb_ = P.inc(rec, "T", 1)
            P.wait("vector", "T", tb_)
            def g(v, fc=fc):
                return v.tensor_add(out=bias_col[:, fc:fc + 1], in0=pp[2][:, 0:1],
                                    in1=wobc[:, fc:fc + 1])
            rec = P.op("vector", g)
            vv = P.inc(rec, "V", 1)
            P.wait("tensor", "V", vv)

        v65 = vx65[:].rearrange("p (sc h e) -> p sc h e", sc=16, h=H)
        rec = P.op("gpsimd", lambda g: g.memset(v65[:, :, :, 64:65], 1.0))
        P.inc(rec, "G", 1)
        for sc_ in range(16):
            for et in range(2):
                for dc in range(DC):
                    def f(t, sc_=sc_, et=et, dc=dc):
                        return t.matmul(out=pp[4][:],
                                        lhsT=XN[:, dc * T + sc_ * 128: dc * T + (sc_ + 1) * 128],
                                        rhs=wvT[:, dc * 1024 + et * 512: dc * 1024 + (et + 1) * 512],
                                        start=(dc == 0), stop=(dc == DC - 1))
                    rec = P.op("tensor", f)
                tv = P.inc(rec, "T", 1)
                P.wait("scalar", "T", tv)
                def g(s, sc_=sc_, et=et):
                    return s.activation(
                        out=v65[:, sc_, et * 8:(et + 1) * 8, 0:64],
                        in_=pp[4][:].rearrange("p (h e) -> p h e", e=64), func=AF.Copy)
                rec = P.op("scalar", g)
                sv_ = P.inc(rec, "S", 1)
                P.wait("tensor", "S", sv_)
        rec = P.op("vector", lambda v: v.memset(nrmmax[:], 0.0))
        P.inc(rec, "V", 1)
        for dc in range(DC):
            for tt in range(NSL):
                t0 = tt * SL
                def fsq(s, dc=dc, t0=t0):
                    return s.activation(out=sqs[:], in_=XN[:, dc * T + t0: dc * T + t0 + SL],
                                        func=AF.Square)
                rec = P.op("scalar", fsq)
                sv_ = P.inc(rec, "S", 1)
                P.wait("tensor", "S", sv_)
                def f(t):
                    return t.matmul(out=pp[5][0:1, :], lhsT=C["onescol_bf"][:], rhs=sqs[:],
                                    start=True, stop=True)
                rec = P.op("tensor", f)
                tv = P.inc(rec, "T", 1)
                P.wait("vector", "T", tv)
                P.op("vector", lambda v: v.reduce_max(out=sml[:, 8:9], in_=pp[5][0:1, :],
                                                      axis=AX.X))
                P.selfsync("vector")
                def g(v, dc=dc):
                    return v.tensor_max(out=nrmmax[:, dc:dc + 1], in0=nrmmax[:, dc:dc + 1],
                                        in1=sml[0:1, 8:9])
                rec = P.op("vector", g)
                P.selfsync("vector")
                vv = P.inc(rec, "V", 1)
                P.wait("tensor", "V", vv)
        P.barrier()
        if dbg is not None:
            P.dma("sync", dbg["vx65e"], vx65[:], sem="DO")
            P.wait("sync", "DO")

    P.barrier()

    dbgsml = P.sbuf("c_dbgsml", [1, 16 * 16], F32) if dbg is not None else None
    otmp = P.sbuf("c_otmp", [64, SL], BF16)
    xwT = P.sbuf("c_xwT", [128, T], BF16)
    ptmp = [P.sbuf(f"c_pt{i}", [128, SL], BF16) for i in range(4)]
    rbsb = P.sbuf("c_rbsb", [64, SL], BF16)
    rrow = P.sbuf("c_rrow", [1, SL], F32)
    chcol = P.sbuf("c_chcol", [128, 1], F32)

    for h in range(H):
        hc = h // 2
        ho = (h % 2) * 64
        rec = P.op("vector", lambda v: v.memset(sml[:, 0:1], 0.0))
        P.inc(rec, "V", 1)
        for tt in range(NSL):
            t0 = tt * SL
            def f(t, h=h, t0=t0, hc=hc, ho=ho):
                return t.matmul(out=pp[4][ho:ho + 64, :],
                                lhsT=wavb[ho:ho + 64, h * 64:(h + 1) * 64],
                                rhs=XN[ho:ho + 64, hc * T + t0: hc * T + t0 + SL],
                                start=True, stop=True)
            rec = P.op("tensor", f)
            tv = P.inc(rec, "T", 1)
            P.wait("scalar", "T", tv)
            def g(s, t0=t0, ho=ho):
                return s.activation(out=xwT[ho:ho + 64, t0:t0 + SL],
                                    in_=pp[4][ho:ho + 64, :], func=AF.Copy)
            rec = P.op("scalar", g)
            def g2(s, t0=t0, ho=ho):
                return s.activation(out=sqs[ho:ho + 64, :], in_=pp[4][ho:ho + 64, :],
                                    func=AF.Square)
            rec = P.op("scalar", g2)
            sv_ = P.inc(rec, "S", 1)
            P.wait("tensor", "S", sv_)
            def f2(t, ho=ho):
                return t.matmul(out=pp[5][0:1, :], lhsT=C["onescol_bf"][ho:ho + 64, :],
                                rhs=sqs[ho:ho + 64, 0:512],
                                start=True, stop=True)
            rec = P.op("tensor", f2)
            tv = P.inc(rec, "T", 1)
            P.wait("vector", "T", tv)
            P.op("vector", lambda v: v.reduce_max(out=sml[:, 9:10], in_=pp[5][0:1, :],
                                                  axis=AX.X))
            P.selfsync("vector")
            rec = P.op("vector", lambda v: v.tensor_max(
                out=sml[:, 0:1], in0=sml[:, 0:1], in1=sml[0:1, 9:10]))
            P.selfsync("vector")
            vv = P.inc(rec, "V", 1)
            P.wait("tensor", "V", vv)
        if dbg is not None:
            rec = P.op("vector", lambda v, h=h: v.tensor_copy(
                out=dbgsml[0:1, h * 16:(h + 1) * 16], in_=sml[:]))
            P.inc(rec, "V", 1)
        rec = P.op("vector", lambda v, hc=hc: v.tensor_copy(
            out=sml[:, 1:2], in_=nrmmax[0:1, hc:hc + 1]))
        vv = P.inc(rec, "V", 1)
        P.selfsync("vector")
        P.wait("scalar", "V", vv)
        rec = P.op("scalar", lambda s: s.activation(out=sml[:, 2:3], in_=sml[:, 0:1],
                                                    func=AF.Sqrt))
        P.selfsync("scalar")
        rec = P.op("scalar", lambda s: s.activation(out=sml[:, 3:4], in_=sml[:, 1:2],
                                                    func=AF.Sqrt))
        sv_ = P.inc(rec, "S", 1)
        P.selfsync("scalar")
        P.wait("vector", "S", sv_)
        rec = P.op("vector", lambda v: v.tensor_scalar(
            out=sml[:, 4:5], in0=sml[:, 2:3], scalar1=None, scalar2=None,
            op0=ALU.mult) if False else v.tensor_mul(out=sml[:, 4:5], in0=sml[:, 2:3],
                                                     in1=sml[:, 3:4]))
        P.selfsync("vector")
        rec = P.op("vector", lambda v: v.tensor_scalar(
            out=sml[:, 5:6], in0=sml[:, 4:5], scalar1=-0.125, scalar2=None,
            op0=ALU.mult))
        vv = P.inc(rec, "V", 1)
        P.selfsync("vector")
        P.wait("tensor", "V", vv)
        rec = P.op("tensor", lambda t: t.matmul(out=pp[5][:, 0:1], lhsT=C["ones1_f32"][:],
                                                rhs=sml[0:1, 5:6], start=True, stop=True))
        tv = P.inc(rec, "T", 1)
        P.wait("vector", "T", tv)
        rec = P.op("vector", lambda v: v.tensor_copy(out=chcol[:], in_=pp[5][:, 0:1]))
        vv = P.inc(rec, "V", 1)
        P.wait("tensor", "V", vv)
        P.wait("scalar", "V", vv)
        sl4 = (pp[0], pp[1], pp[4], pp[5])
        tOlast = None
        for tb in range(NSL):
            t0 = tb * SL
            tO = {}
            sexp = {}

            def emit_o(j, h=h):
                P.wait("tensor", "S", sexp[j])
                def f2(t, h=h, j=j):
                    return t.matmul(out=pp[2][0:65, :],
                                    lhsT=vx65[:, j * (H * 65) + h * 65: j * (H * 65) + (h + 1) * 65],
                                    rhs=ptmp[j % 4][:],
                                    start=(j == 0), stop=(j == 15))
                rec2 = P.op("tensor", f2)
                tO[j] = P.inc(rec2, "T", 1)

            for sc_ in range(16):
                slot = sc_ % 4
                if sc_ >= 4:
                    P.wait("tensor", "S", sexp[sc_ - 4])
                def f(t, sc_=sc_, t0=t0, slot=slot, hc=hc, ho=ho):
                    return t.matmul(out=sl4[slot][:],
                                    lhsT=XN[ho:ho + 64, hc * T + sc_ * 128: hc * T + (sc_ + 1) * 128],
                                    rhs=xwT[ho:ho + 64, t0:t0 + SL],
                                    start=True, stop=True)
                rec = P.op("tensor", f)
                tv = P.inc(rec, "T", 1)
                P.wait("scalar", "T", tv)
                if sc_ >= 4:
                    P.wait("scalar", "T", tO[sc_ - 4])
                elif tOlast is not None:
                    P.wait("scalar", "T", tOlast[12 + sc_])
                def g(s, slot=slot, sc_=sc_):
                    return s.activation(out=ptmp[slot][:], in_=sl4[slot][:], func=AF.Exp,
                                        scale=0.125, bias=chcol[:])
                rec = P.op("scalar", g)
                sexp[sc_] = P.inc(rec, "S", 1)
                if sc_ >= 1:
                    emit_o(sc_ - 1)
            emit_o(15)
            tOlast = tO
            tv = tO[15]
            P.wait("vector", "T", tv)
            rec = P.op("vector", lambda v: v.reciprocal(out=rrow[:], in_=pp[2][64:65, :]))
            vv = P.inc(rec, "V", 1)
            P.selfsync("vector")
            P.wait("tensor", "V", vv)
            rec = P.op("tensor", lambda t: t.matmul(out=pp[3][0:64, :],
                                                    lhsT=C["ones1_f32"][:, 0:64],
                                                    rhs=rrow[:], start=True, stop=True))
            tv = P.inc(rec, "T", 1)
            P.wait("vector", "T", tv)
            P.op("vector", lambda v: v.tensor_copy(out=rbsb[:], in_=pp[3][0:64, :]))
            P.selfsync("vector")
            if ho == 0:
                def g4(v, hc=hc, t0=t0):
                    return v.tensor_mul(out=o_bf[0:64, hc * T + t0: hc * T + t0 + SL],
                                        in0=pp[2][0:64, :], in1=rbsb[:])
                rec = P.op("vector", g4)
                vv = P.inc(rec, "V", 1)
                P.wait("tensor", "V", vv)
            else:
                P.wait("vector", "DOB")
                def g4(v):
                    return v.tensor_mul(out=otmp[:], in0=pp[2][0:64, :], in1=rbsb[:])
                rec = P.op("vector", g4)
                vv = P.inc(rec, "V", 1)
                P.wait("gpsimd", "V", vv)
                def dmo(hc=hc, t0=t0):
                    return P.dma("gpsimd", o_bf[64:128, hc * T + t0: hc * T + t0 + SL],
                                 otmp[:], sem="DOB")
                dmo()
                P.wait("tensor", "V", vv)
    P.barrier(("tensor", "vector", "scalar"))

    if dbg is not None:
        P.barrier()
        P.wait("sync", "DOB")
        P.dma("sync", dbg["woT"], woT[:], sem="DO")
        P.dma("sync", dbg["obf"], o_bf[:], sem="DO")
        P.dma("sync", dbg["xwT"], xwT[:], sem="DO")
        P.dma("sync", dbg["vx65"], vx65[:], sem="DO")
        P.dma("sync", dbg["sml"], dbgsml[:], sem="DO")
    P.wait("tensor", "DOB")
    opev = {}
    for it, (fc, tt) in enumerate([(f_, t_) for f_ in range(DC) for t_ in range(NSL)]):
        t0 = tt * SL
        bank = pp[6] if it % 2 == 0 else pp[7]
        if it >= 2:
            P.wait("tensor", "V", opev[it - 2])
        for ec in range(DC):
            def f(t, fc=fc, ec=ec, t0=t0, bank=bank):
                return t.matmul(out=bank[:],
                                lhsT=woT[:, ec * 1024 + fc * 128: ec * 1024 + (fc + 1) * 128],
                                rhs=o_bf[:, ec * T + t0: ec * T + t0 + SL],
                                start=(ec == 0), stop=(ec == DC - 1))
            rec = P.op("tensor", f)
        tv = P.inc(rec, "T", 1)
        P.wait("vector", "T", tv)
        def g(v, fc=fc, t0=t0, bank=bank):
            return v.scalar_tensor_tensor(
                out=XT[:, fc * T + t0: fc * T + t0 + SL], in0=bank[:],
                scalar=bias_col[:, fc:fc + 1],
                in1=XT[:, fc * T + t0: fc * T + t0 + SL],
                op0=ALU.add, op1=ALU.add)
        rec = P.op("vector", g)
        opev[it] = P.inc(rec, "V", 1)
    P.wait("tensor", "V", opev[31])
    P.barrier()



TW = T + 1
DBG_LAYER = 0

def emit_ln_to(P, XT, C, pb1, pb2, pst1, pst2, xsq, ta, rows, outT, out_colw, out_off,
               out_dtypecast=False, zb=None):
    ps1 = pst1[0:1, :]
    ps2 = pst2[0:1, :]
    negm, rrow, mrrow, msq, var = rows["negm"], rows["r"], rows["mr"], rows["msq"], rows["var"]
    xnb = xsq
    for i in range(NSL):
        t0 = i * SL
        for c in range(DC):
            def f(v, c=c, t0=t0):
                return v.tensor_copy(out=xsq[c][:], in_=XT[:, c * T + t0: c * T + t0 + SL])
            rec = P.op("vector", f)
        vcast = P.inc(rec, "V", 1)
        P.wait("tensor", "V", vcast)
        for c in range(DC):
            def f(t, c=c):
                return t.matmul(out=ps1, lhsT=C["onescol_bf"][:], rhs=xsq[c][:],
                                start=(c == 0), stop=(c == DC - 1))
            rec = P.op("tensor", f)
        ts1 = P.inc(rec, "T", 1)
        P.wait("scalar", "V", vcast)
        P.wait("scalar", "T", ts1)
        for c in range(DC):
            def f(s, c=c):
                return s.activation(out=xsq[c][:], in_=xsq[c][:], func=AF.Square)
            rec = P.op("scalar", f)
            sv = P.inc(rec, "S", 1)
            P.wait("tensor", "S", sv)
            def f2(t, c=c):
                return t.matmul(out=ps2, lhsT=C["onescol_bf"][:], rhs=xsq[c][:],
                                start=(c == 0), stop=(c == DC - 1))
            rec2 = P.op("tensor", f2)
        ts2 = P.inc(rec2, "T", 1)
        P.wait("scalar", "T", ts1)
        rec = P.op("scalar", lambda s: s.activation(
            out=negm[:], in_=ps1, func=AF.Copy, scale=-1.0 / D))
        P.inc(rec, "S", 1)
        P.selfsync("scalar")
        rec = P.op("scalar", lambda s: s.activation(
            out=msq[:], in_=negm[:], func=AF.Square))
        smsq = P.inc(rec, "S", 1)
        P.wait("vector", "T", ts2)
        P.wait("vector", "S", smsq)
        rec = P.op("vector", lambda v: v.scalar_tensor_tensor(
            out=var[:], in0=ps2, scalar=1.0 / D,
            in1=msq[:], op0=ALU.mult, op1=ALU.subtract))
        vvar = P.inc(rec, "V", 1)
        P.wait("scalar", "V", vvar)
        rec = P.op("scalar", lambda s: s.activation(
            out=var[:], in_=var[:], func=AF.Sqrt, bias=C["epscol"][:]))
        sstd = P.inc(rec, "S", 1)
        P.wait("vector", "S", sstd)
        P.op("vector", lambda v: v.reciprocal(out=rrow[:], in_=var[:]))
        P.selfsync("vector")
        rec = P.op("vector", lambda v: v.tensor_mul(
            out=mrrow[:], in0=negm[:], in1=rrow[:]))
        vfin = P.inc(rec, "V", 1)
        P.wait("tensor", "V", vfin)
        P.op("tensor", lambda t: t.matmul(
            out=pb1[:], lhsT=C["ones1_f32"][:], rhs=rrow[:], start=True, stop=True))
        rec = P.op("tensor", lambda t: t.matmul(
            out=pb2[:], lhsT=C["ones1_f32"][:], rhs=mrrow[:], start=True, stop=True))
        tb = P.inc(rec, "T", 1)
        P.wait("vector", "T", tb)
        for c in range(DC):
            dst = outT[:, c * out_colw + out_off + t0: c * out_colw + out_off + t0 + SL]
            def f(v, c=c, t0=t0):
                return v.tensor_mul(out=ta[:], in0=XT[:, c * T + t0: c * T + t0 + SL],
                                    in1=pb1[:])
            P.op("vector", f)
            def f2(v, dst=dst):
                return v.tensor_add(out=dst, in0=ta[:], in1=pb2[:])
            rec = P.op("vector", f2)
            if zb is not None:
                zdst = zb[:, c * out_colw + out_off + t0: c * out_colw + out_off + t0 + SL]
                def f3(v, zdst=zdst):
                    return v.tensor_add(out=zdst, in0=ta[:], in1=pb2[:])
                rec = P.op("vector", f3)
        vap = P.inc(rec, "V", 1)
        P.wait("tensor", "V", vap)
    P.barrier()


def emit_block_d(P, XT_in_ap, out_ap, C, aps, pp, dbg=None):
    H8 = 8
    HID = 2730
    NHC = 22
    rag = HID - 21 * 128

    zT = P.sbuf("d_zT", [128, DC * TW], F32)
    zb = P.sbuf("d_zb", [128, DC * TW], BF16)
    sml = P.sbuf("d_sml", [1, 16], F32)
    Rp = P.sbuf("d_Rp", [128, 128], BF16)
    P.op("gpsimd", lambda g: g.memset(Rp[:], 1.0))
    P.selfsync("gpsimd")
    rec = P.op("gpsimd", lambda g: g.affine_select(
        out=Rp[:], in_=Rp[:], pattern=[[-64, 2], [32, 2], [-1, 32]],
        compare_op=ALU.is_equal, fill=0.0, base=-32, channel_multiplier=1))
    P.inc(rec, "G", 1)
    zv = zT[:].rearrange("p (c t) -> p c t", c=DC)
    zbv = zb[:].rearrange("p (c t) -> p c t", c=DC)
    P.op("gpsimd", lambda g: g.memset(zv[:, :, 0:1], 0.0))
    rec = P.op("gpsimd", lambda g: g.memset(zbv[:, :, 0:1], 0.0))
    P.inc(rec, "G", 1)

    with contextlib.ExitStack() as s0:
        sb = lambda n, sh, d=F32: s0.enter_context(P.nc.sbuf_tensor(n, sh, d))
        XT = sb("d_XT", [128, DC * T])
        P.dma("sync", XT[:], XT_in_ap[:, :], sem="DS")
        P.wait("vector", "DS")
        P.barrier()
        rows = {k: sb(f"d_lnrow_{k}", [1, SL]) for k in ("negm", "r", "mr", "msq", "var")}
        xsq = [sb(f"d_lnxsq{i}", [128, SL], BF16) for i in range(DC)]
        ta = sb("d_lnta", [128, SL])
        emit_ln_to(P, XT, C, pp[0], pp[1], pp[2], pp[3], xsq, ta, rows,
                   zT, TW, 1, zb=zb)

    if dbg is not None:
        P.barrier()
        P.dma("sync", dbg["z0"], zT[:], sem="DO")
    for l in range(2):
        emit_d_layer(P, C, aps, pp, zT, zb, sml, Rp, l, dbg=dbg)

    if dbg is not None:
        P.barrier()
        P.dma("sync", dbg["zfin"], zT[:], sem="DO")
        for e_ in ENGINES:
            P.wait(e_, "DO")
    with contextlib.ExitStack() as s3:
        sb = lambda n, sh, d=F32: s3.enter_context(P.nc.sbuf_tensor(n, sh, d))
        Xf = sb("d_xtok", [128, DC * T])
        outsb = sb("d_outsb", [128, D])
        P.dma("sync", Xf[:], XT_in_ap[:, :], sem="DS")
        P.wait("vector", "DS")
        P.barrier()
        sums = sb("d_sums", [128, SL], F32)
        sumb = sb("d_sumb", [128, SL], BF16)
        for tc in range(T // 128):
            for dc in range(DC):
                def fadd(v, tc=tc, dc=dc):
                    return v.tensor_add(
                        out=sums[:, 0:128],
                        in0=zT[:, dc * TW + 1 + tc * 128: dc * TW + 1 + (tc + 1) * 128],
                        in1=Xf[:, dc * T + tc * 128: dc * T + (tc + 1) * 128])
                rec = P.op("vector", fadd)
                va = P.inc(rec, "V", 1)
                P.wait("tensor", "V", va)
                def ftr(t, dc=dc):
                    return t.transpose(out=pp[4][:, 0:128], in_=sums[:, 0:128],
                                       identity=C["idn_f32"][:])
                rec = P.op("tensor", ftr)
                tv = P.inc(rec, "T", 1)
                P.wait("vector", "T", tv)
                def fev(v, dc=dc):
                    return v.tensor_copy(out=outsb[:, dc * 128:(dc + 1) * 128],
                                         in_=pp[4][:, 0:128])
                rec = P.op("vector", fev)
                ve = P.inc(rec, "V", 1)
                P.wait("tensor", "V", ve)
            P.wait("sync", "V", ve)
            do_ = P.dma("sync", out_ap[tc * 128:(tc + 1) * 128, :], outsb[:], sem="DO")
            if tc + 1 < T // 128:
                P.wait("vector", "DO", do_)
        P.wait("sync", "DO")


def emit_d_layer(P, C, aps, pp, zT, zb, sml, Rp, l, dbg=None):
    if getattr(P, "_phase", None) is not None:
        P.set_phase(f"d_attn{l}")
    if dbg is not None and l == DBG_LAYER and DBG_LAYER > 0:
        P.barrier()
        P.dma("sync", dbg["zb1"], zb[:], sem="DO")
        P.dma("sync", dbg["zb1f"], zT[:], sem="DO")
        for e_ in ENGINES:
            P.wait(e_, "DO")
    H8 = 8
    HID = 2730
    NHC = 22
    rag = HID - 21 * 128

    def zsl(c, t0, n=SL, shifted=False):
        off = 0 if shifted and c >= 4 else 1
        return zb[:, c * TW + off + t0: c * TW + off + t0 + n]

    attn_scope = contextlib.ExitStack()
    asb = lambda n, sh, d=F32: attn_scope.enter_context(P.nc.sbuf_tensor(n, sh, d))
    qf = asb(f"d_qf{l}", [128, 4 * T], BF16)
    kf = asb(f"d_kf{l}", [128, 4 * T], BF16)
    v65 = asb(f"d_v65{l}", [128, 16 * (H8 * 64)], BF16)
    o_bf = asb(f"d_obf{l}", [128, 4 * T], BF16)

    with contextlib.ExitStack() as s1:
        sb = lambda n, sh, d=F32: s1.enter_context(P.nc.sbuf_tensor(n, sh, d))
        wq = [sb(f"d_wq{l}_{i}", [128, DC * 128], BF16) for i in range(2)]
        tabs = [sb(f"d_tab{l}_{i}", [128, T], BF16) for i in range(4)]
        vtmp = sb(f"d_vtmp{l}", [128, SL], BF16)
        qs2 = sb(f"d_qs2{l}", [128, 2])
        ks2 = sb(f"d_ks2{l}", [128, 2])
        mask2 = sb(f"d_mask2{l}", [128, 2], BF16)
        qraw = sb(f"d_qraw{l}", [128, SL])
        qsq = sb(f"d_qsq{l}", [128, SL], BF16)
        rn2 = sb(f"d_rn2{l}", [2, SL])
        rn2b = sb(f"d_rn2b{l}", [2, SL], BF16)
        bc2 = sb(f"d_bc2{l}", [2, 128], BF16)
        qnf = sb(f"d_qnf{l}", [128, SL])
        qnb = sb(f"d_qnb{l}", [128, SL], BF16)
        t1 = sb(f"d_t1{l}", [128, SL])
        t2 = sb(f"d_t2{l}", [128, SL])

        P.op("gpsimd", lambda g: g.memset(mask2[:], 1.0))
        P.selfsync("gpsimd")
        P.op("gpsimd", lambda g: g.affine_select(
            out=mask2[:], in_=mask2[:], pattern=[[-64, 2]],
            compare_op=ALU.is_ge, fill=0.0, base=0, channel_multiplier=1))
        P.selfsync("gpsimd")
        P.op("gpsimd", lambda g: g.affine_select(
            out=mask2[:], in_=mask2[:], pattern=[[64, 2]],
            compare_op=ALU.is_ge, fill=0.0, base=63, channel_multiplier=-1))
        P.op("gpsimd", lambda g: g.memset(bc2[:], 1.0))
        P.selfsync("gpsimd")
        P.op("gpsimd", lambda g: g.affine_select(
            out=bc2[:], in_=bc2[:], pattern=[[1, 128]],
            compare_op=ALU.is_ge, fill=0.0, base=0, channel_multiplier=-64))
        P.selfsync("gpsimd")
        rec = P.op("gpsimd", lambda g: g.affine_select(
            out=bc2[:], in_=bc2[:], pattern=[[-1, 128]],
            compare_op=ALU.is_ge, fill=0.0, base=63, channel_multiplier=64))
        P.inc(rec, "G", 1)

        wq_done = {}
        def dma_wq(fc, par):
            wq_done[fc] = ("DGQ" + str(par), P.dma(
                "gpsimd", wq[par][:].rearrange("p (c f) -> p c f", c=DC),
                aps["wqkv"][l][:, fc * 128:(fc + 1) * 128].rearrange(
                    "(c p) f -> p c f", p=128), sem="DGQ" + str(par)))
        dma_wq(0, 0)
        P.dma("sync", qs2[0:64, 0:1], aps["qs"][l].rearrange("(a e) -> e a", a=1), sem="DS")
        P.dma("sync", qs2[64:128, 0:1], aps["qs"][l].rearrange("(a e) -> e a", a=1), sem="DS")
        P.dma("sync", ks2[0:64, 0:1], aps["ks"][l].rearrange("(a e) -> e a", a=1), sem="DS")
        dk = P.dma("sync", ks2[64:128, 0:1], aps["ks"][l].rearrange("(a e) -> e a", a=1), sem="DS")
        P.wait("vector", "DS", dk)
        for b0 in range(0, 128, 64):
            P.op("vector", lambda v, b0=b0: v.tensor_copy(
                out=qs2[b0:b0 + 32, 1:2], in_=qs2[b0 + 32:b0 + 64, 0:1]))
            P.op("vector", lambda v, b0=b0: v.tensor_copy(
                out=qs2[b0 + 32:b0 + 64, 1:2], in_=qs2[b0:b0 + 32, 0:1]))
            P.op("vector", lambda v, b0=b0: v.tensor_copy(
                out=ks2[b0:b0 + 32, 1:2], in_=ks2[b0 + 32:b0 + 64, 0:1]))
            P.op("vector", lambda v, b0=b0: v.tensor_copy(
                out=ks2[b0 + 32:b0 + 64, 1:2], in_=ks2[b0:b0 + 32, 0:1]))
        P.selfsync("vector")
        for (ti, nm, col, sc) in ((0, "csq", 0, qs2), (1, "ssq", 1, qs2),
                                  (2, "csk", 0, ks2), (3, "ssk", 1, ks2)):
            for sl4 in range(NSL):
                dt_ = P.dma("sync", qraw[:], aps[nm][:, sl4 * SL:(sl4 + 1) * SL], sem="DS")
                P.wait("vector", "DS", dt_)
                def f(v, ti=ti, col=col, sc=sc, sl4=sl4):
                    return v.tensor_scalar(out=tabs[ti][:, sl4 * SL:(sl4 + 1) * SL],
                                           in0=qraw[:],
                                           scalar1=sc[:, col:col + 1], scalar2=None,
                                           op0=ALU.mult)
                rec = P.op("vector", f)
                vt = P.inc(rec, "V", 1)
                P.wait("sync", "V", vt)
        P.wait("tensor", "V", vt)
        P.wait("tensor", "DG")
        P.wait("tensor", "G")

        tq_after = {}
        for fc in range(12):
            kind = "q" if fc < 4 else ("k" if fc < 8 else "v")
            hc = fc % 4
            if fc + 1 < 12:
                if fc >= 1:
                    P.wait("gpsimd", "T", tq_after[fc - 1])
                dma_wq(fc + 1, (fc + 1) % 2)
            P.wait("tensor", wq_done[fc][0], wq_done[fc][1])
            for tt in range(NSL):
                t0 = tt * SL
                for dc in range(DC):
                    def f(t, fc=fc, dc=dc, t0=t0):
                        return t.matmul(out=pp[0][:],
                                        lhsT=wq[fc % 2][:, dc * 128:(dc + 1) * 128],
                                        rhs=zsl(dc, t0, shifted=True),
                                        start=(dc == 0), stop=(dc == DC - 1))
                    rec = P.op("tensor", f)
                tm = P.inc(rec, "T", 1)
                if kind == "v":
                    P.wait("scalar", "T", tm)
                    def g(s):
                        return s.activation(out=vtmp[:], in_=pp[0][:], func=AF.Copy)
                    rec = P.op("scalar", g)
                    sv = P.inc(rec, "S", 1)
                    P.wait("tensor", "S", sv)
                    for q4 in range(4):
                        def ftr(t, q4=q4):
                            return t.transpose(
                                out=pp[1][:, q4 * 64:(q4 + 1) * 64].bitcast(BF16),
                                in_=vtmp[:, q4 * 128:(q4 + 1) * 128],
                                identity=C["idn_bf"][:])
                        rec = P.op("tensor", ftr)
                    tv2 = P.inc(rec, "T", 1)
                    P.wait("vector", "T", tv2)
                    def gv(v, hc=hc, tt=tt):
                        return v.tensor_copy(
                            out=v65[:].rearrange("p (sc h e) -> p sc h e", sc=16, h=H8)[
                                :, tt * 4:(tt + 1) * 4, 2 * hc:2 * hc + 2, :],
                            in_=pp[1][:, 0:256].bitcast(BF16).rearrange(
                                "p (sc h e) -> p sc h e", sc=4, h=2))
                    rec = P.op("vector", gv)
                    vv = P.inc(rec, "V", 1)
                    P.wait("tensor", "V", vv)
                else:
                    dstf = qf if kind == "q" else kf
                    ct, st_, scl = (tabs[0], tabs[1], qs2) if kind == "q" else (tabs[2], tabs[3], ks2)
                    P.wait("scalar", "T", tm)
                    rec = P.op("scalar", lambda s: s.activation(out=qraw[:], in_=pp[0][:],
                                                                func=AF.Copy))
                    P.selfsync("scalar")
                    rec = P.op("scalar", lambda s: s.activation(out=qsq[:], in_=qraw[:],
                                                                func=AF.Square))
                    sv = P.inc(rec, "S", 1)
                    P.wait("tensor", "S", sv)
                    rec = P.op("tensor", lambda t: t.matmul(out=pp[1][0:2, :], lhsT=mask2[:],
                                                            rhs=qsq[:], start=True, stop=True))
                    tv2 = P.inc(rec, "T", 1)
                    P.wait("scalar", "T", tv2)
                    rec = P.op("scalar", lambda s: s.activation(out=rn2[:], in_=pp[1][0:2, :],
                                                                func=AF.Sqrt))
                    sv2 = P.inc(rec, "S", 1)
                    P.wait("vector", "S", sv2)
                    P.op("vector", lambda v: v.tensor_scalar(
                        out=rn2[:], in0=rn2[:], scalar1=1e-12, scalar2=None, op0=ALU.max))
                    P.selfsync("vector")
                    P.op("vector", lambda v: v.reciprocal(out=rn2[:], in_=rn2[:]))
                    P.selfsync("vector")
                    rec = P.op("vector", lambda v: v.tensor_copy(out=rn2b[:], in_=rn2[:]))
                    vv = P.inc(rec, "V", 1)
                    P.wait("tensor", "V", vv)
                    rec = P.op("tensor", lambda t: t.matmul(out=pp[2][:], lhsT=bc2[:],
                                                            rhs=rn2b[:], start=True, stop=True))
                    tv3 = P.inc(rec, "T", 1)
                    P.wait("vector", "T", tv3)
                    rec = P.op("vector", lambda v: v.tensor_mul(out=qnf[:], in0=qraw[:],
                                                                in1=pp[2][:]))
                    P.selfsync("vector")
                    rec = P.op("vector", lambda v: v.tensor_copy(out=qnb[:], in_=qnf[:]))
                    vv2 = P.inc(rec, "V", 1)
                    P.wait("tensor", "V", vv2)
                    rec = P.op("tensor", lambda t: t.matmul(out=pp[3][:], lhsT=Rp[:],
                                                            rhs=qnb[:], start=True, stop=True))
                    tv4 = P.inc(rec, "T", 1)
                    def f1(v, ct=ct, t0=t0):
                        return v.tensor_mul(out=t1[:], in0=qnf[:], in1=ct[:, t0:t0 + SL])
                    P.op("vector", f1)
                    P.wait("vector", "T", tv4)
                    def f2_(v, st_=st_, t0=t0):
                        return v.tensor_mul(out=t2[:], in0=pp[3][:], in1=st_[:, t0:t0 + SL])
                    P.op("vector", f2_)
                    P.selfsync("vector")
                    def f3(v, dstf=dstf, hc=hc, t0=t0):
                        return v.tensor_add(out=dstf[:, hc * T + t0: hc * T + t0 + SL],
                                            in0=t1[:], in1=t2[:])
                    rec = P.op("vector", f3)
                    vv3 = P.inc(rec, "V", 1)
                    P.wait("tensor", "V", vv3)
                    P.wait("scalar", "V", vv3)
            tq_after[fc] = P.counts["T"]
        P.barrier(("tensor", "vector", "scalar", "gpsimd"))

    with contextlib.ExitStack() as s2:
        sb = lambda n, sh, d=F32: s2.enter_context(P.nc.sbuf_tensor(n, sh, d))
        SC = sb(f"d_SC{l}", [128, 1024])
        msk = [sb(f"d_msk{l}_{q}", [128, 1024]) for q in range(4)]
        nrx = sb(f"d_nrx{l}", [128, 1])
        rsum = sb(f"d_rsum{l}", [128, 1])
        PN = sb(f"d_PN{l}", [128, 1024], BF16)
        PTsb = sb(f"d_PTsb{l}", [128, 8 * SL], BF16)
        otmp = sb(f"d_otmp{l}", [64, SL], BF16)
        for q in range(4):
            def bm0(g, q=q):
                return g.memset(msk[q][:], 0.0)
            P.op("gpsimd", bm0)
            P.selfsync("gpsimd")
            def bm1(g, q=q):
                return g.affine_select(out=msk[q][:], in_=msk[q][:], pattern=[[-1, 1024]],
                                       compare_op=ALU.is_ge, fill=-3e38,
                                       base=q * 128 + 512, channel_multiplier=1)
            P.op("gpsimd", bm1)
            P.selfsync("gpsimd")
            def bm2(g, q=q):
                return g.affine_select(out=msk[q][:], in_=msk[q][:], pattern=[[1, 1024]],
                                       compare_op=ALU.is_ge, fill=-3e38,
                                       base=-(q * 128), channel_multiplier=-1)
            rec = P.op("gpsimd", bm2)
        gm = P.inc(rec, "G", 1)
        P.wait("vector", "G", gm)
        for h in range(H8):
            hc, ho = h // 2, (h % 2) * 64
            for w in range(4):
                qt0 = w * SL
                kc0 = 0 if w > 0 else 4
                for qc in range(4):
                    for kt in range(2):
                        if w == 0 and kt == 0:
                            continue
                        def f(t, h=h, hc=hc, ho=ho, w=w, qc=qc, kt=kt, qt0=qt0):
                            ks = (w - 1) * SL + kt * SL
                            return t.matmul(
                                out=pp[kt][:],
                                lhsT=qf[ho:ho + 64, hc * T + qt0 + qc * 128: hc * T + qt0 + (qc + 1) * 128],
                                rhs=kf[ho:ho + 64, hc * T + ks: hc * T + ks + SL],
                                start=True, stop=True)
                        rec = P.op("tensor", f)
                    tm = P.inc(rec, "T", 1)
                    P.wait("vector", "T", tm)
                    f0 = SL if (w == 0) else 0
                    for kt in range(2):
                        if w == 0 and kt == 0:
                            continue
                        def g(v, kt=kt, qc=qc):
                            return v.tensor_add(out=SC[:, kt * SL:(kt + 1) * SL],
                                                in0=pp[kt][:],
                                                in1=msk[qc][:, kt * SL:(kt + 1) * SL])
                        rec = P.op("vector", g)
                    vv = P.inc(rec, "V", 1)
                    P.wait("tensor", "V", vv)
                    def rmx(v, f0=f0):
                        return v.reduce_max(out=nrx[:], in_=SC[:, f0:1024], axis=AX.X,
                                            negate=True)
                    rec = P.op("vector", rmx)
                    vv2 = P.inc(rec, "V", 1)
                    P.selfsync("vector")
                    P.wait("scalar", "V", vv2)
                    def ex(s, f0=f0):
                        return s.activation(out=PN[:, f0:1024], in_=SC[:, f0:1024],
                                            func=AF.Exp, bias=nrx[:], accum_out=rsum[:])
                    rec = P.op("scalar", ex)
                    sv = P.inc(rec, "S", 1)
                    P.wait("vector", "S", sv)
                    P.op("vector", lambda v: v.reciprocal(out=rsum[:], in_=rsum[:]))
                    P.selfsync("vector")
                    def nm(v, f0=f0):
                        return v.tensor_scalar(out=PN[:, f0:1024], in0=PN[:, f0:1024],
                                               scalar1=rsum[:], scalar2=None,
                                               op0=ALU.mult)
                    rec = P.op("vector", nm)
                    vv3 = P.inc(rec, "V", 1)
                    P.wait("tensor", "V", vv3)
                    for kc in range(kc0, 8):
                        def ftr(t, kc=kc):
                            return t.transpose(
                                out=pp[2][:, (kc % 4) * 64:((kc % 4) + 1) * 64].bitcast(BF16),
                                in_=PN[:, kc * 128:(kc + 1) * 128],
                                identity=C["idn_bf"][:])
                        rec = P.op("tensor", ftr)
                        if kc % 4 == 3:
                            tv = P.inc(rec, "T", 1)
                            P.wait("vector", "T", tv)
                            def gev(v, kc=kc, qc=qc):
                                return v.tensor_copy(
                                    out=PTsb[:].rearrange("p (kc q) -> p kc q", kc=8)[
                                        :, kc - 3:kc + 1, qc * 128:(qc + 1) * 128],
                                    in_=pp[2][:, 0:256].bitcast(BF16).rearrange(
                                        "p (kc q) -> p kc q", kc=4))
                            rec = P.op("vector", gev)
                            vv4 = P.inc(rec, "V", 1)
                            P.wait("tensor", "V", vv4)
                for kc in range(kc0, 8):
                    sglob = 4 * w - 4 + kc
                    def f(t, h=h, kc=kc, sglob=sglob, kc0=kc0):
                        return t.matmul(
                            out=pp[3][0:64, :],
                            lhsT=v65[:, sglob * (H8 * 64) + h * 64: sglob * (H8 * 64) + (h + 1) * 64],
                            rhs=PTsb[:, kc * SL:(kc + 1) * SL],
                            start=(kc == kc0), stop=(kc == 7))
                    rec = P.op("tensor", f)
                to = P.inc(rec, "T", 1)
                P.wait("vector", "T", to)
                if ho == 0:
                    def ge(v, hc=hc, qt0=qt0):
                        return v.tensor_copy(out=o_bf[0:64, hc * T + qt0: hc * T + qt0 + SL],
                                             in_=pp[3][0:64, :])
                    rec = P.op("vector", ge)
                    ve = P.inc(rec, "V", 1)
                    P.wait("tensor", "V", ve)
                else:
                    P.wait("vector", "DOB")
                    rec = P.op("vector", lambda v: v.tensor_copy(out=otmp[:],
                                                                 in_=pp[3][0:64, :]))
                    ve = P.inc(rec, "V", 1)
                    P.wait("gpsimd", "V", ve)
                    P.dma("gpsimd", o_bf[64:128, hc * T + qt0: hc * T + qt0 + SL],
                          otmp[:], sem="DOB")
                    P.wait("tensor", "V", ve)
        P.wait("tensor", "DOB")
        P.barrier(("tensor", "vector", "scalar", "gpsimd"))

    with contextlib.ExitStack() as s2b:
        sb = lambda n, sh, d=F32: s2b.enter_context(P.nc.sbuf_tensor(n, sh, d))
        wo_bf = sb(f"d_wo{l}", [128, 4 * 1024], BF16)
        P.dma("gpsimd", wo_bf[:].rearrange("p (c f) -> p c f", c=4),
              aps["wo"][l].rearrange("(c p) f -> p c f", p=128), sem="DG")
        P.wait("tensor", "DG")
        for fc in range(DC):
            for tt in range(NSL):
                t0 = tt * SL
                for ec in range(4):
                    def f(t, fc=fc, ec=ec, t0=t0):
                        return t.matmul(out=pp[4][:],
                                        lhsT=wo_bf[:, ec * 1024 + fc * 128: ec * 1024 + (fc + 1) * 128],
                                        rhs=o_bf[:, ec * T + t0: ec * T + t0 + SL],
                                        start=(ec == 0), stop=(ec == 3))
                    rec = P.op("tensor", f)
                tw = P.inc(rec, "T", 1)
                P.wait("vector", "T", tw)
                def fz(v, fc=fc, t0=t0):
                    return v.tensor_add(out=zb[:, fc * TW + 1 + t0: fc * TW + 1 + t0 + SL],
                                        in0=pp[4][:],
                                        in1=zT[:, fc * TW + 1 + t0: fc * TW + 1 + t0 + SL])
                P.op("vector", fz)
                def fr(v, fc=fc, t0=t0):
                    return v.tensor_add(out=zT[:, fc * TW + 1 + t0: fc * TW + 1 + t0 + SL],
                                        in0=pp[4][:],
                                        in1=zT[:, fc * TW + 1 + t0: fc * TW + 1 + t0 + SL])
                rec = P.op("vector", fr)
                vr = P.inc(rec, "V", 1)
                P.wait("tensor", "V", vr)
        P.barrier(("tensor", "vector", "scalar", "gpsimd"))
    if dbg is not None and l == DBG_LAYER:
        P.barrier()
        P.dma("sync", dbg["qf0"], qf[:], sem="DO")
        P.dma("sync", dbg["kf0"], kf[:], sem="DO")
        P.dma("sync", dbg["v650"], v65[:], sem="DO")
        P.dma("sync", dbg["obf0"], o_bf[:], sem="DO")
        P.dma("sync", dbg["zattn0"], zT[:], sem="DO")
        for e_ in ("sync", "tensor", "vector", "scalar", "gpsimd"):
            P.wait(e_, "DO")
    P.barrier()
    attn_scope.close()

    if getattr(P, "_phase", None) is not None:
        P.set_phase(f"d_ffn{l}")
    HIDW = [128] * 21 + [rag]
    with contextlib.ExitStack() as s3:
        sb = lambda n, sh, d=F32: s3.enter_context(P.nc.sbuf_tensor(n, sh, d))
        hn = sb(f"d_hn{l}", [128, DC * T], BF16)
        gcol = sb(f"d_gcol{l}", [128, 16])
        b1acol = sb(f"d_b1a{l}", [128, NHC])
        b1gcol = sb(f"d_b1g{l}", [128, NHC])
        b2col = sb(f"d_b2c{l}", [128, 8])
        lnscratch = contextlib.ExitStack()
        sbl = lambda n, sh, d=F32: lnscratch.enter_context(P.nc.sbuf_tensor(n, sh, d))
        gb8 = sbl(f"d_gb8{l}", [8, 128])
        b1s = sbl(f"d_b1s{l}", [21, 128])
        b1t = sbl(f"d_b1t{l}", [1, 128])
        rows = {k: sbl(f"d_f_lnrow_{k}{l}", [1, SL]) for k in ("negm", "r", "mr", "msq", "var")}
        xsq = sbl(f"d_f_xsq{l}", [128, SL], BF16)
        ta = sbl(f"d_f_ta{l}", [128, SL])
        P.dma("sync", gb8[:], aps["lng"][l].rearrange("(a j) -> a j", j=128), sem="DS")
        P.wait("tensor", "DS")
        rec = P.op("tensor", lambda t: t.transpose(out=pp[5][:, 0:8], in_=gb8[:],
                                                   identity=C["idn_f32"][0:8, 0:8]))
        t1v = P.inc(rec, "T", 1)
        P.wait("sync", "T", t1v)
        P.dma("sync", gb8[:], aps["lnb"][l].rearrange("(a j) -> a j", j=128), sem="DS")
        P.wait("tensor", "DS")
        rec = P.op("tensor", lambda t: t.transpose(out=pp[5][:, 8:16], in_=gb8[:],
                                                   identity=C["idn_f32"][0:8, 0:8]))
        tg = P.inc(rec, "T", 1)
        P.wait("vector", "T", tg)
        rec = P.op("vector", lambda v: v.tensor_copy(out=gcol[:], in_=pp[5][:, 0:16]))
        vg = P.inc(rec, "V", 1)
        P.wait("tensor", "V", vg)
        for (base, dstc) in ((0, b1acol), (HID, b1gcol)):
            P.wait("sync", "T")
            P.dma("sync", b1s[:], aps["b1"][l, base:base + 21 * 128].rearrange(
                "(a j) -> a j", j=128), sem="DS")
            P.dma("sync", b1t[:, 0:rag], aps["b1"][l, base + 21 * 128:base + HID].rearrange(
                "(a j) -> a j", a=1), sem="DS")
            P.wait("tensor", "DS")
            P.op("tensor", lambda t: t.transpose(out=pp[5][:, 0:21], in_=b1s[:],
                                                 identity=C["idn_f32"][0:21, 0:21]))
            rec = P.op("tensor", lambda t: t.transpose(
                out=pp[6][0:rag, 0:1], in_=b1t[:, 0:rag],
                identity=C["idn_f32"][0:1, 0:1]))
            tb1 = P.inc(rec, "T", 1)
            P.wait("vector", "T", tb1)
            def gv(v, dstc=dstc):
                return v.tensor_copy(out=dstc[:, 0:21], in_=pp[5][:, 0:21])
            P.op("vector", gv)
            def gv2(v, dstc=dstc):
                return v.tensor_copy(out=dstc[0:rag, 21:22], in_=pp[6][0:rag, 0:1])
            rec = P.op("vector", gv2)
            vb1 = P.inc(rec, "V", 1)
            P.wait("tensor", "V", vb1)
        P.wait("sync", "T")
        P.dma("sync", gb8[:], aps["b2"][l].rearrange("(a j) -> a j", j=128), sem="DS")
        P.wait("tensor", "DS")
        rec = P.op("tensor", lambda t: t.transpose(out=pp[5][:, 0:8], in_=gb8[:],
                                                   identity=C["idn_f32"][0:8, 0:8]))
        tb2 = P.inc(rec, "T", 1)
        P.wait("vector", "T", tb2)
        rec = P.op("vector", lambda v: v.tensor_copy(out=b2col[:], in_=pp[5][:, 0:8]))
        vb2 = P.inc(rec, "V", 1)
        P.wait("tensor", "V", vb2)

        for i in range(NSL):
            t0 = i * SL
            for c in range(DC):
                def f(t, c=c, t0=t0):
                    return t.matmul(out=pp[2][0:1, :], lhsT=C["onescol_bf"][:],
                                    rhs=zsl(c, t0, shifted=True),
                                    start=(c == 0), stop=(c == DC - 1))
                rec = P.op("tensor", f)
            ts1 = P.inc(rec, "T", 1)
            for c in range(DC):
                def f(s, c=c, t0=t0):
                    return s.activation(out=xsq[:], in_=zsl(c, t0, shifted=True),
                                        func=AF.Square)
                rec = P.op("scalar", f)
                sv = P.inc(rec, "S", 1)
                P.wait("tensor", "S", sv)
                def f2(t, c=c):
                    return t.matmul(out=pp[3][0:1, :], lhsT=C["onescol_bf"][:],
                                    rhs=xsq[:], start=(c == 0), stop=(c == DC - 1))
                rec2 = P.op("tensor", f2)
                ts2 = P.inc(rec2, "T", 1)
                P.wait("scalar", "T", ts2)
            P.wait("scalar", "T", ts1)
            rec = P.op("scalar", lambda s: s.activation(
                out=rows["negm"][:], in_=pp[2][0:1, :], func=AF.Copy, scale=-1.0 / D))
            P.inc(rec, "S", 1)
            P.selfsync("scalar")
            rec = P.op("scalar", lambda s: s.activation(
                out=rows["msq"][:], in_=rows["negm"][:], func=AF.Square))
            smsq = P.inc(rec, "S", 1)
            P.wait("vector", "S", smsq)
            P.wait("vector", "T", ts2)
            rec = P.op("vector", lambda v: v.scalar_tensor_tensor(
                out=rows["var"][:], in0=pp[3][0:1, :], scalar=1.0 / D,
                in1=rows["msq"][:], op0=ALU.mult, op1=ALU.subtract))
            vvar = P.inc(rec, "V", 1)
            P.wait("scalar", "V", vvar)
            rec = P.op("scalar", lambda s: s.activation(
                out=rows["var"][:], in_=rows["var"][:], func=AF.Sqrt, bias=C["epscol"][:]))
            sstd = P.inc(rec, "S", 1)
            P.wait("vector", "S", sstd)
            P.op("vector", lambda v: v.reciprocal(out=rows["r"][:], in_=rows["var"][:]))
            P.selfsync("vector")
            rec = P.op("vector", lambda v: v.tensor_mul(
                out=rows["mr"][:], in0=rows["negm"][:], in1=rows["r"][:]))
            vfin = P.inc(rec, "V", 1)
            if dbg is not None and "rows0x" in dbg and l == DBG_LAYER and i == 0:
                P.wait("sync", "V", vfin)
                P.dma("sync", dbg["rows0"][0:1, :], rows["negm"][:], sem="DO")
                P.dma("sync", dbg["rows0"][1:2, :], rows["r"][:], sem="DO")
                dro = P.counts["DO"]
                P.wait("vector", "DO", dro)
                P.wait("tensor", "DO", dro)
            P.wait("tensor", "V", vfin)
            P.op("tensor", lambda t: t.matmul(
                out=pp[2][:], lhsT=C["ones1_f32"][:], rhs=rows["r"][:],
                start=True, stop=True))
            rec = P.op("tensor", lambda t: t.matmul(
                out=pp[3][:], lhsT=C["ones1_f32"][:], rhs=rows["mr"][:],
                start=True, stop=True))
            tb = P.inc(rec, "T", 1)
            P.wait("vector", "T", tb)
            for c in range(DC):
                off = 0 if c >= 4 else 1
                def f(v, c=c, t0=t0, off=off):
                    return v.tensor_mul(
                        out=ta[:],
                        in0=zT[:, c * TW + off + t0: c * TW + off + t0 + SL],
                        in1=pp[2][:])
                P.op("vector", f)
                def f2(v):
                    return v.tensor_add(out=ta[:], in0=ta[:], in1=pp[3][:])
                rec = P.op("vector", f2)
                if dbg is not None and "u0x" in dbg and l == DBG_LAYER and i == 0 and c == 0:
                    vu = P.inc(rec, "V", 1)
                    P.wait("sync", "V", vu)
                    P.dma("sync", dbg["u0"], ta[:], sem="DO")
                    duo = P.counts["DO"]
                    P.wait("vector", "DO", duo)
                def f3(v, c=c, t0=t0):
                    return v.tensor_scalar(out=hn[:, c * T + t0: c * T + t0 + SL],
                                           in0=ta[:], scalar1=gcol[:, c:c + 1],
                                           scalar2=gcol[:, 8 + c:9 + c],
                                           op0=ALU.mult, op1=ALU.add)
                rec = P.op("vector", f3)
            vap = P.inc(rec, "V", 1)
            P.wait("tensor", "V", vap)
        P.barrier(("tensor", "vector", "scalar", "gpsimd"))
        lnscratch.close()

        w2bf = sb(f"d_w2bf{l}", [128, NHC * 1024], BF16)
        mm1scratch = contextlib.ExitStack()
        sbm = lambda n, sh, d=F32: mm1scratch.enter_context(P.nc.sbuf_tensor(n, sh, d))
        w1cb = [sbm(f"d_w1cb{l}_{i}", [128, 8 * 128], BF16) for i in range(4)]
        sg = sbm(f"d_sg{l}", [128, 2 * SL])
        hgt = [sbm(f"d_hgt{l}_{i}", [128, SL], BF16) for i in range(2)]
        for c in range(NHC):
            wdt = HIDW[c]
            P.dma("gpsimd", w2bf[0:wdt, c * 1024:(c + 1) * 1024],
                  aps["w2"][l, c * 128:c * 128 + wdt, :], sem="DG")
        wq2done = P.counts["DG"]
        P.wait("tensor", "DG", wq2done)

        w1_done = {}
        def dma_w1(c_, par):
            wdt_ = HIDW[c_]
            for (k, base) in ((0, 0), (1, HID)):
                inap = aps["w1"][l][:, base + c_ * 128: base + c_ * 128 + wdt_].rearrange(
                    "(dc p) j -> p dc j", p=128)
                v = P.dma("gpsimd", w1cb[2 * par + k][:, 0:8 * wdt_].rearrange(
                    "p (dc j) -> p dc j", dc=DC), inap, sem="DW1" + str(par))
            w1_done[c_] = ("DW1" + str(par), v)

        hgd = {}
        tmm_after = {}
        mm1v = {}
        mm1s = {}
        dma_w1(0, 0)
        for c in range(NHC):
            par = c % 2
            if c + 1 < NHC:
                if c >= 1:
                    P.wait("gpsimd", "T", tmm_after[c - 1])
                dma_w1(c + 1, (c + 1) % 2)
            wdt = HIDW[c]
            P.wait("tensor", w1_done[c][0], w1_done[c][1])
            for tt in range(NSL):
                t0 = tt * SL
                idx = c * NSL + tt
                par2 = idx % 2
                bk0, bk1 = (pp[0], pp[1]) if par2 == 0 else (pp[2], pp[3])
                if idx >= 2:
                    P.wait("tensor", "V", mm1v[idx - 2])
                    P.wait("tensor", "S", mm1s[idx - 2])
                for (psumt, k) in ((bk0, 0), (bk1, 1)):
                    for dc in range(DC):
                        def f(t, psumt=psumt, k=k, dc=dc, t0=t0, par=par, wdt=wdt):
                            return t.matmul(
                                out=psumt[0:wdt, :],
                                lhsT=w1cb[2 * par + k][:, dc * wdt:(dc + 1) * wdt],
                                rhs=hn[:, dc * T + t0: dc * T + t0 + SL],
                                start=(dc == 0), stop=(dc == DC - 1))
                        rec = P.op("tensor", f)
                tmm = P.inc(rec, "T", 1)
                P.wait("scalar", "T", tmm)
                if idx >= 2:
                    P.wait("scalar", "V", mm1v[idx - 2])
                def fg(s, c=c, wdt=wdt, bk1=bk1, par2=par2):
                    return s.activation(out=sg[0:wdt, par2 * SL:(par2 + 1) * SL],
                                        in_=bk1[0:wdt, :],
                                        func=AF.Gelu, bias=b1gcol[0:wdt, c:c + 1])
                rec = P.op("scalar", fg)
                mm1s[idx] = ssi = P.inc(rec, "S", 1)
                P.wait("vector", "S", ssi)
                if idx >= 2:
                    P.wait("vector", "DHG" + str(par2), hgd[idx - 2][1])
                def fh(v, c=c, wdt=wdt, par2=par2, bk0=bk0):
                    return v.scalar_tensor_tensor(
                        out=hgt[par2][0:wdt, :],
                        in0=bk0[0:wdt, :], scalar=b1acol[0:wdt, c:c + 1],
                        in1=sg[0:wdt, par2 * SL:(par2 + 1) * SL],
                        op0=ALU.add, op1=ALU.mult)
                rec = P.op("vector", fh)
                mm1v[idx] = vg_ = P.inc(rec, "V", 1)
                P.wait("gpsimd", "V", vg_)
                hgd[idx] = ("DHG" + str(par2), P.dma(
                    "gpsimd", aps["hgscr2"][c * 128:c * 128 + wdt, t0:t0 + SL],
                    hgt[par2][0:wdt, :], sem="DHG" + str(par2)))
            tmm_after[c] = P.counts["T"]
        P.wait("tensor", "V")
        P.wait("tensor", "S")
        P.wait("sync", "DHG0")
        P.wait("sync", "DHG1")
        mm1scratch.close()
        hgtt = sb(f"d_hgtt{l}", [128, NHC * SL], BF16)
        tprev = [None]
        for tt in range(NSL):
            t0 = tt * SL
            if tprev[0] is not None:
                P.wait("sync", "T", tprev[0])
            hgld = P.dma("sync", hgtt[:].rearrange("p (c t) -> p c t", c=NHC),
                         aps["hgscr2"][:, t0:t0 + SL].rearrange("(c p) t -> p c t", p=128),
                         sem="DHGR")
            P.wait("tensor", "DHGR", hgld)
            for c in range(NHC):
                wdt = HIDW[c]
                for fc in range(DC):
                    def f(t, c=c, fc=fc, wdt=wdt):
                        return t.matmul(
                            out=pp[fc][:],
                            lhsT=w2bf[0:wdt, c * 1024 + fc * 128: c * 1024 + (fc + 1) * 128],
                            rhs=hgtt[0:wdt, c * SL:(c + 1) * SL],
                            start=(c == 0), stop=(c == NHC - 1))
                    rec = P.op("tensor", f)
            tm2 = P.inc(rec, "T", 1)
            tprev[0] = tm2
            P.wait("vector", "T", tm2)
            for fc in range(DC):
                def fz(v, fc=fc, t0=t0):
                    return v.scalar_tensor_tensor(
                        out=zb[:, fc * TW + 1 + t0: fc * TW + 1 + t0 + SL],
                        in0=pp[fc][:], scalar=b2col[:, fc:fc + 1],
                        in1=zT[:, fc * TW + 1 + t0: fc * TW + 1 + t0 + SL],
                        op0=ALU.add, op1=ALU.add)
                P.op("vector", fz)
                def fo(v, fc=fc, t0=t0):
                    return v.scalar_tensor_tensor(
                        out=zT[:, fc * TW + 1 + t0: fc * TW + 1 + t0 + SL],
                        in0=pp[fc][:], scalar=b2col[:, fc:fc + 1],
                        in1=zT[:, fc * TW + 1 + t0: fc * TW + 1 + t0 + SL],
                        op0=ALU.add, op1=ALU.add)
                rec = P.op("vector", fo)
            vz = P.inc(rec, "V", 1)
            P.wait("tensor", "V", vz)
        P.barrier()
        if dbg is not None and l == 0:
            P.barrier()
            P.dma("sync", dbg["hn0"], hn[:], sem="DO")
            P.dma("sync", dbg["zl0"], zT[:], sem="DO")
            for e_ in ("sync", "tensor", "vector", "scalar", "gpsimd"):
                P.wait(e_, "DO")
    P.barrier()



def dft_consts():
    n = np.arange(128, dtype=np.float64)
    M = np.cos(2 * np.pi * np.outer(n, n) / 128.0).astype(np.float32)
    S = np.sin(2 * np.pi * np.outer(n, n) / 128.0).astype(np.float32)
    return M, (-S).astype(np.float32)


def build_launch_a(detect_races=True):
    P = Prog(detect_races=detect_races, name="la")
    x = P.dram_in("x", [T, D], F32)
    wqkv = P.dram_in("wqkv", [D, 3 * D], F32)
    wout = P.dram_in("wout", [D, D], F32)
    dftc = P.dram_in("dftc", [128, 128], F32)
    dftsn = P.dram_in("dftsn", [128, 128], F32)
    xfm = P.dram_out("xfm", [128, DC * T], F32)

    C = emit_common_consts(P)
    XT = P.sbuf("XT", [128, DC * T], F32)
    XN = P.sbuf("XN", [128, DC * T], BF16)
    pp = [P.psum(f"pp{i}", [128, 512], F32) for i in range(6)]
    for s in ("T", "V", "S", "G", "DS", "DO"):
        P.sem(s)
    P.barrier()

    emit_load_xt_from_tokmajor(P, x, XT, C, pp[:4])

    with contextlib.ExitStack() as lnscope:
        sb = lambda n, s, d=F32: lnscope.enter_context(P.nc.sbuf_tensor(n, s, d))
        rows = {k: sb(f"lnrow_{k}", [1, SL]) for k in ("negm", "r", "mr", "msq", "var")}
        xsq = [sb(f"lnxsq{i}", [128, SL], BF16) for i in range(DC)]
        ta = sb("lnta", [128, SL])
        emit_ln(P, XT, XN, C, pp[0], pp[1], pp[2], pp[3], xsq, ta, rows)

    emit_block_a(P, XT, XN, C, wqkv, wout, dftc, dftsn, pp)

    P.dma("sync", xfm, XT[:], sem="DO")
    P.wait("sync", "DO")
    return P


def build_launch_b(detect_races=True, dbg_on=True):
    P = Prog(detect_races=detect_races, name="lb")
    xfm_in = P.dram_in("xfm", [128, DC * T], F32)
    rw = P.dram_in("rw", [8, D], F32)
    rb = P.dram_in("rb", [8], F32)
    w1 = P.dram_in("w1", [8, 2 * 4096, D], F32)
    b1 = P.dram_in("b1", [8, 2 * 4096], F32)
    w2 = P.dram_in("w2", [8, D, 4096], F32)
    b2 = P.dram_in("b2", [8, D], F32)
    xfm2 = P.dram_out("xfm2", [128, DC * T], F32)

    C = emit_common_consts(P)
    XT = P.sbuf("XT", [128, DC * T], F32)
    XN = P.sbuf("XN", [128, DC * T], BF16)
    pp = [P.psum(f"pp{i}", [128, 512], F32) for i in range(8)]
    for s in ("T", "V", "S", "G", "DS", "DG", "DO"):
        P.sem(s)
    P.dma("sync", XT[:], xfm_in[:, :], sem="DS")
    P.wait("vector", "DS")
    P.barrier()

    with contextlib.ExitStack() as lnscope:
        sb = lambda n, s, d=F32: lnscope.enter_context(P.nc.sbuf_tensor(n, s, d))
        rows = {k: sb(f"lnrow_{k}", [1, SL]) for k in ("negm", "r", "mr", "msq", "var")}
        xsq = [sb(f"lnxsq{i}", [128, SL], BF16) for i in range(DC)]
        ta = sb("lnta", [128, SL])
        emit_ln(P, XT, XN, C, pp[0], pp[1], pp[2], pp[3], xsq, ta, rows)

    dbg = None if not dbg_on else {
           "weT": P.dram_out("dbg_weT", [8, T], F32),
           "logitsT": P.dram_out("dbg_logitsT", [8, T], F32),
           "W8": P.dram_out("dbg_W8", [128, 16 * 8], F32),
           "hg": P.dram_out("dbg_hg", [128, 32 * 1024], BF16),
           "Wb": P.dram_out("dbg_Wb", [128, T], F32)}
    emit_block_b(P, XT, XN, C, rw, rb, w1, b1, w2, b2, pp, dbg=dbg)

    P.dma("sync", xfm2, XT[:], sem="DO")
    P.wait("sync", "DO")
    return P


def build_launch_c(detect_races=True, dbg_on=True):
    P = Prog(detect_races=detect_races, name="lc")
    xfm_in = P.dram_in("xfm", [128, DC * T], F32)
    wav = P.dram_in("wav", [16, 64, 64], F32)
    wv = P.dram_in("wv", [D, D], F32)
    wvb = P.dram_in("wvb", [D], F32)
    wo = P.dram_in("wo", [D, D], F32)
    wob = P.dram_in("wob", [D], F32)
    xfm2 = P.dram_out("xfm2", [128, DC * T], F32)

    C = emit_common_consts(P)
    XT = P.sbuf("XT", [128, DC * T], F32)
    XN = P.sbuf("XN", [128, DC * T], BF16)
    pp = [P.psum(f"pp{i}", [128, 512], F32) for i in range(8)]
    for sm in ("T", "V", "S", "G", "DS", "DG", "DO", "DOB"):
        P.sem(sm)
    P.dma("sync", XT[:], xfm_in[:, :], sem="DS")
    P.wait("vector", "DS")
    P.barrier()

    with contextlib.ExitStack() as lnscope:
        sb = lambda n, sh, d=F32: lnscope.enter_context(P.nc.sbuf_tensor(n, sh, d))
        rows = {k: sb(f"lnrow_{k}", [1, SL]) for k in ("negm", "r", "mr", "msq", "var")}
        xsq = [sb(f"lnxsq{i}", [128, SL], BF16) for i in range(DC)]
        ta = sb("lnta", [128, SL])
        emit_ln(P, XT, XN, C, pp[0], pp[1], pp[2], pp[3], xsq, ta, rows)

    dbg = None if not dbg_on else {
           "obf": P.dram_out("dbg_obf", [128, DC * T], BF16),
           "vx65e": P.dram_out("dbg_vx65e", [128, 16 * 16 * 65], BF16),
           "woT": P.dram_out("dbg_woT", [128, DC * 1024], BF16),
           "xwT": P.dram_out("dbg_xwT", [128, T], BF16),
           "vx65": P.dram_out("dbg_vx65", [128, 16 * 16 * 65], BF16),
           "sml": P.dram_out("dbg_sml", [1, 256], F32)}
    emit_block_c(P, XT, XN, C, wav, wv, wvb, wo, wob, pp, dbg=dbg)

    P.dma("sync", xfm2, XT[:], sem="DO")
    P.wait("sync", "DO")
    return P


def d_rotary_consts():
    hi = np.arange(0, 64, 2, dtype=np.float64)
    inv_freq = 1.0 / 10000.0 ** (hi / 64.0)
    t = np.arange(T, dtype=np.float64)
    fr = np.concatenate([np.outer(t, inv_freq), np.outer(t, inv_freq)], axis=1)
    base = (hi + 0.4 * 64) / (1.4 * 64)
    base = np.concatenate([base, base])
    power = (t - T // 2) / 256.0
    scale = base[None, :] ** power[:, None]
    cos, sin = np.cos(fr), np.sin(fr)
    sgn = np.where(np.arange(64) < 32, -1.0, 1.0)
    csq = (cos * scale * 8.0).T
    ssq = (sin * scale * 8.0).T * sgn[:, None]
    csk = (cos / scale).T
    ssk = (sin / scale).T * sgn[:, None]
    tile = lambda a: np.concatenate([a, a], axis=0).astype(np.float32)
    return tile(csq), tile(ssq), tile(csk), tile(ssk)


def build_launch_d(detect_races=True, dbg_on=True):
    P = Prog(detect_races=detect_races, name="ld")
    xfm_in = P.dram_in("xfm", [128, DC * T], F32)
    aps = {
        "wqkv": P.dram_in("wqkv", [2, D, 1536], F32),
        "qs": P.dram_in("qs", [2, 64], F32),
        "ks": P.dram_in("ks", [2, 64], F32),
        "wo": P.dram_in("wo", [2, 512, D], F32),
        "lng": P.dram_in("lng", [2, D], F32),
        "lnb": P.dram_in("lnb", [2, D], F32),
        "w1": P.dram_in("w1", [2, D, 5460], F32),
        "b1": P.dram_in("b1", [2, 5460], F32),
        "w2": P.dram_in("w2", [2, 2730, D], F32),
        "b2": P.dram_in("b2", [2, D], F32),
        "csq": P.dram_in("csq", [128, T], F32),
        "ssq": P.dram_in("ssq", [128, T], F32),
        "csk": P.dram_in("csk", [128, T], F32),
        "ssk": P.dram_in("ssk", [128, T], F32),
        "hgscr2": P.dram_tmp("hgscr2", [22 * 128, T], BF16),
    }
    out = P.dram_out("out", [T, D], F32)

    C = emit_common_consts(P)
    pp = [P.psum(f"pp{i}", [128, 512], F32) for i in range(8)]
    for sm in ("T", "V", "S", "G", "DS", "DG", "DO", "DOB", "DW1", "DT2", "DC2"):
        P.sem(sm)
    P.barrier()
    dbg = None if not dbg_on else {
           "z0": P.dram_out("dbg_z0", [128, DC * TW], F32),
           "qf0": P.dram_out("dbg_qf0", [128, 4 * T], BF16),
           "kf0": P.dram_out("dbg_kf0", [128, 4 * T], BF16),
           "v650": P.dram_out("dbg_v650", [128, 16 * 8 * 64], BF16),
           "obf0": P.dram_out("dbg_obf0", [128, 4 * T], BF16),
           "zattn0": P.dram_out("dbg_zattn0", [128, DC * TW], F32),
           "hn0": P.dram_out("dbg_hn0", [128, DC * T], BF16),
           "zl0": P.dram_out("dbg_zl0", [128, DC * TW], F32),
           "rows0": P.dram_out("dbg_rows0", [2, SL], F32),
           "u0": P.dram_out("dbg_u0", [128, SL], F32),
           "zfin": P.dram_out("dbg_zfin", [128, DC * TW], F32),
           "zb1": P.dram_out("dbg_zb1", [128, DC * TW], BF16),
           "zb1f": P.dram_out("dbg_zb1f", [128, DC * TW], F32),
           "ps47": P.dram_out("dbg_ps47", [128, 4 * SL], BF16)}
    emit_block_d(P, xfm_in, out, C, aps, pp, dbg=dbg)
    return P


def build_launch_full(detect_races=True):
    P = Prog(detect_races=detect_races, name="lf")
    x = P.dram_in("x", [T, D], F32)
    a_wqkv = P.dram_in("i_a_wqkv", [D, 3 * D], F32)
    a_wout = P.dram_in("i_a_wout", [D, D], F32)
    dftc = P.dram_in("dftc", [128, 128], F32)
    dftsn = P.dram_in("dftsn", [128, 128], F32)
    b_rw = P.dram_in("i_b_rw", [8, D], F32)
    b_rb = P.dram_in("i_b_rb", [8], F32)
    b_w1 = P.dram_in("i_b_w1", [8, 2 * 4096, D], F32)
    b_b1 = P.dram_in("i_b_b1", [8, 2 * 4096], F32)
    b_w2 = P.dram_in("i_b_w2", [8, D, 4096], F32)
    b_b2 = P.dram_in("i_b_b2", [8, D], F32)
    c_wav = P.dram_in("i_c_wav", [16, 64, 64], F32)
    c_wv = P.dram_in("i_c_wv", [D, D], F32)
    c_wvb = P.dram_in("i_c_wvb", [D], F32)
    c_wo = P.dram_in("i_c_wo", [D, D], F32)
    c_wob = P.dram_in("i_c_wob", [D], F32)
    d_aps = {
        "wqkv": P.dram_in("i_d_wqkv", [2, D, 1536], F32),
        "qs": P.dram_in("i_d_qs", [2, 64], F32),
        "ks": P.dram_in("i_d_ks", [2, 64], F32),
        "wo": P.dram_in("i_d_wo", [2, 512, D], F32),
        "lng": P.dram_in("i_d_lng", [2, D], F32),
        "lnb": P.dram_in("i_d_lnb", [2, D], F32),
        "w1": P.dram_in("i_d_w1", [2, D, 5460], F32),
        "b1": P.dram_in("i_d_b1", [2, 5460], F32),
        "w2": P.dram_in("i_d_w2", [2, 2730, D], F32),
        "b2": P.dram_in("i_d_b2", [2, D], F32),
        "csq": P.dram_in("csq", [128, T], F32),
        "ssq": P.dram_in("ssq", [128, T], F32),
        "csk": P.dram_in("csk", [128, T], F32),
        "ssk": P.dram_in("ssk", [128, T], F32),
        "hgscr2": P.dram_tmp("hgscr2", [22 * 128, T], BF16),
    }
    xfm_cd = P.dram_tmp("xfm_cd", [128, DC * T], F32)
    out = P.dram_out("out", [T, D], F32)

    C = emit_common_consts(P)
    pp = [P.psum(f"pp{i}", [128, 512], F32) for i in range(8)]
    for sm in ("T", "V", "S", "G", "DS", "DG", "DO", "DOB"):
        P.sem(sm)
    P.barrier()

    def lnpass(tag, XT, XN):
        with contextlib.ExitStack() as lnscope:
            sb = lambda n, sh, d=F32: lnscope.enter_context(P.nc.sbuf_tensor(n, sh, d))
            rows = {k: sb(f"{tag}lnrow_{k}", [1, SL]) for k in ("negm", "r", "mr", "msq", "var")}
            xsq = [sb(f"{tag}lnxsq{i}", [128, SL], BF16) for i in range(DC)]
            ta = sb(f"{tag}lnta", [128, SL])
            emit_ln(P, XT, XN, C, pp[0], pp[1], pp[2], pp[3], xsq, ta, rows)
        P.barrier()

    with contextlib.ExitStack() as abc:
        sb = lambda n, sh, d=F32: abc.enter_context(P.nc.sbuf_tensor(n, sh, d))
        XT = sb("XT", [128, DC * T])
        XN = sb("XN", [128, DC * T], BF16)
        P.set_phase("load")
        emit_load_xt_from_tokmajor(P, x, XT, C, pp[:4])
        P.barrier()
        P.set_phase("ln_a")
        lnpass("a", XT, XN)
        P.push_scope()
        P.set_phase("blk_a")
        emit_block_a(P, XT, XN, C, a_wqkv, a_wout, dftc, dftsn, pp[:6])
        P.barrier()
        P.pop_scope()
        P.set_phase("ln_b")
        lnpass("b", XT, XN)
        P.push_scope()
        P.set_phase("blk_b")
        emit_block_b(P, XT, XN, C, b_rw, b_rb, b_w1, b_b1, b_w2, b_b2, pp)
        P.barrier()
        P.pop_scope()
        P.set_phase("ln_c")
        lnpass("c", XT, XN)
        P.push_scope()
        P.set_phase("blk_c")
        emit_block_c(P, XT, XN, C, c_wav, c_wv, c_wvb, c_wo, c_wob, pp)
        P.barrier()
        P.pop_scope()
        P.set_phase("handoff")
        P.dma("sync", xfm_cd, XT[:], sem="DO")
        for e_ in ENGINES:
            P.wait(e_, "DO")
    P.barrier()
    P.set_phase("blk_d")
    emit_block_d(P, xfm_cd, out, C, d_aps, pp, dbg=None)
    P.set_phase(None)
    return P


_PROGS = {}


def _get_prog(name):
    if name not in _PROGS:
        builders = {"a": build_launch_a,
                    "b": lambda: build_launch_b(dbg_on=False),
                    "c": lambda: build_launch_c(dbg_on=False),
                    "d": lambda: build_launch_d(dbg_on=False),
                    "full": build_launch_full}
        P = builders[name]()
        _PROGS[name] = (P, P.finalize())
    return _PROGS[name]


FUSED = True


def kernel(**inputs):
    f32 = lambda k: np.ascontiguousarray(np.asarray(inputs[k], dtype=np.float32))
    x = f32("L0_attn_x")
    dftc, dftsn = dft_consts()
    csq, ssq, csk, ssk = d_rotary_consts()
    cores = list(range(NCORES))

    if FUSED:
        _, ncf = _get_prog("full")
        in_f = [{"x": x[b],
                 "i_a_wqkv": f32("L0_attn_w_qkv"), "i_a_wout": f32("L0_attn_w_out"),
                 "dftc": dftc, "dftsn": dftsn,
                 "i_b_rw": f32("L0_ffn_router_w"), "i_b_rb": f32("L0_ffn_router_b"),
                 "i_b_w1": f32("L0_ffn_w1"), "i_b_b1": f32("L0_ffn_b1"),
                 "i_b_w2": f32("L0_ffn_w2"), "i_b_b2": f32("L0_ffn_b2"),
                 "i_c_wav": f32("L1_attn_w_att_val"),
                 "i_c_wv": f32("L1_attn_wv_w"), "i_c_wvb": f32("L1_attn_wv_b"),
                 "i_c_wo": f32("L1_attn_out_w"), "i_c_wob": f32("L1_attn_out_b"),
                 "i_d_wqkv": f32("L1_ffn_wqkv"),
                 "i_d_qs": f32("L1_ffn_q_scale"), "i_d_ks": f32("L1_ffn_k_scale"),
                 "i_d_wo": f32("L1_ffn_wo"),
                 "i_d_lng": f32("L1_ffn_ln_g"), "i_d_lnb": f32("L1_ffn_ln_b"),
                 "i_d_w1": f32("L1_ffn_w1"), "i_d_b1": f32("L1_ffn_b1"),
                 "i_d_w2": f32("L1_ffn_w2"), "i_d_b2": f32("L1_ffn_b2"),
                 "csq": csq, "ssq": ssq, "csk": csk, "ssk": ssk} for b in cores]
        res = run_bass_kernel_spmd(ncf, in_f, core_ids=cores)
        return np.stack([np.asarray(res.results[b]["out"]) for b in cores], axis=0)

    _, nca = _get_prog("a")
    in_a = [{"x": x[b], "wqkv": f32("L0_attn_w_qkv"), "wout": f32("L0_attn_w_out"),
             "dftc": dftc, "dftsn": dftsn} for b in cores]
    res = run_bass_kernel_spmd(nca, in_a, core_ids=cores)
    xfm = [np.asarray(res.results[b]["xfm"]) for b in cores]

    _, ncb = _get_prog("b")
    in_b = [{"xfm": xfm[b], "rw": f32("L0_ffn_router_w"), "rb": f32("L0_ffn_router_b"),
             "w1": f32("L0_ffn_w1"), "b1": f32("L0_ffn_b1"),
             "w2": f32("L0_ffn_w2"), "b2": f32("L0_ffn_b2")} for b in cores]
    res = run_bass_kernel_spmd(ncb, in_b, core_ids=cores)
    xfm = [np.asarray(res.results[b]["xfm2"]) for b in cores]

    _, ncc = _get_prog("c")
    in_c = [{"xfm": xfm[b], "wav": f32("L1_attn_w_att_val"),
             "wv": f32("L1_attn_wv_w"), "wvb": f32("L1_attn_wv_b"),
             "wo": f32("L1_attn_out_w"), "wob": f32("L1_attn_out_b")} for b in cores]
    res = run_bass_kernel_spmd(ncc, in_c, core_ids=cores)
    xfm = [np.asarray(res.results[b]["xfm2"]) for b in cores]

    _, ncd = _get_prog("d")
    in_d = [{"xfm": xfm[b], "wqkv": f32("L1_ffn_wqkv"),
             "qs": f32("L1_ffn_q_scale"), "ks": f32("L1_ffn_k_scale"),
             "wo": f32("L1_ffn_wo"), "lng": f32("L1_ffn_ln_g"), "lnb": f32("L1_ffn_ln_b"),
             "w1": f32("L1_ffn_w1"), "b1": f32("L1_ffn_b1"),
             "w2": f32("L1_ffn_w2"), "b2": f32("L1_ffn_b2"),
             "csq": csq, "ssq": ssq, "csk": csk, "ssk": ssk} for b in cores]
    res = run_bass_kernel_spmd(ncd, in_d, core_ids=cores)
    return np.stack([np.asarray(res.results[b]["out"]) for b in cores], axis=0)


# version 10
# speedup vs baseline: 1.0598x; 1.0598x over previous
import contextlib
import numpy as np
import ml_dtypes
import concourse.bass as bass
import concourse.mybir as mybir
from concourse.bass_utils import run_bass_kernel_spmd

F32 = mybir.dt.float32
BF16 = mybir.dt.bfloat16
AF = mybir.ActivationFunctionType
ALU = mybir.AluOpType
AX = mybir.AxisListType

NCORES = 8
T = 2048
D = 1024
DC = D // 128
NSL = 4
SL = 512

ENGINES = ("sync", "tensor", "vector", "scalar", "gpsimd")


class Prog:

    def __init__(self, name="prog", detect_races=True):
        self.nc = bass.Bass("TRN2", target_bir_lowering=False,
                            detect_race_conditions=detect_races)
        self.streams = {e: [] for e in ENGINES}
        self.counts = {}
        self.sems = {}
        self._semnames = []
        self._es = contextlib.ExitStack()
        self.last_compute = {e: None for e in ENGINES}
        self._dummy_tiles = self._es.enter_context(
            self.nc.sbuf_tensor("dummy_sem_tile", [1, 16], F32))
        for e in ENGINES:
            self.sem("B" + e)

    def sem(self, name):
        if name not in self.counts:
            self.counts[name] = 0
            self._semnames.append(name)
        return name

    def sbuf(self, name, shape, dtype=F32):
        es = self._scopes[-1] if getattr(self, "_scopes", None) else self._es
        return es.enter_context(self.nc.sbuf_tensor(name, shape, dtype))

    def push_scope(self):
        if not hasattr(self, "_scopes"):
            self._scopes = []
        self._scopes.append(contextlib.ExitStack())

    def pop_scope(self):
        self._scopes.pop().close()

    def psum(self, name, shape, dtype=F32):
        return self._es.enter_context(self.nc.psum_tensor(name, shape, dtype))

    def dram_in(self, name, shape, dtype=F32):
        return self.nc.dram_tensor(name, shape, dtype, kind="ExternalInput").ap()

    def dram_out(self, name, shape, dtype=F32):
        return self.nc.dram_tensor(name, shape, dtype, kind="ExternalOutput").ap()

    def dram_tmp(self, name, shape, dtype=F32):
        return self.nc.dram_tensor(name, shape, dtype, kind="Internal").ap()

    def set_phase(self, name):
        self._phase = name

    def op(self, eng, fn, real=True):
        rec = {"fn": fn, "incs": [], "eng": eng,
               "ph": getattr(self, "_phase", None)}
        self.streams[eng].append(rec)
        if real:
            self.last_compute[eng] = rec
        return rec

    def _dummy_op(self, eng):
        dt = self._dummy_tiles
        if eng == "tensor":
            return self.op("tensor", lambda t: t.nop())
        if eng == "vector":
            return self.op("vector", lambda v: v.memset(dt[:, 0:4], 0.0))
        if eng == "scalar":
            return self.op("scalar", lambda s: s.memzero(dt[:, 4:8]))
        if eng == "gpsimd":
            return self.op("gpsimd", lambda g: g.memset(dt[:, 8:12], 0.0))
        raise ValueError(eng)

    def inc(self, rec, sem, n=1):
        self.sem(sem)
        if rec["incs"]:
            rec = self._dummy_op(rec["eng"])
        rec["incs"].append((sem, n))
        self.counts[sem] += n
        return self.counts[sem]

    def wait(self, eng, sem, val=None):
        self.sem(sem)
        if val is None:
            val = self.counts[sem]
        if val <= 0:
            return
        def f(e, sem=sem, val=val):
            e.wait_ge(self.sems[sem], val)
        self.op(eng, f, real=False)

    def selfsync(self, eng):
        sem = "X" + eng
        self.sem(sem)
        v = self.inc(self.last_compute[eng], sem, 1)
        self.wait(eng, sem, v)

    def dma(self, eng, out, in_, sem="DS", **kw):
        self.sem(sem)
        def f(e, out=out, in_=in_, kw=kw):
            return e.dma_start(out=out, in_=in_, **kw)
        rec = {"fn": f, "incs": [], "eng": eng,
               "ph": getattr(self, "_phase", None)}
        self.streams[eng].append(rec)
        val = self.inc(rec, sem, 16)
        return val

    def barrier(self, engines=ENGINES):
        vals = {}
        for e in engines:
            if self.last_compute[e] is not None:
                self.inc(self.last_compute[e], "B" + e, 1)
                self.last_compute[e] = None
            vals[e] = self.counts["B" + e]
        for e in engines:
            for o in engines:
                if o != e and vals[o] > 0:
                    self.wait(e, "B" + o, vals[o])

    def finalize(self):
        for name in self._semnames:
            self.sems[name] = self._es.enter_context(self.nc.semaphore(name))
        with self.nc.Block() as block:
            for eng in ENGINES:
                def runner(e, eng=eng):
                    for rec in self.streams[eng]:
                        ph = rec.get("ph")
                        if ph is None:
                            ins = rec["fn"](e)
                            for (sem, n) in rec["incs"]:
                                ins.then_inc(self.sems[sem], n)
                        else:
                            with self.nc.named_scope(ph):
                                ins = rec["fn"](e)
                                for (sem, n) in rec["incs"]:
                                    ins.then_inc(self.sems[sem], n)
                getattr(block, eng)(runner)
        self._es.close()
        return self.nc



def emit_common_consts(P):
    C = {}
    C["idn_bf"] = P.sbuf("idn_bf", [128, 128], BF16)
    C["idn_f32"] = P.sbuf("idn_f32", [128, 128], F32)
    C["onescol_bf"] = P.sbuf("onescol_bf", [128, 1], BF16)
    C["ones1_f32"] = P.sbuf("ones1_f32", [1, 128], F32)
    C["epscol"] = P.sbuf("epscol", [1, 1], F32)

    def emit(g):
        g.memset(C["idn_bf"][:], 1.0)
        return g.affine_select(out=C["idn_bf"][:], in_=C["idn_bf"][:],
                               pattern=[[-1, 128]], compare_op=ALU.is_equal,
                               fill=0.0, base=0, channel_multiplier=1)
    P.op("gpsimd", emit)
    def emit2(g):
        g.memset(C["idn_f32"][:], 1.0)
        return g.affine_select(out=C["idn_f32"][:], in_=C["idn_f32"][:],
                               pattern=[[-1, 128]], compare_op=ALU.is_equal,
                               fill=0.0, base=0, channel_multiplier=1)
    P.op("gpsimd", emit2)
    P.op("gpsimd", lambda g: g.memset(C["onescol_bf"][:], 1.0))
    P.op("gpsimd", lambda g: g.memset(C["ones1_f32"][:], 1.0))
    P.op("gpsimd", lambda g: g.memset(C["epscol"][:], 1e-5))
    return C


def emit_load_xt_from_tokmajor(P, x_ap, XT, C, pp):
    with P.nc.sbuf_tensor("xtok_stage", [128, (T // 128) * D], F32) as Xtok:
        inap = x_ap.rearrange("(tc p) d -> p tc d", p=128)
        outap = Xtok[:].rearrange("p (tc d) -> p tc d", d=D)
        P.dma("sync", outap, inap, sem="DS")
        P.wait("tensor", "DS")
        ntc = T // 128
        nps = len(pp)
        XTv = XT[:].rearrange("p (c t) -> p c t", c=DC)
        P.sem("PSF")
        psf_base = P.counts["PSF"]
        fill = 0
        for tc in range(ntc):
            for q in range(2):
                slot = fill % nps
                if fill >= nps:
                    P.wait("tensor", "PSF", psf_base + fill - nps + 1)
                for j in range(4):
                    dc = 4 * q + j
                    def f(t, tc=tc, dc=dc, slot=slot, j=j):
                        return t.transpose(out=pp[slot][:, j * 128:(j + 1) * 128],
                                           in_=Xtok[:, tc * D + dc * 128: tc * D + (dc + 1) * 128],
                                           identity=C["idn_f32"][:])
                    rec = P.op("tensor", f)
                tv = P.inc(rec, "T", 1)
                P.wait("vector", "T", tv)
                def g(v, tc=tc, q=q, slot=slot):
                    return v.tensor_copy(
                        out=XTv[:, 4 * q:4 * q + 4, tc * 128:(tc + 1) * 128],
                        in_=pp[slot][:].rearrange("p (j t) -> p j t", j=4))
                rec2 = P.op("vector", g)
                P.inc(rec2, "PSF", 1)
                fill += 1
        P.barrier()


def emit_ln(P, XT, XN, C, pb1, pb2, pst1, pst2, xsq, ta, rows):
    ps1 = pst1[0:1, :]
    ps2 = pst2[0:1, :]
    negm, rrow, mrrow, msq, var = rows["negm"], rows["r"], rows["mr"], rows["msq"], rows["var"]
    for i in range(NSL):
        t0 = i * SL
        for c in range(DC):
            def f(v, c=c, t0=t0):
                return v.tensor_copy(out=XN[:, c * T + t0: c * T + t0 + SL],
                                     in_=XT[:, c * T + t0: c * T + t0 + SL])
            rec = P.op("vector", f)
        vcast = P.inc(rec, "V", 1)
        P.wait("tensor", "V", vcast)
        for c in range(DC):
            def f(t, c=c, t0=t0):
                return t.matmul(out=ps1, lhsT=C["onescol_bf"][:],
                                rhs=XN[:, c * T + t0: c * T + t0 + SL],
                                start=(c == 0), stop=(c == DC - 1))
            rec = P.op("tensor", f)
        ts1 = P.inc(rec, "T", 1)
        P.wait("scalar", "V", vcast)
        for c in range(DC):
            def f(s, c=c, t0=t0):
                return s.activation(out=xsq[c][:], in_=XN[:, c * T + t0: c * T + t0 + SL],
                                    func=AF.Square)
            rec = P.op("scalar", f)
            sv = P.inc(rec, "S", 1)
            P.wait("tensor", "S", sv)
            def f2(t, c=c):
                return t.matmul(out=ps2, lhsT=C["onescol_bf"][:], rhs=xsq[c][:],
                                start=(c == 0), stop=(c == DC - 1))
            rec2 = P.op("tensor", f2)
        ts2 = P.inc(rec2, "T", 1)
        P.wait("scalar", "T", ts1)
        rec = P.op("scalar", lambda s: s.activation(
            out=negm[:], in_=ps1, func=AF.Copy, scale=-1.0 / D))
        P.inc(rec, "S", 1)
        rec = P.op("scalar", lambda s: s.activation(
            out=msq[:], in_=negm[:], func=AF.Square))
        smsq = P.inc(rec, "S", 1)
        P.wait("vector", "T", ts2)
        P.wait("vector", "S", smsq)
        rec = P.op("vector", lambda v: v.scalar_tensor_tensor(
            out=var[:], in0=ps2, scalar=1.0 / D,
            in1=msq[:], op0=ALU.mult, op1=ALU.subtract))
        vvar = P.inc(rec, "V", 1)
        P.wait("scalar", "V", vvar)
        rec = P.op("scalar", lambda s: s.activation(
            out=var[:], in_=var[:], func=AF.Sqrt,
            bias=C["epscol"][:]))
        sstd = P.inc(rec, "S", 1)
        P.wait("vector", "S", sstd)
        P.op("vector", lambda v: v.reciprocal(
            out=rrow[:], in_=var[:]))
        rec = P.op("vector", lambda v: v.tensor_mul(
            out=mrrow[:], in0=negm[:], in1=rrow[:]))
        vfin = P.inc(rec, "V", 1)
        P.wait("tensor", "V", vfin)
        P.op("tensor", lambda t: t.matmul(
            out=pb1[:], lhsT=C["ones1_f32"][:], rhs=rrow[:],
            start=True, stop=True))
        rec = P.op("tensor", lambda t: t.matmul(
            out=pb2[:], lhsT=C["ones1_f32"][:], rhs=mrrow[:],
            start=True, stop=True))
        tb = P.inc(rec, "T", 1)
        P.wait("vector", "T", tb)
        for c in range(DC):
            def f(v, c=c, t0=t0):
                return v.tensor_mul(out=ta[:], in0=XT[:, c * T + t0: c * T + t0 + SL],
                                    in1=pb1[:])
            P.op("vector", f)
            def f2(v, c=c, t0=t0):
                return v.tensor_add(out=XN[:, c * T + t0: c * T + t0 + SL],
                                    in0=ta[:], in1=pb2[:])
            rec = P.op("vector", f2)
        vap = P.inc(rec, "V", 1)
        P.wait("tensor", "V", vap)
    P.barrier()



def emit_block_a(P, XT, XN, C, wqkv_ap, wout_ap, dftc_ap, dftsn_ap, pp):
    pq, pk, pv, pf0, pf1, pf2 = pp
    H = 8

    dstage = P.sbuf("dft_stage", [128, 256], F32)
    Cbf = P.sbuf("dft_c_bf", [128, 128], BF16)
    Snbf = P.sbuf("dft_sn_bf", [128, 128], BF16)
    P.dma("sync", dstage[:, 0:128], dftc_ap[:, :], sem="DS")
    P.dma("sync", dstage[:, 128:256], dftsn_ap[:, :], sem="DS")
    P.wait("vector", "DS")
    P.op("vector", lambda v: v.tensor_copy(out=Cbf[:], in_=dstage[:, 0:128]))
    rec = P.op("vector", lambda v: v.tensor_copy(out=Snbf[:], in_=dstage[:, 128:256]))
    P.inc(rec, "V", 1)

    wq = [P.sbuf(f"wq{i}", [128, 1024], BF16) for i in range(2)]
    wk = [P.sbuf(f"wk{i}", [128, 1024], BF16) for i in range(2)]
    wv = [P.sbuf(f"wv{i}", [128, 1024], BF16) for i in range(2)]
    woutbf = P.sbuf("woutbf", [128, DC * 1024], BF16)

    qb = P.sbuf("a_qb", [128, SL], BF16)
    kb = P.sbuf("a_kb", [128, SL], BF16)
    vb = P.sbuf("a_vb", [128, SL], BF16)
    f6 = [P.sbuf(f"a_f6_{i}", [128, SL], F32) for i in range(6)]
    kvre = P.sbuf("a_kvre", [128, SL], F32)
    kvim = P.sbuf("a_kvim", [128, SL], F32)
    cre = P.sbuf("a_cre", [128, SL], F32)
    cim = P.sbuf("a_cim", [128, SL], F32)
    tm = [P.sbuf(f"a_tm{i}", [128, SL], F32) for i in range(2)]
    zre = P.sbuf("a_zre", [128, SL], BF16)
    zim = P.sbuf("a_zim", [128, SL], BF16)
    carry = P.sbuf("a_carry", [128, 2], F32)
    vals = P.sbuf("a_vals", [128, H * T], BF16)

    def dma_wcol(buf, f0, sem):
        inap = wqkv_ap[:, f0:f0 + 128].rearrange("(c p) j -> p c j", p=128)
        outap = buf[:].rearrange("p (c j) -> p c j", c=DC)
        return P.dma("gpsimd", outap, inap, sem=sem)

    dma_wcol(wq[0], 0, "DW0")
    dma_wcol(wk[0], D, "DW0")
    dma_wcol(wv[0], 2 * D, "DW0")
    wout_in = wout_ap.rearrange("(c p) f -> p c f", p=128)
    wout_out = woutbf[:].rearrange("p (c f) -> p c f", c=DC)
    P.dma("gpsimd", wout_out, wout_in, sem="DWO")

    for h in range(H):
        if h + 1 < H:
            sem = "DW" + str((h + 1) % 2)
            dma_wcol(wq[(h + 1) % 2], (h + 1) * 128, sem)
            dma_wcol(wk[(h + 1) % 2], D + (h + 1) * 128, sem)
            dma_wcol(wv[(h + 1) % 2], 2 * D + (h + 1) * 128, sem)
        P.wait("tensor", "DW" + str(h % 2), 48 * (h // 2 + 1))
        rec = P.op("vector", lambda v: v.memset(carry[:], 0.0))
        P.inc(rec, "V", 1)
        for i in range(NSL):
            t0 = i * SL
            wqb, wkb, wvb = wq[h % 2], wk[h % 2], wv[h % 2]
            for (psumt, wbuf) in ((pq, wqb), (pk, wkb), (pv, wvb)):
                for c in range(DC):
                    def f(t, psumt=psumt, wbuf=wbuf, c=c, t0=t0):
                        return t.matmul(out=psumt[:],
                                        lhsT=wbuf[:, c * 128:(c + 1) * 128],
                                        rhs=XN[:, c * T + t0: c * T + t0 + SL],
                                        start=(c == 0), stop=(c == DC - 1))
                    rec = P.op("tensor", f)
            tqkv = P.inc(rec, "T", 1)
            P.wait("scalar", "T", tqkv)
            P.op("scalar", lambda s: s.activation(out=qb[:], in_=pq[:], func=AF.Copy))
            P.op("scalar", lambda s: s.activation(out=kb[:], in_=pk[:], func=AF.Copy))
            rec = P.op("scalar", lambda s: s.activation(out=vb[:], in_=pv[:], func=AF.Copy))
            sev = P.inc(rec, "S", 1)
            P.wait("tensor", "S", sev)
            plan = [(pq, Cbf, qb), (pk, Snbf, qb), (pv, Cbf, kb),
                    (pf0, Snbf, kb), (pf1, Cbf, vb), (pf2, Snbf, vb)]
            for (psumt, mat, src) in plan:
                def f(t, psumt=psumt, mat=mat, src=src):
                    return t.matmul(out=psumt[:], lhsT=mat[:], rhs=src[:],
                                    start=True, stop=True)
                rec = P.op("tensor", f)
            tdft = P.inc(rec, "T", 1)
            P.wait("scalar", "T", tdft)
            P.wait("vector", "T", tdft)
            for (psumt, idx, eng) in [(pq, 0, "scalar"), (pk, 1, "vector"),
                                      (pv, 2, "scalar"), (pf0, 3, "vector"),
                                      (pf1, 4, "scalar"), (pf2, 5, "vector")]:
                if eng == "scalar":
                    def f(s, psumt=psumt, idx=idx):
                        return s.activation(out=f6[idx][:], in_=psumt[:], func=AF.Copy)
                else:
                    def f(v, psumt=psumt, idx=idx):
                        return v.tensor_copy(out=f6[idx][:], in_=psumt[:])
                P.op(eng, f)
            sdft = P.inc(P.last_compute["scalar"], "S", 1)
            vdft = P.inc(P.last_compute["vector"], "V", 1)
            P.wait("gpsimd", "S", sdft)
            P.wait("gpsimd", "V", vdft)
            P.op("gpsimd", lambda g: g.tensor_mul(out=tm[0][:], in0=f6[2][:], in1=f6[4][:]))
            P.op("gpsimd", lambda g: g.tensor_mul(out=tm[1][:], in0=f6[3][:], in1=f6[5][:]))
            P.op("gpsimd", lambda g: g.tensor_sub(out=kvre[:], in0=tm[0][:], in1=tm[1][:]))
            P.op("gpsimd", lambda g: g.tensor_mul(out=tm[0][:], in0=f6[2][:], in1=f6[5][:]))
            P.op("gpsimd", lambda g: g.tensor_mul(out=tm[1][:], in0=f6[3][:], in1=f6[4][:]))
            rec = P.op("gpsimd", lambda g: g.tensor_add(out=kvim[:], in0=tm[0][:], in1=tm[1][:]))
            gkv = P.inc(rec, "G", 1)
            P.wait("vector", "G", gkv)
            P.op("vector", lambda v: v.tensor_tensor_scan(
                out=cre[:], data0=kvre[:], data1=kvre[:], initial=0.0,
                op0=ALU.add, op1=ALU.bypass))
            P.selfsync("vector")
            P.op("vector", lambda v: v.tensor_add(
                out=cre[:], in0=cre[:], in1=carry[:, 0:1].broadcast_to([128, SL])))
            P.op("vector", lambda v: v.tensor_tensor_scan(
                out=cim[:], data0=kvim[:], data1=kvim[:], initial=0.0,
                op0=ALU.add, op1=ALU.bypass))
            P.selfsync("vector")
            P.op("vector", lambda v: v.tensor_add(
                out=cim[:], in0=cim[:], in1=carry[:, 1:2].broadcast_to([128, SL])))
            P.op("vector", lambda v: v.tensor_copy(out=carry[:, 0:1], in_=cre[:, SL - 1:SL]))
            rec = P.op("vector", lambda v: v.tensor_copy(out=carry[:, 1:2], in_=cim[:, SL - 1:SL]))
            vscan = P.inc(rec, "V", 1)
            P.wait("gpsimd", "V", vscan)
            P.op("gpsimd", lambda g: g.tensor_mul(out=tm[0][:], in0=cre[:], in1=f6[0][:]))
            P.op("gpsimd", lambda g: g.tensor_mul(out=tm[1][:], in0=cim[:], in1=f6[1][:]))
            P.op("gpsimd", lambda g: g.tensor_add(out=zre[:], in0=tm[0][:], in1=tm[1][:]))
            P.op("gpsimd", lambda g: g.tensor_mul(out=tm[0][:], in0=cim[:], in1=f6[0][:]))
            P.op("gpsimd", lambda g: g.tensor_mul(out=tm[1][:], in0=cre[:], in1=f6[1][:]))
            rec = P.op("gpsimd", lambda g: g.tensor_sub(out=zim[:], in0=tm[0][:], in1=tm[1][:]))
            gz = P.inc(rec, "G", 1)
            P.wait("tensor", "G", gz)
            P.op("tensor", lambda t: t.matmul(out=pq[:], lhsT=Cbf[:], rhs=zre[:],
                                              start=True, stop=False))
            rec = P.op("tensor", lambda t: t.matmul(out=pq[:], lhsT=Snbf[:], rhs=zim[:],
                                                    start=False, stop=True))
            tifft = P.inc(rec, "T", 1)
            P.wait("scalar", "T", tifft)
            rec = P.op("scalar", lambda s, h=h, t0=t0: s.activation(
                out=vals[:, h * T + t0: h * T + t0 + SL], in_=pq[:], func=AF.Copy,
                scale=1.0 / 128.0))
            P.inc(rec, "S", 1)
            P.barrier(("tensor", "vector", "scalar", "gpsimd"))

    P.wait("tensor", "DWO")
    for fc in range(DC):
        for i in range(NSL):
            t0 = i * SL
            for c in range(DC):
                def f(t, fc=fc, c=c, t0=t0):
                    return t.matmul(out=pk[:],
                                    lhsT=woutbf[:, c * 1024 + fc * 128: c * 1024 + (fc + 1) * 128],
                                    rhs=vals[:, c * T + t0: c * T + t0 + SL],
                                    start=(c == 0), stop=(c == DC - 1))
                rec = P.op("tensor", f)
            tw = P.inc(rec, "T", 1)
            P.wait("vector", "T", tw)
            def f2(v, fc=fc, t0=t0):
                return v.tensor_add(out=XT[:, fc * T + t0: fc * T + t0 + SL],
                                    in0=pk[:], in1=XT[:, fc * T + t0: fc * T + t0 + SL])
            rec = P.op("vector", f2)
            vres = P.inc(rec, "V", 1)
            P.wait("tensor", "V", vres)
    P.barrier()



def emit_block_b(P, XT, XN, C, rw_ap, rb_ap, w1_ap, b1_ap, w2_ap, b2_ap, pp, dbg=None):
    NE = 8
    HID = 4096
    NPAIR = 32

    w2scr = P.dram_tmp("w2scr", [D, HID], BF16)
    hgscr = P.dram_tmp("hgscr", [128, NPAIR * 1024], BF16)

    weT = P.sbuf("b_weT", [8, T], F32)
    W8 = P.sbuf("b_W8", [128, 16 * 8], F32)

    with contextlib.ExitStack() as rscope:
        sb = lambda n, s, d=F32: rscope.enter_context(P.nc.sbuf_tensor(n, s, d))
        rw = sb("b_rw", [8, D])
        rwb = sb("b_rwb", [8, D], BF16)
        rbcol = sb("b_rb", [8, 1])
        rTb = sb("b_rTb", [128, 8 * 8], BF16)
        logitsT = sb("b_logitsT", [8, T])
        m1c = sb("b_m1c", [128, 1]); m2c = sb("b_m2c", [128, 1])
        nm1c = sb("b_nm1c", [128, 1]); sc8 = sb("b_sc8", [128, 8])
        E8 = sb("b_E8", [128, 8]); G8 = sb("b_G8", [128, 8])
        ed1 = sb("b_ed1", [128, 1]); r1 = sb("b_r1", [128, 1])

        P.dma("sync", rw[:], rw_ap[:, :], sem="DS")
        P.dma("sync", rbcol[:], rb_ap.rearrange("(e o) -> e o", o=1), sem="DS")
        P.wait("vector", "DS")
        rec = P.op("vector", lambda v: v.tensor_copy(out=rwb[:], in_=rw[:]))
        vr = P.inc(rec, "V", 1)
        P.wait("tensor", "V", vr)
        for c in range(DC):
            def f(t, c=c):
                return t.transpose(out=pp[c % 2][:, 0:4].bitcast(BF16),
                                   in_=rwb[0:8, c * 128:(c + 1) * 128],
                                   identity=C["idn_bf"][0:8, 0:8])
            rec = P.op("tensor", f)
            tv = P.inc(rec, "T", 1)
            P.wait("vector", "T", tv)
            def g(v, c=c):
                return v.tensor_copy(out=rTb[:, c * 8:(c + 1) * 8],
                                     in_=pp[c % 2][:, 0:4].bitcast(BF16))
            rec = P.op("vector", g)
            vv = P.inc(rec, "V", 1)
            P.wait("tensor", "V", vv)
        for tt in range(NSL):
            t0 = tt * SL
            for c in range(DC):
                def f(t, c=c, t0=t0):
                    return t.matmul(out=pp[2][0:8, :], lhsT=rTb[:, c * 8:(c + 1) * 8],
                                    rhs=XN[:, c * T + t0: c * T + t0 + SL],
                                    start=(c == 0), stop=(c == DC - 1))
                rec = P.op("tensor", f)
            tl = P.inc(rec, "T", 1)
            P.wait("vector", "T", tl)
            def f2(v, t0=t0):
                return v.tensor_scalar(out=logitsT[:, t0:t0 + SL], in0=pp[2][0:8, :],
                                       scalar1=rbcol[:], scalar2=None, op0=ALU.add)
            rec = P.op("vector", f2)
            sl_ = P.inc(rec, "V", 1)
            P.wait("tensor", "V", sl_)
        if dbg is not None:
            P.wait("sync", "V")
            P.dma("sync", dbg["logitsT"], logitsT[:], sem="DO")
        for tc in range(16):
            def f(t, tc=tc):
                return t.transpose(out=pp[3][:, 0:8],
                                   in_=logitsT[0:8, tc * 128:(tc + 1) * 128],
                                   identity=C["idn_f32"][0:8, 0:8])
            rec = P.op("tensor", f)
            tv = P.inc(rec, "T", 1)
            P.wait("vector", "T", tv)
            L8 = pp[3][:, 0:8]
            P.op("vector", lambda v, L8=L8: v.reduce_max(out=m1c[:], in_=L8, axis=AX.X))
            P.selfsync("vector")
            P.op("vector", lambda v, L8=L8: v.tensor_sub(
                out=sc8[:], in0=L8, in1=m1c[:].broadcast_to([128, 8])))
            P.selfsync("vector")
            P.op("vector", lambda v: v.tensor_scalar(
                out=G8[:], in0=sc8[:], scalar1=0.0, scalar2=None, op0=ALU.is_ge))
            P.selfsync("vector")
            rec = P.op("vector", lambda v, L8=L8: v.scalar_tensor_tensor(
                out=E8[:], in0=G8[:], scalar=-1e30, in1=L8,
                op0=ALU.mult, op1=ALU.add))
            P.selfsync("vector")
            P.op("vector", lambda v: v.reduce_max(out=m2c[:], in_=E8[:], axis=AX.X))
            P.selfsync("vector")
            rec = P.op("vector", lambda v: v.tensor_sub(
                out=nm1c[:], in0=m2c[:], in1=m1c[:]))
            vt = P.inc(rec, "V", 1)
            P.wait("scalar", "V", vt)
            P.op("scalar", lambda s: s.activation(out=E8[:], in_=sc8[:], func=AF.Exp))
            P.selfsync("scalar")
            rec = P.op("scalar", lambda s: s.activation(
                out=ed1[:], in_=nm1c[:], func=AF.Exp))
            st_ = P.inc(rec, "S", 1)
            P.wait("vector", "S", st_)
            P.op("vector", lambda v: v.tensor_scalar_add(out=ed1[:], in0=ed1[:], scalar1=1.0))
            P.selfsync("vector")
            P.op("vector", lambda v: v.reciprocal(out=r1[:], in_=ed1[:]))
            P.selfsync("vector")
            P.op("vector", lambda v, L8=L8: v.tensor_tensor(
                out=G8[:], in0=L8, in1=m2c[:].broadcast_to([128, 8]), op=ALU.is_ge))
            P.selfsync("vector")
            P.op("vector", lambda v: v.tensor_mul(out=E8[:], in0=E8[:], in1=G8[:]))
            P.selfsync("vector")
            rec = P.op("vector", lambda v, tc=tc: v.tensor_mul(
                out=W8[:, tc * 8:(tc + 1) * 8], in0=E8[:],
                in1=r1[:].broadcast_to([128, 8])))
            vt2 = P.inc(rec, "V", 1)
            P.wait("tensor", "V", vt2)
            def f3(t, tc=tc):
                return t.transpose(out=pp[4][0:8, 0:128],
                                   in_=W8[:, tc * 8:(tc + 1) * 8],
                                   identity=C["idn_f32"][:])
            rec = P.op("tensor", f3)
            tw = P.inc(rec, "T", 1)
            P.wait("vector", "T", tw)
            def f4(v, tc=tc):
                return v.tensor_copy(out=weT[0:8, tc * 128:(tc + 1) * 128],
                                     in_=pp[4][0:8, 0:128])
            rec = P.op("vector", f4)
            vt3 = P.inc(rec, "V", 1)
            P.wait("tensor", "V", vt3)
        P.barrier()

    Wb = P.sbuf("b_Wb", [128, T], F32)
    b1s = P.sbuf("b_b1s", [64, 128], F32)
    b2s = P.sbuf("b_b2s", [8, 128], F32)
    b1col = P.sbuf("b_b1col", [128, 64], F32)
    b2col = P.sbuf("b_b2col", [128, 8], F32)
    w1r = [P.sbuf(f"b_w1r{i}", [128, 2 * 1024], BF16) for i in range(2)]
    w1t = [P.sbuf(f"b_w1t{i}", [128, 2 * 1024], BF16) for i in range(2)]
    w2t = [P.sbuf(f"b_w2t{i}", [128, 1024], BF16) for i in range(3)]
    sg = P.sbuf("b_sg", [128, 2 * SL], F32)
    werow = sg[0:1, 0:SL]
    hg = P.sbuf("b_hg", [128, NPAIR * 1024], BF16)
    hgtmp = P.sbuf("b_hgtmp", [128, 1024], BF16)

    pA, pB, ptr = pp[0], pp[1], pp[2]
    for s in ("DW1", "DC2", "DT2", "DT22", "DHG", "DHGR"):
        P.sem(s)

    nw1 = [0]
    ttr = {}
    mm1evac = {}

    w1p_done = {}
    def dma_w1pair(e_, c_):
        par = nw1[0] % 2
        for (k, fc) in ((0, c_), (1, c_ + 32)):
            inap = w1_ap[e_, fc * 128:(fc + 1) * 128, :]
            v = P.dma("gpsimd", w1r[par][:, k * 1024:(k + 1) * 1024], inap,
                      sem="DWP" + str(par))
        w1p_done[nw1[0]] = ("DWP" + str(par), v)
        nw1[0] += 1

    def emit_w1pair_transpose(ldidx):
        par = ldidx % 2
        P.wait("tensor", w1p_done[ldidx][0], w1p_done[ldidx][1])
        for k in range(2):
            for j in range(8):
                def f(t, par=par, k=k, j=j):
                    return t.transpose(out=ptr[:, j * 64:(j + 1) * 64].bitcast(BF16),
                                       in_=w1r[par][:, k * 1024 + j * 128: k * 1024 + (j + 1) * 128],
                                       identity=C["idn_bf"][:])
                rec = P.op("tensor", f)
            tv = P.inc(rec, "T", 1)
            P.wait("vector", "T", tv)
            def g(v, par=par, k=k):
                return v.tensor_copy(out=w1t[par][:, k * 1024:(k + 1) * 1024],
                                     in_=ptr[:].bitcast(BF16))
            rec = P.op("vector", g)
            vv = P.inc(rec, "V", 1)
            P.wait("tensor", "V", vv)
        ttr[ldidx] = P.counts["T"]

    for e in range(NE):
        if getattr(P, "_phase", None) is not None:
            P.set_phase(f"b_e{e}")
        P.wait("gpsimd", "DT20")
        P.wait("gpsimd", "DT21")
        P.wait("gpsimd", "DT22")
        P.dma("gpsimd", w2scr, w2_ap[e, :, :], sem="DC2")
        P.dma("sync", b1s[:], b1_ap[e, :].rearrange("(fc j) -> fc j", j=128), sem="DS")
        P.dma("sync", b2s[:], b2_ap[e, :].rearrange("(dc j) -> dc j", j=128), sem="DS")
        P.wait("tensor", "DS")
        P.op("tensor", lambda t: t.transpose(out=ptr[:, 0:64], in_=b1s[:],
                                             identity=C["idn_f32"][0:64, 0:64]))
        P.op("tensor", lambda t: t.transpose(out=ptr[:, 64:72], in_=b2s[:],
                                             identity=C["idn_f32"][0:8, 0:8]))
        for tt in range(NSL):
            dg = P.dma("gpsimd", werow[:], weT[e:e + 1, tt * SL:(tt + 1) * SL], sem="DG")
            P.wait("tensor", "DG", dg)
            def fwb(t, tt=tt):
                return t.matmul(out=pA[:], lhsT=C["ones1_f32"][:],
                                rhs=werow[:],
                                start=True, stop=True)
            rec2 = P.op("tensor", fwb)
            tb = P.inc(rec2, "T", 1)
            P.wait("vector", "T", tb)
            def gwb(v, tt=tt):
                return v.tensor_copy(out=Wb[:, tt * SL:(tt + 1) * SL], in_=pA[:])
            rec2 = P.op("vector", gwb)
            vwb = P.inc(rec2, "V", 1)
            P.wait("gpsimd", "V", vwb)
        P.op("vector", lambda v: v.tensor_copy(out=b1col[:], in_=ptr[:, 0:64]))
        rec2 = P.op("vector", lambda v: v.tensor_copy(out=b2col[:], in_=ptr[:, 64:72]))
        vbias = P.inc(rec2, "V", 1)
        P.wait("tensor", "V", vbias)

        if e == 0:
            dma_w1pair(0, 0)
            emit_w1pair_transpose(0)
        for c in range(NPAIR):
            ldidx = e * NPAIR + c
            nxt = (e, c + 1) if c + 1 < NPAIR else (e + 1, 0)
            if nxt[0] < NE:
                if ldidx >= 1:
                    P.wait("gpsimd", "T", ttr[ldidx - 1])
                dma_w1pair(nxt[0], nxt[1])
            par = ldidx % 2
            for p2 in range(2):
                bk = pp[0:4] if p2 == 0 else pp[4:8]
                if ldidx >= 1:
                    P.wait("tensor", "V", mm1evac[(ldidx - 1, p2)])
                for dc in range(DC):
                    for k in range(2):
                        for tl in range(2):
                            tt = 2 * p2 + tl
                            def f(t, k=k, dc=dc, tl=tl, tt=tt, par=par, bk=bk):
                                return t.matmul(
                                    out=bk[2 * k + tl][:],
                                    lhsT=w1t[par][:, k * 1024 + dc * 128: k * 1024 + (dc + 1) * 128],
                                    rhs=XN[:, dc * T + tt * SL: dc * T + (tt + 1) * SL],
                                    start=(dc == 0), stop=(dc == DC - 1))
                            rec = P.op("tensor", f)
                tmm = P.inc(rec, "T", 1)
                P.wait("scalar", "T", tmm)
                for tl in range(2):
                    def fsilu(s, c=c, tl=tl, bk=bk):
                        return s.activation(out=sg[:, tl * SL:(tl + 1) * SL],
                                            in_=bk[tl][:], func=AF.Silu,
                                            bias=b1col[:, c:c + 1])
                    rec = P.op("scalar", fsilu)
                ssi = P.inc(rec, "S", 1)
                P.wait("vector", "S", ssi)
                for tl in range(2):
                    tt = 2 * p2 + tl
                    if tt < 2:
                        dst = hg[:, c * 1024 + tt * SL: c * 1024 + (tt + 1) * SL]
                    else:
                        if tt == 2:
                            P.wait("vector", "DHG", 16 * (e * NPAIR + c))
                        dst = hgtmp[:, (tt - 2) * SL: (tt - 1) * SL]
                    def fgate(v, c=c, dst=dst, tl=tl, bk=bk):
                        return v.scalar_tensor_tensor(
                            out=dst, in0=bk[2 + tl][:],
                            scalar=b1col[:, 32 + c:33 + c],
                            in1=sg[:, tl * SL:(tl + 1) * SL],
                            op0=ALU.add, op1=ALU.mult)
                    rec = P.op("vector", fgate)
                mm1evac[(ldidx, p2)] = P.inc(rec, "V", 1)
            P.wait("gpsimd", "V")
            P.dma("gpsimd", hgscr[:, c * 1024:(c + 1) * 1024], hgtmp[:], sem="DHG")
            if nxt[0] < NE:
                P.wait("tensor", "V", mm1evac[(ldidx, 0)])
                emit_w1pair_transpose(ldidx + 1)

        P.wait("tensor", "V")
        P.wait("sync", "DC2")
        tmj = {}
        for hi in range(2):
            if hi == 1:
                P.wait("gpsimd", "T")
                P.wait("gpsimd", "DHG", 16 * NPAIR * (e + 1))
                P.dma("gpsimd", hg[:], hgscr[:, :], sem="DHGR")
                P.wait("tensor", "DHGR")
            for tt in range(2):
                for c in range(NPAIR):
                    j = (hi * 2 + tt) * NPAIR + c
                    if j >= 3:
                        P.wait("sync", "T", tmj[j - 3])
                    jd = P.dma("sync", w2t[j % 3][:], w2scr[:, c * 128:(c + 1) * 128],
                               sem="DT2" + str(j % 3), transpose=True)
                    P.wait("tensor", "DT2" + str(j % 3), jd)
                    for dout in range(DC):
                        def f(t, j=j, dout=dout, c=c, tt=tt):
                            return t.matmul(
                                out=pp[dout][:],
                                lhsT=w2t[j % 3][:, dout * 128:(dout + 1) * 128],
                                rhs=hg[:, c * 1024 + tt * SL: c * 1024 + (tt + 1) * SL],
                                start=(c == 0), stop=(c == NPAIR - 1))
                        rec = P.op("tensor", f)
                    tmj[j] = P.inc(rec, "T", 1)
                P.wait("vector", "T")
                t0 = (hi * 2 + tt) * SL
                for dout in range(DC):
                    def fo(v, dout=dout, t0=t0):
                        return v.scalar_tensor_tensor(
                            out=sg[:, 0:SL], in0=pp[dout][:], scalar=b2col[:, dout:dout + 1],
                            in1=Wb[:, t0:t0 + SL], op0=ALU.add, op1=ALU.mult)
                    P.op("vector", fo)
                    def fr(v, dout=dout, t0=t0):
                        return v.tensor_add(out=XT[:, dout * T + t0: dout * T + t0 + SL],
                                            in0=sg[:, 0:SL],
                                            in1=XT[:, dout * T + t0: dout * T + t0 + SL])
                    rec = P.op("vector", fr)
                vo = P.inc(rec, "V", 1)
                P.wait("tensor", "V", vo)
        P.barrier(("tensor", "vector", "scalar", "gpsimd"))
    P.barrier()
    if dbg is not None:
        P.dma("sync", dbg["W8"], W8[:], sem="DO")
        P.dma("sync", dbg["weT"], weT[:], sem="DO")
        P.dma("sync", dbg["hg"], hg[:], sem="DO")
        P.dma("sync", dbg["Wb"], Wb[:], sem="DO")



def emit_block_c(P, XT, XN, C, wav_ap, wv_ap, wvb_ap, wo_ap, wob_ap, pp, dbg=None):
    H = 16
    HD = 64

    woT = P.sbuf("c_woT", [128, DC * 1024], BF16)
    wavb = P.sbuf("c_wavb", [128, H * 64], BF16)
    bias_col = P.sbuf("c_bias", [128, 8], F32)
    vx65 = P.sbuf("c_vx65", [128, 16 * (H * 65)], BF16)
    o_bf = P.sbuf("c_obf", [128, DC * T], BF16)
    nrmmax = P.sbuf("c_nrmmax", [1, 16], F32)
    sml = P.sbuf("c_sml", [1, 16], F32)
    sqs = P.sbuf("c_sqs", [128, SL], BF16)

    with contextlib.ExitStack() as ws:
        sb = lambda n, sh, d=F32: ws.enter_context(P.nc.sbuf_tensor(n, sh, d))
        wvT = sb("c_wvT", [128, DC * 1024], BF16)
        wstage = sb("c_wstage", [128, 1024], BF16)
        vbs = sb("c_vbs", [8, 128], F32)
        obs = sb("c_obs", [8, 128], F32)
        for (w_ap, dstT) in ((wv_ap, wvT), (wo_ap, woT)):
            for c in range(DC):
                P.wait("gpsimd", "T")
                dw = P.dma("gpsimd", wstage[:], w_ap[c * 128:(c + 1) * 128, :], sem="DG")
                P.wait("tensor", "DG", dw)
                for j in range(8):
                    def f(t, j=j):
                        return t.transpose(out=pp[0][:, j * 64:(j + 1) * 64].bitcast(BF16),
                                           in_=wstage[:, j * 128:(j + 1) * 128],
                                           identity=C["idn_bf"][:])
                    rec = P.op("tensor", f)
                tv = P.inc(rec, "T", 1)
                P.wait("vector", "T", tv)
                def g(v, c=c, dstT=dstT):
                    return v.tensor_copy(
                        out=dstT[:].rearrange("p (dc e) -> p dc e", dc=DC)[:, :, c * 128:(c + 1) * 128],
                        in_=pp[0][:].bitcast(BF16).rearrange("p (dc e) -> p dc e", dc=DC))
                rec = P.op("vector", g)
                vv = P.inc(rec, "V", 1)
                P.wait("tensor", "V", vv)
        wavs = sb("c_wavs", [128, H * 64], F32)
        P.dma("sync", wavs[0:64, :].rearrange("d (h e) -> d h e", h=H),
              wav_ap.rearrange("h d e -> d h e"), sem="DS")
        dw = P.dma("sync", wavs[64:128, :].rearrange("d (h e) -> d h e", h=H),
                   wav_ap.rearrange("h d e -> d h e"), sem="DS")
        P.wait("vector", "DS", dw)
        rec = P.op("vector", lambda v: v.tensor_copy(out=wavb[:], in_=wavs[:]))
        P.inc(rec, "V", 1)
        P.dma("sync", vbs[:], wvb_ap.rearrange("(a j) -> a j", j=128), sem="DS")
        dw = P.dma("sync", obs[:], wob_ap.rearrange("(a j) -> a j", j=128), sem="DS")
        P.wait("tensor", "DS", dw)
        P.op("tensor", lambda t: t.transpose(out=pp[1][:, 0:8], in_=vbs[:],
                                             identity=C["idn_f32"][0:8, 0:8]))
        rec = P.op("tensor", lambda t: t.transpose(out=pp[1][:, 8:16], in_=obs[:],
                                                   identity=C["idn_f32"][0:8, 0:8]))
        tv = P.inc(rec, "T", 1)
        P.wait("vector", "T", tv)
        wvbc = sb("c_wvbc", [128, 8], BF16)
        wobc = sb("c_wobc", [128, 8], F32)
        P.op("vector", lambda v: v.tensor_copy(out=wvbc[:], in_=pp[1][:, 0:8]))
        rec = P.op("vector", lambda v: v.tensor_copy(out=wobc[:], in_=pp[1][:, 8:16]))
        vv = P.inc(rec, "V", 1)
        P.wait("tensor", "V", vv)
        for fc in range(DC):
            for ec in range(DC):
                def f(t, fc=fc, ec=ec):
                    return t.matmul(out=pp[2][:, 0:1],
                                    lhsT=woT[:, ec * 1024 + fc * 128:(ec * 1024) + (fc + 1) * 128],
                                    rhs=wvbc[:, ec:ec + 1],
                                    start=(ec == 0), stop=(ec == DC - 1))
                rec = P.op("tensor", f)
            tb_ = P.inc(rec, "T", 1)
            P.wait("vector", "T", tb_)
            def g(v, fc=fc):
                return v.tensor_add(out=bias_col[:, fc:fc + 1], in0=pp[2][:, 0:1],
                                    in1=wobc[:, fc:fc + 1])
            rec = P.op("vector", g)
            vv = P.inc(rec, "V", 1)
            P.wait("tensor", "V", vv)

        v65 = vx65[:].rearrange("p (sc h e) -> p sc h e", sc=16, h=H)
        rec = P.op("gpsimd", lambda g: g.memset(v65[:, :, :, 64:65], 1.0))
        P.inc(rec, "G", 1)
        for sc_ in range(16):
            for et in range(2):
                for dc in range(DC):
                    def f(t, sc_=sc_, et=et, dc=dc):
                        return t.matmul(out=pp[4][:],
                                        lhsT=XN[:, dc * T + sc_ * 128: dc * T + (sc_ + 1) * 128],
                                        rhs=wvT[:, dc * 1024 + et * 512: dc * 1024 + (et + 1) * 512],
                                        start=(dc == 0), stop=(dc == DC - 1))
                    rec = P.op("tensor", f)
                tv = P.inc(rec, "T", 1)
                P.wait("scalar", "T", tv)
                def g(s, sc_=sc_, et=et):
                    return s.activation(
                        out=v65[:, sc_, et * 8:(et + 1) * 8, 0:64],
                        in_=pp[4][:].rearrange("p (h e) -> p h e", e=64), func=AF.Copy)
                rec = P.op("scalar", g)
                sv_ = P.inc(rec, "S", 1)
                P.wait("tensor", "S", sv_)
        rec = P.op("vector", lambda v: v.memset(nrmmax[:], 0.0))
        P.inc(rec, "V", 1)
        for dc in range(DC):
            for tt in range(NSL):
                t0 = tt * SL
                def fsq(s, dc=dc, t0=t0):
                    return s.activation(out=sqs[:], in_=XN[:, dc * T + t0: dc * T + t0 + SL],
                                        func=AF.Square)
                rec = P.op("scalar", fsq)
                sv_ = P.inc(rec, "S", 1)
                P.wait("tensor", "S", sv_)
                def f(t):
                    return t.matmul(out=pp[5][0:1, :], lhsT=C["onescol_bf"][:], rhs=sqs[:],
                                    start=True, stop=True)
                rec = P.op("tensor", f)
                tv = P.inc(rec, "T", 1)
                P.wait("vector", "T", tv)
                P.op("vector", lambda v: v.reduce_max(out=sml[:, 8:9], in_=pp[5][0:1, :],
                                                      axis=AX.X))
                P.selfsync("vector")
                def g(v, dc=dc):
                    return v.tensor_max(out=nrmmax[:, dc:dc + 1], in0=nrmmax[:, dc:dc + 1],
                                        in1=sml[0:1, 8:9])
                rec = P.op("vector", g)
                P.selfsync("vector")
                vv = P.inc(rec, "V", 1)
                P.wait("tensor", "V", vv)
        P.barrier()
        if dbg is not None:
            P.dma("sync", dbg["vx65e"], vx65[:], sem="DO")
            P.wait("sync", "DO")

    P.barrier()

    dbgsml = P.sbuf("c_dbgsml", [1, 16 * 16], F32) if dbg is not None else None
    otmp = P.sbuf("c_otmp", [64, SL], BF16)
    xwT = P.sbuf("c_xwT", [128, T], BF16)
    ptmp = [P.sbuf(f"c_pt{i}", [128, SL], BF16) for i in range(4)]
    rbsb = P.sbuf("c_rbsb", [64, SL], BF16)
    rrow = P.sbuf("c_rrow", [1, SL], F32)
    chcol = P.sbuf("c_chcol", [128, 1], F32)

    for h in range(H):
        hc = h // 2
        ho = (h % 2) * 64
        rec = P.op("vector", lambda v: v.memset(sml[:, 0:1], 0.0))
        P.inc(rec, "V", 1)
        for tt in range(NSL):
            t0 = tt * SL
            def f(t, h=h, t0=t0, hc=hc, ho=ho):
                return t.matmul(out=pp[4][ho:ho + 64, :],
                                lhsT=wavb[ho:ho + 64, h * 64:(h + 1) * 64],
                                rhs=XN[ho:ho + 64, hc * T + t0: hc * T + t0 + SL],
                                start=True, stop=True)
            rec = P.op("tensor", f)
            tv = P.inc(rec, "T", 1)
            P.wait("scalar", "T", tv)
            def g(s, t0=t0, ho=ho):
                return s.activation(out=xwT[ho:ho + 64, t0:t0 + SL],
                                    in_=pp[4][ho:ho + 64, :], func=AF.Copy)
            rec = P.op("scalar", g)
            def g2(s, t0=t0, ho=ho):
                return s.activation(out=sqs[ho:ho + 64, :], in_=pp[4][ho:ho + 64, :],
                                    func=AF.Square)
            rec = P.op("scalar", g2)
            sv_ = P.inc(rec, "S", 1)
            P.wait("tensor", "S", sv_)
            def f2(t, ho=ho):
                return t.matmul(out=pp[5][0:1, :], lhsT=C["onescol_bf"][ho:ho + 64, :],
                                rhs=sqs[ho:ho + 64, 0:512],
                                start=True, stop=True)
            rec = P.op("tensor", f2)
            tv = P.inc(rec, "T", 1)
            P.wait("vector", "T", tv)
            P.op("vector", lambda v: v.reduce_max(out=sml[:, 9:10], in_=pp[5][0:1, :],
                                                  axis=AX.X))
            P.selfsync("vector")
            rec = P.op("vector", lambda v: v.tensor_max(
                out=sml[:, 0:1], in0=sml[:, 0:1], in1=sml[0:1, 9:10]))
            P.selfsync("vector")
            vv = P.inc(rec, "V", 1)
            P.wait("tensor", "V", vv)
        if dbg is not None:
            rec = P.op("vector", lambda v, h=h: v.tensor_copy(
                out=dbgsml[0:1, h * 16:(h + 1) * 16], in_=sml[:]))
            P.inc(rec, "V", 1)
        rec = P.op("vector", lambda v, hc=hc: v.tensor_copy(
            out=sml[:, 1:2], in_=nrmmax[0:1, hc:hc + 1]))
        vv = P.inc(rec, "V", 1)
        P.selfsync("vector")
        P.wait("scalar", "V", vv)
        rec = P.op("scalar", lambda s: s.activation(out=sml[:, 2:3], in_=sml[:, 0:1],
                                                    func=AF.Sqrt))
        P.selfsync("scalar")
        rec = P.op("scalar", lambda s: s.activation(out=sml[:, 3:4], in_=sml[:, 1:2],
                                                    func=AF.Sqrt))
        sv_ = P.inc(rec, "S", 1)
        P.selfsync("scalar")
        P.wait("vector", "S", sv_)
        rec = P.op("vector", lambda v: v.tensor_scalar(
            out=sml[:, 4:5], in0=sml[:, 2:3], scalar1=None, scalar2=None,
            op0=ALU.mult) if False else v.tensor_mul(out=sml[:, 4:5], in0=sml[:, 2:3],
                                                     in1=sml[:, 3:4]))
        P.selfsync("vector")
        rec = P.op("vector", lambda v: v.tensor_scalar(
            out=sml[:, 5:6], in0=sml[:, 4:5], scalar1=-0.125, scalar2=None,
            op0=ALU.mult))
        vv = P.inc(rec, "V", 1)
        P.selfsync("vector")
        P.wait("tensor", "V", vv)
        rec = P.op("tensor", lambda t: t.matmul(out=pp[5][:, 0:1], lhsT=C["ones1_f32"][:],
                                                rhs=sml[0:1, 5:6], start=True, stop=True))
        tv = P.inc(rec, "T", 1)
        P.wait("vector", "T", tv)
        rec = P.op("vector", lambda v: v.tensor_copy(out=chcol[:], in_=pp[5][:, 0:1]))
        vv = P.inc(rec, "V", 1)
        P.wait("tensor", "V", vv)
        P.wait("scalar", "V", vv)
        sl4 = (pp[0], pp[1], pp[4], pp[5])
        tOlast = None
        for tb in range(NSL):
            t0 = tb * SL
            tO = {}
            sexp = {}

            def emit_o(j, h=h):
                P.wait("tensor", "S", sexp[j])
                def f2(t, h=h, j=j):
                    return t.matmul(out=pp[2][0:65, :],
                                    lhsT=vx65[:, j * (H * 65) + h * 65: j * (H * 65) + (h + 1) * 65],
                                    rhs=ptmp[j % 4][:],
                                    start=(j == 0), stop=(j == 15))
                rec2 = P.op("tensor", f2)
                tO[j] = P.inc(rec2, "T", 1)

            for sc_ in range(16):
                slot = sc_ % 4
                if sc_ >= 4:
                    P.wait("tensor", "S", sexp[sc_ - 4])
                def f(t, sc_=sc_, t0=t0, slot=slot, hc=hc, ho=ho):
                    return t.matmul(out=sl4[slot][:],
                                    lhsT=XN[ho:ho + 64, hc * T + sc_ * 128: hc * T + (sc_ + 1) * 128],
                                    rhs=xwT[ho:ho + 64, t0:t0 + SL],
                                    start=True, stop=True)
                rec = P.op("tensor", f)
                tv = P.inc(rec, "T", 1)
                P.wait("scalar", "T", tv)
                if sc_ >= 4:
                    P.wait("scalar", "T", tO[sc_ - 4])
                elif tOlast is not None:
                    P.wait("scalar", "T", tOlast[12 + sc_])
                def g(s, slot=slot, sc_=sc_):
                    return s.activation(out=ptmp[slot][:], in_=sl4[slot][:], func=AF.Exp,
                                        scale=0.125, bias=chcol[:])
                rec = P.op("scalar", g)
                sexp[sc_] = P.inc(rec, "S", 1)
                if sc_ >= 1:
                    emit_o(sc_ - 1)
            emit_o(15)
            tOlast = tO
            tv = tO[15]
            P.wait("vector", "T", tv)
            rec = P.op("vector", lambda v: v.reciprocal(out=rrow[:], in_=pp[2][64:65, :]))
            vv = P.inc(rec, "V", 1)
            P.selfsync("vector")
            P.wait("tensor", "V", vv)
            rec = P.op("tensor", lambda t: t.matmul(out=pp[3][0:64, :],
                                                    lhsT=C["ones1_f32"][:, 0:64],
                                                    rhs=rrow[:], start=True, stop=True))
            tv = P.inc(rec, "T", 1)
            P.wait("vector", "T", tv)
            P.op("vector", lambda v: v.tensor_copy(out=rbsb[:], in_=pp[3][0:64, :]))
            P.selfsync("vector")
            if ho == 0:
                def g4(v, hc=hc, t0=t0):
                    return v.tensor_mul(out=o_bf[0:64, hc * T + t0: hc * T + t0 + SL],
                                        in0=pp[2][0:64, :], in1=rbsb[:])
                rec = P.op("vector", g4)
                vv = P.inc(rec, "V", 1)
                P.wait("tensor", "V", vv)
            else:
                P.wait("vector", "DOB")
                def g4(v):
                    return v.tensor_mul(out=otmp[:], in0=pp[2][0:64, :], in1=rbsb[:])
                rec = P.op("vector", g4)
                vv = P.inc(rec, "V", 1)
                P.wait("gpsimd", "V", vv)
                def dmo(hc=hc, t0=t0):
                    return P.dma("gpsimd", o_bf[64:128, hc * T + t0: hc * T + t0 + SL],
                                 otmp[:], sem="DOB")
                dmo()
                P.wait("tensor", "V", vv)
    P.barrier(("tensor", "vector", "scalar"))

    if dbg is not None:
        P.barrier()
        P.wait("sync", "DOB")
        P.dma("sync", dbg["woT"], woT[:], sem="DO")
        P.dma("sync", dbg["obf"], o_bf[:], sem="DO")
        P.dma("sync", dbg["xwT"], xwT[:], sem="DO")
        P.dma("sync", dbg["vx65"], vx65[:], sem="DO")
        P.dma("sync", dbg["sml"], dbgsml[:], sem="DO")
    P.wait("tensor", "DOB")
    opev = {}
    for it, (fc, tt) in enumerate([(f_, t_) for f_ in range(DC) for t_ in range(NSL)]):
        t0 = tt * SL
        bank = pp[6] if it % 2 == 0 else pp[7]
        if it >= 2:
            P.wait("tensor", "V", opev[it - 2])
        for ec in range(DC):
            def f(t, fc=fc, ec=ec, t0=t0, bank=bank):
                return t.matmul(out=bank[:],
                                lhsT=woT[:, ec * 1024 + fc * 128: ec * 1024 + (fc + 1) * 128],
                                rhs=o_bf[:, ec * T + t0: ec * T + t0 + SL],
                                start=(ec == 0), stop=(ec == DC - 1))
            rec = P.op("tensor", f)
        tv = P.inc(rec, "T", 1)
        P.wait("vector", "T", tv)
        def g(v, fc=fc, t0=t0, bank=bank):
            return v.scalar_tensor_tensor(
                out=XT[:, fc * T + t0: fc * T + t0 + SL], in0=bank[:],
                scalar=bias_col[:, fc:fc + 1],
                in1=XT[:, fc * T + t0: fc * T + t0 + SL],
                op0=ALU.add, op1=ALU.add)
        rec = P.op("vector", g)
        opev[it] = P.inc(rec, "V", 1)
    P.wait("tensor", "V", opev[31])
    P.barrier()



TW = T + 1
DBG_LAYER = 0

def emit_ln_to(P, XT, C, pb1, pb2, pst1, pst2, xsq, ta, rows, outT, out_colw, out_off,
               out_dtypecast=False, zb=None):
    ps1 = pst1[0:1, :]
    ps2 = pst2[0:1, :]
    negm, rrow, mrrow, msq, var = rows["negm"], rows["r"], rows["mr"], rows["msq"], rows["var"]
    xnb = xsq
    for i in range(NSL):
        t0 = i * SL
        for c in range(DC):
            def f(v, c=c, t0=t0):
                return v.tensor_copy(out=xsq[c][:], in_=XT[:, c * T + t0: c * T + t0 + SL])
            rec = P.op("vector", f)
        vcast = P.inc(rec, "V", 1)
        P.wait("tensor", "V", vcast)
        for c in range(DC):
            def f(t, c=c):
                return t.matmul(out=ps1, lhsT=C["onescol_bf"][:], rhs=xsq[c][:],
                                start=(c == 0), stop=(c == DC - 1))
            rec = P.op("tensor", f)
        ts1 = P.inc(rec, "T", 1)
        P.wait("scalar", "V", vcast)
        P.wait("scalar", "T", ts1)
        for c in range(DC):
            def f(s, c=c):
                return s.activation(out=xsq[c][:], in_=xsq[c][:], func=AF.Square)
            rec = P.op("scalar", f)
            sv = P.inc(rec, "S", 1)
            P.wait("tensor", "S", sv)
            def f2(t, c=c):
                return t.matmul(out=ps2, lhsT=C["onescol_bf"][:], rhs=xsq[c][:],
                                start=(c == 0), stop=(c == DC - 1))
            rec2 = P.op("tensor", f2)
        ts2 = P.inc(rec2, "T", 1)
        P.wait("scalar", "T", ts1)
        rec = P.op("scalar", lambda s: s.activation(
            out=negm[:], in_=ps1, func=AF.Copy, scale=-1.0 / D))
        P.inc(rec, "S", 1)
        P.selfsync("scalar")
        rec = P.op("scalar", lambda s: s.activation(
            out=msq[:], in_=negm[:], func=AF.Square))
        smsq = P.inc(rec, "S", 1)
        P.wait("vector", "T", ts2)
        P.wait("vector", "S", smsq)
        rec = P.op("vector", lambda v: v.scalar_tensor_tensor(
            out=var[:], in0=ps2, scalar=1.0 / D,
            in1=msq[:], op0=ALU.mult, op1=ALU.subtract))
        vvar = P.inc(rec, "V", 1)
        P.wait("scalar", "V", vvar)
        rec = P.op("scalar", lambda s: s.activation(
            out=var[:], in_=var[:], func=AF.Sqrt, bias=C["epscol"][:]))
        sstd = P.inc(rec, "S", 1)
        P.wait("vector", "S", sstd)
        P.op("vector", lambda v: v.reciprocal(out=rrow[:], in_=var[:]))
        P.selfsync("vector")
        rec = P.op("vector", lambda v: v.tensor_mul(
            out=mrrow[:], in0=negm[:], in1=rrow[:]))
        vfin = P.inc(rec, "V", 1)
        P.wait("tensor", "V", vfin)
        P.op("tensor", lambda t: t.matmul(
            out=pb1[:], lhsT=C["ones1_f32"][:], rhs=rrow[:], start=True, stop=True))
        rec = P.op("tensor", lambda t: t.matmul(
            out=pb2[:], lhsT=C["ones1_f32"][:], rhs=mrrow[:], start=True, stop=True))
        tb = P.inc(rec, "T", 1)
        P.wait("vector", "T", tb)
        for c in range(DC):
            dst = outT[:, c * out_colw + out_off + t0: c * out_colw + out_off + t0 + SL]
            def f(v, c=c, t0=t0):
                return v.tensor_mul(out=ta[:], in0=XT[:, c * T + t0: c * T + t0 + SL],
                                    in1=pb1[:])
            P.op("vector", f)
            def f2(v, dst=dst):
                return v.tensor_add(out=dst, in0=ta[:], in1=pb2[:])
            rec = P.op("vector", f2)
            if zb is not None:
                zdst = zb[:, c * out_colw + out_off + t0: c * out_colw + out_off + t0 + SL]
                def f3(v, zdst=zdst):
                    return v.tensor_add(out=zdst, in0=ta[:], in1=pb2[:])
                rec = P.op("vector", f3)
        vap = P.inc(rec, "V", 1)
        P.wait("tensor", "V", vap)
    P.barrier()


def emit_block_d(P, XT_in_ap, out_ap, C, aps, pp, dbg=None):
    H8 = 8
    HID = 2730
    NHC = 22
    rag = HID - 21 * 128

    zT = P.sbuf("d_zT", [128, DC * TW], F32)
    zb = P.sbuf("d_zb", [128, DC * TW], BF16)
    sml = P.sbuf("d_sml", [1, 16], F32)
    Rp = P.sbuf("d_Rp", [128, 128], BF16)
    def emitR(g):
        g.memset(Rp[:], 1.0)
        return g.affine_select(out=Rp[:], in_=Rp[:],
                               pattern=[[-64, 2], [32, 2], [-1, 32]],
                               compare_op=ALU.is_equal, fill=0.0, base=-32,
                               channel_multiplier=1)
    rec = P.op("gpsimd", emitR)
    P.inc(rec, "G", 1)
    zv = zT[:].rearrange("p (c t) -> p c t", c=DC)
    zbv = zb[:].rearrange("p (c t) -> p c t", c=DC)
    P.op("gpsimd", lambda g: g.memset(zv[:, :, 0:1], 0.0))
    rec = P.op("gpsimd", lambda g: g.memset(zbv[:, :, 0:1], 0.0))
    P.inc(rec, "G", 1)

    with contextlib.ExitStack() as s0:
        sb = lambda n, sh, d=F32: s0.enter_context(P.nc.sbuf_tensor(n, sh, d))
        XT = sb("d_XT", [128, DC * T])
        P.dma("sync", XT[:], XT_in_ap[:, :], sem="DS")
        P.wait("vector", "DS")
        P.barrier()
        rows = {k: sb(f"d_lnrow_{k}", [1, SL]) for k in ("negm", "r", "mr", "msq", "var")}
        xsq = [sb(f"d_lnxsq{i}", [128, SL], BF16) for i in range(DC)]
        ta = sb("d_lnta", [128, SL])
        emit_ln_to(P, XT, C, pp[0], pp[1], pp[2], pp[3], xsq, ta, rows,
                   zT, TW, 1, zb=zb)

    if dbg is not None:
        P.barrier()
        P.dma("sync", dbg["z0"], zT[:], sem="DO")
    for l in range(2):
        emit_d_layer(P, C, aps, pp, zT, zb, sml, Rp, l, dbg=dbg)

    if dbg is not None:
        P.barrier()
        P.dma("sync", dbg["zfin"], zT[:], sem="DO")
        for e_ in ENGINES:
            P.wait(e_, "DO")
    with contextlib.ExitStack() as s3:
        sb = lambda n, sh, d=F32: s3.enter_context(P.nc.sbuf_tensor(n, sh, d))
        Xf = sb("d_xtok", [128, DC * T])
        outsb = sb("d_outsb", [128, D])
        P.dma("sync", Xf[:], XT_in_ap[:, :], sem="DS")
        P.wait("vector", "DS")
        P.barrier()
        sums = sb("d_sums", [128, SL], F32)
        sumb = sb("d_sumb", [128, SL], BF16)
        for tc in range(T // 128):
            for dc in range(DC):
                def fadd(v, tc=tc, dc=dc):
                    return v.tensor_add(
                        out=sums[:, 0:128],
                        in0=zT[:, dc * TW + 1 + tc * 128: dc * TW + 1 + (tc + 1) * 128],
                        in1=Xf[:, dc * T + tc * 128: dc * T + (tc + 1) * 128])
                rec = P.op("vector", fadd)
                va = P.inc(rec, "V", 1)
                P.wait("tensor", "V", va)
                def ftr(t, dc=dc):
                    return t.transpose(out=pp[4][:, 0:128], in_=sums[:, 0:128],
                                       identity=C["idn_f32"][:])
                rec = P.op("tensor", ftr)
                tv = P.inc(rec, "T", 1)
                P.wait("vector", "T", tv)
                def fev(v, dc=dc):
                    return v.tensor_copy(out=outsb[:, dc * 128:(dc + 1) * 128],
                                         in_=pp[4][:, 0:128])
                rec = P.op("vector", fev)
                ve = P.inc(rec, "V", 1)
                P.wait("tensor", "V", ve)
            P.wait("sync", "V", ve)
            do_ = P.dma("sync", out_ap[tc * 128:(tc + 1) * 128, :], outsb[:], sem="DO")
            if tc + 1 < T // 128:
                P.wait("vector", "DO", do_)
        P.wait("sync", "DO")


def emit_d_layer(P, C, aps, pp, zT, zb, sml, Rp, l, dbg=None):
    if getattr(P, "_phase", None) is not None:
        P.set_phase(f"d_attn{l}")
    if dbg is not None and l == DBG_LAYER and DBG_LAYER > 0:
        P.barrier()
        P.dma("sync", dbg["zb1"], zb[:], sem="DO")
        P.dma("sync", dbg["zb1f"], zT[:], sem="DO")
        for e_ in ENGINES:
            P.wait(e_, "DO")
    H8 = 8
    HID = 2730
    NHC = 22
    rag = HID - 21 * 128

    def zsl(c, t0, n=SL, shifted=False):
        off = 0 if shifted and c >= 4 else 1
        return zb[:, c * TW + off + t0: c * TW + off + t0 + n]

    attn_scope = contextlib.ExitStack()
    asb = lambda n, sh, d=F32: attn_scope.enter_context(P.nc.sbuf_tensor(n, sh, d))
    qf = asb(f"d_qf{l}", [128, 4 * T], BF16)
    kf = asb(f"d_kf{l}", [128, 4 * T], BF16)
    v65 = asb(f"d_v65{l}", [128, 16 * (H8 * 64)], BF16)
    o_bf = asb(f"d_obf{l}", [128, 4 * T], BF16)

    with contextlib.ExitStack() as s1:
        sb = lambda n, sh, d=F32: s1.enter_context(P.nc.sbuf_tensor(n, sh, d))
        wq = [sb(f"d_wq{l}_{i}", [128, DC * 128], BF16) for i in range(2)]
        tabs = [sb(f"d_tab{l}_{i}", [128, T], BF16) for i in range(4)]
        vtmp = sb(f"d_vtmp{l}", [128, SL], BF16)
        qs2 = sb(f"d_qs2{l}", [128, 2])
        ks2 = sb(f"d_ks2{l}", [128, 2])
        mask2 = sb(f"d_mask2{l}", [128, 2], BF16)
        qraw = sb(f"d_qraw{l}", [128, SL])
        qsq = sb(f"d_qsq{l}", [128, SL], BF16)
        rn2 = sb(f"d_rn2{l}", [2, SL])
        rn2b = sb(f"d_rn2b{l}", [2, SL], BF16)
        bc2 = sb(f"d_bc2{l}", [2, 128], BF16)
        qnf = sb(f"d_qnf{l}", [128, SL])
        qnb = sb(f"d_qnb{l}", [128, SL], BF16)
        t1 = sb(f"d_t1{l}", [128, SL])
        t2 = sb(f"d_t2{l}", [128, SL])

        def emitm2(g):
            g.memset(mask2[:], 1.0)
            g.affine_select(out=mask2[:], in_=mask2[:], pattern=[[-64, 2]],
                            compare_op=ALU.is_ge, fill=0.0, base=0,
                            channel_multiplier=1)
            return g.affine_select(out=mask2[:], in_=mask2[:], pattern=[[64, 2]],
                                   compare_op=ALU.is_ge, fill=0.0, base=63,
                                   channel_multiplier=-1)
        P.op("gpsimd", emitm2)
        def emitbc2(g):
            g.memset(bc2[:], 1.0)
            g.affine_select(out=bc2[:], in_=bc2[:], pattern=[[1, 128]],
                            compare_op=ALU.is_ge, fill=0.0, base=0,
                            channel_multiplier=-64)
            return g.affine_select(out=bc2[:], in_=bc2[:], pattern=[[-1, 128]],
                                   compare_op=ALU.is_ge, fill=0.0, base=63,
                                   channel_multiplier=64)
        rec = P.op("gpsimd", emitbc2)
        P.inc(rec, "G", 1)

        wq_done = {}
        def dma_wq(fc, par):
            wq_done[fc] = ("DGQ" + str(par), P.dma(
                "gpsimd", wq[par][:].rearrange("p (c f) -> p c f", c=DC),
                aps["wqkv"][l][:, fc * 128:(fc + 1) * 128].rearrange(
                    "(c p) f -> p c f", p=128), sem="DGQ" + str(par)))
        dma_wq(0, 0)
        P.dma("sync", qs2[0:64, 0:1], aps["qs"][l].rearrange("(a e) -> e a", a=1), sem="DS")
        P.dma("sync", qs2[64:128, 0:1], aps["qs"][l].rearrange("(a e) -> e a", a=1), sem="DS")
        P.dma("sync", ks2[0:64, 0:1], aps["ks"][l].rearrange("(a e) -> e a", a=1), sem="DS")
        dk = P.dma("sync", ks2[64:128, 0:1], aps["ks"][l].rearrange("(a e) -> e a", a=1), sem="DS")
        P.wait("vector", "DS", dk)
        for b0 in range(0, 128, 64):
            P.op("vector", lambda v, b0=b0: v.tensor_copy(
                out=qs2[b0:b0 + 32, 1:2], in_=qs2[b0 + 32:b0 + 64, 0:1]))
            P.op("vector", lambda v, b0=b0: v.tensor_copy(
                out=qs2[b0 + 32:b0 + 64, 1:2], in_=qs2[b0:b0 + 32, 0:1]))
            P.op("vector", lambda v, b0=b0: v.tensor_copy(
                out=ks2[b0:b0 + 32, 1:2], in_=ks2[b0 + 32:b0 + 64, 0:1]))
            P.op("vector", lambda v, b0=b0: v.tensor_copy(
                out=ks2[b0 + 32:b0 + 64, 1:2], in_=ks2[b0:b0 + 32, 0:1]))
        P.selfsync("vector")
        for (ti, nm, col, sc) in ((0, "csq", 0, qs2), (1, "ssq", 1, qs2),
                                  (2, "csk", 0, ks2), (3, "ssk", 1, ks2)):
            for sl4 in range(NSL):
                dt_ = P.dma("sync", qraw[:], aps[nm][:, sl4 * SL:(sl4 + 1) * SL], sem="DS")
                P.wait("vector", "DS", dt_)
                def f(v, ti=ti, col=col, sc=sc, sl4=sl4):
                    return v.tensor_scalar(out=tabs[ti][:, sl4 * SL:(sl4 + 1) * SL],
                                           in0=qraw[:],
                                           scalar1=sc[:, col:col + 1], scalar2=None,
                                           op0=ALU.mult)
                rec = P.op("vector", f)
                vt = P.inc(rec, "V", 1)
                P.wait("sync", "V", vt)
        P.wait("tensor", "V", vt)
        P.wait("tensor", "DG")
        P.wait("tensor", "G")

        tq_after = {}
        for fc in range(12):
            kind = "q" if fc < 4 else ("k" if fc < 8 else "v")
            hc = fc % 4
            if fc + 1 < 12:
                if fc >= 1:
                    P.wait("gpsimd", "T", tq_after[fc - 1])
                dma_wq(fc + 1, (fc + 1) % 2)
            P.wait("tensor", wq_done[fc][0], wq_done[fc][1])
            for tt in range(NSL):
                t0 = tt * SL
                for dc in range(DC):
                    def f(t, fc=fc, dc=dc, t0=t0):
                        return t.matmul(out=pp[0][:],
                                        lhsT=wq[fc % 2][:, dc * 128:(dc + 1) * 128],
                                        rhs=zsl(dc, t0, shifted=True),
                                        start=(dc == 0), stop=(dc == DC - 1))
                    rec = P.op("tensor", f)
                tm = P.inc(rec, "T", 1)
                if kind == "v":
                    P.wait("scalar", "T", tm)
                    def g(s):
                        return s.activation(out=vtmp[:], in_=pp[0][:], func=AF.Copy)
                    rec = P.op("scalar", g)
                    sv = P.inc(rec, "S", 1)
                    P.wait("tensor", "S", sv)
                    for q4 in range(4):
                        def ftr(t, q4=q4):
                            return t.transpose(
                                out=pp[1][:, q4 * 64:(q4 + 1) * 64].bitcast(BF16),
                                in_=vtmp[:, q4 * 128:(q4 + 1) * 128],
                                identity=C["idn_bf"][:])
                        rec = P.op("tensor", ftr)
                    tv2 = P.inc(rec, "T", 1)
                    P.wait("vector", "T", tv2)
                    def gv(v, hc=hc, tt=tt):
                        return v.tensor_copy(
                            out=v65[:].rearrange("p (sc h e) -> p sc h e", sc=16, h=H8)[
                                :, tt * 4:(tt + 1) * 4, 2 * hc:2 * hc + 2, :],
                            in_=pp[1][:, 0:256].bitcast(BF16).rearrange(
                                "p (sc h e) -> p sc h e", sc=4, h=2))
                    rec = P.op("vector", gv)
                    vv = P.inc(rec, "V", 1)
                    P.wait("tensor", "V", vv)
                else:
                    dstf = qf if kind == "q" else kf
                    ct, st_, scl = (tabs[0], tabs[1], qs2) if kind == "q" else (tabs[2], tabs[3], ks2)
                    P.wait("scalar", "T", tm)
                    rec = P.op("scalar", lambda s: s.activation(out=qraw[:], in_=pp[0][:],
                                                                func=AF.Copy))
                    P.selfsync("scalar")
                    rec = P.op("scalar", lambda s: s.activation(out=qsq[:], in_=qraw[:],
                                                                func=AF.Square))
                    sv = P.inc(rec, "S", 1)
                    P.wait("tensor", "S", sv)
                    rec = P.op("tensor", lambda t: t.matmul(out=pp[1][0:2, :], lhsT=mask2[:],
                                                            rhs=qsq[:], start=True, stop=True))
                    tv2 = P.inc(rec, "T", 1)
                    P.wait("scalar", "T", tv2)
                    rec = P.op("scalar", lambda s: s.activation(out=rn2[:], in_=pp[1][0:2, :],
                                                                func=AF.Sqrt))
                    sv2 = P.inc(rec, "S", 1)
                    P.wait("vector", "S", sv2)
                    P.op("vector", lambda v: v.tensor_scalar(
                        out=rn2[:], in0=rn2[:], scalar1=1e-12, scalar2=None, op0=ALU.max))
                    P.selfsync("vector")
                    P.op("vector", lambda v: v.reciprocal(out=rn2[:], in_=rn2[:]))
                    P.selfsync("vector")
                    rec = P.op("vector", lambda v: v.tensor_copy(out=rn2b[:], in_=rn2[:]))
                    vv = P.inc(rec, "V", 1)
                    P.wait("tensor", "V", vv)
                    rec = P.op("tensor", lambda t: t.matmul(out=pp[2][:], lhsT=bc2[:],
                                                            rhs=rn2b[:], start=True, stop=True))
                    tv3 = P.inc(rec, "T", 1)
                    P.wait("vector", "T", tv3)
                    rec = P.op("vector", lambda v: v.tensor_mul(out=qnf[:], in0=qraw[:],
                                                                in1=pp[2][:]))
                    P.selfsync("vector")
                    rec = P.op("vector", lambda v: v.tensor_copy(out=qnb[:], in_=qnf[:]))
                    vv2 = P.inc(rec, "V", 1)
                    P.wait("tensor", "V", vv2)
                    rec = P.op("tensor", lambda t: t.matmul(out=pp[3][:], lhsT=Rp[:],
                                                            rhs=qnb[:], start=True, stop=True))
                    tv4 = P.inc(rec, "T", 1)
                    def f1(v, ct=ct, t0=t0):
                        return v.tensor_mul(out=t1[:], in0=qnf[:], in1=ct[:, t0:t0 + SL])
                    P.op("vector", f1)
                    P.wait("vector", "T", tv4)
                    def f2_(v, st_=st_, t0=t0):
                        return v.tensor_mul(out=t2[:], in0=pp[3][:], in1=st_[:, t0:t0 + SL])
                    P.op("vector", f2_)
                    P.selfsync("vector")
                    def f3(v, dstf=dstf, hc=hc, t0=t0):
                        return v.tensor_add(out=dstf[:, hc * T + t0: hc * T + t0 + SL],
                                            in0=t1[:], in1=t2[:])
                    rec = P.op("vector", f3)
                    vv3 = P.inc(rec, "V", 1)
                    P.wait("tensor", "V", vv3)
                    P.wait("scalar", "V", vv3)
            tq_after[fc] = P.counts["T"]
        P.barrier(("tensor", "vector", "scalar", "gpsimd"))

    with contextlib.ExitStack() as s2:
        sb = lambda n, sh, d=F32: s2.enter_context(P.nc.sbuf_tensor(n, sh, d))
        SC = sb(f"d_SC{l}", [128, 1024])
        msk = [sb(f"d_msk{l}_{q}", [128, 1024]) for q in range(4)]
        nrx = sb(f"d_nrx{l}", [128, 1])
        rsum = sb(f"d_rsum{l}", [128, 1])
        PN = sb(f"d_PN{l}", [128, 1024], BF16)
        PTsb = sb(f"d_PTsb{l}", [128, 8 * SL], BF16)
        otmp = sb(f"d_otmp{l}", [64, SL], BF16)
        for q in range(4):
            def bm(g, q=q):
                g.memset(msk[q][:], 0.0)
                g.affine_select(out=msk[q][:], in_=msk[q][:], pattern=[[-1, 1024]],
                                compare_op=ALU.is_ge, fill=-3e38,
                                base=q * 128 + 512, channel_multiplier=1)
                return g.affine_select(out=msk[q][:], in_=msk[q][:], pattern=[[1, 1024]],
                                       compare_op=ALU.is_ge, fill=-3e38,
                                       base=-(q * 128), channel_multiplier=-1)
            rec = P.op("gpsimd", bm)
        gm = P.inc(rec, "G", 1)
        P.wait("vector", "G", gm)
        for h in range(H8):
            hc, ho = h // 2, (h % 2) * 64
            for w in range(4):
                qt0 = w * SL
                kc0 = 0 if w > 0 else 4
                for qc in range(4):
                    for kt in range(2):
                        if w == 0 and kt == 0:
                            continue
                        def f(t, h=h, hc=hc, ho=ho, w=w, qc=qc, kt=kt, qt0=qt0):
                            ks = (w - 1) * SL + kt * SL
                            return t.matmul(
                                out=pp[kt][:],
                                lhsT=qf[ho:ho + 64, hc * T + qt0 + qc * 128: hc * T + qt0 + (qc + 1) * 128],
                                rhs=kf[ho:ho + 64, hc * T + ks: hc * T + ks + SL],
                                start=True, stop=True)
                        rec = P.op("tensor", f)
                    tm = P.inc(rec, "T", 1)
                    P.wait("vector", "T", tm)
                    f0 = SL if (w == 0) else 0
                    for kt in range(2):
                        if w == 0 and kt == 0:
                            continue
                        def g(v, kt=kt, qc=qc):
                            return v.tensor_add(out=SC[:, kt * SL:(kt + 1) * SL],
                                                in0=pp[kt][:],
                                                in1=msk[qc][:, kt * SL:(kt + 1) * SL])
                        rec = P.op("vector", g)
                    vv = P.inc(rec, "V", 1)
                    P.wait("tensor", "V", vv)
                    def rmx(v, f0=f0):
                        return v.reduce_max(out=nrx[:], in_=SC[:, f0:1024], axis=AX.X,
                                            negate=True)
                    rec = P.op("vector", rmx)
                    vv2 = P.inc(rec, "V", 1)
                    P.selfsync("vector")
                    P.wait("scalar", "V", vv2)
                    def ex(s, f0=f0):
                        return s.activation(out=PN[:, f0:1024], in_=SC[:, f0:1024],
                                            func=AF.Exp, bias=nrx[:], accum_out=rsum[:])
                    rec = P.op("scalar", ex)
                    sv = P.inc(rec, "S", 1)
                    P.wait("vector", "S", sv)
                    P.op("vector", lambda v: v.reciprocal(out=rsum[:], in_=rsum[:]))
                    P.selfsync("vector")
                    def nm(v, f0=f0):
                        return v.tensor_scalar(out=PN[:, f0:1024], in0=PN[:, f0:1024],
                                               scalar1=rsum[:], scalar2=None,
                                               op0=ALU.mult)
                    rec = P.op("vector", nm)
                    vv3 = P.inc(rec, "V", 1)
                    P.wait("tensor", "V", vv3)
                    for kc in range(kc0, 8):
                        def ftr(t, kc=kc):
                            return t.transpose(
                                out=pp[2][:, (kc % 4) * 64:((kc % 4) + 1) * 64].bitcast(BF16),
                                in_=PN[:, kc * 128:(kc + 1) * 128],
                                identity=C["idn_bf"][:])
                        rec = P.op("tensor", ftr)
                        if kc % 4 == 3:
                            tv = P.inc(rec, "T", 1)
                            P.wait("vector", "T", tv)
                            def gev(v, kc=kc, qc=qc):
                                return v.tensor_copy(
                                    out=PTsb[:].rearrange("p (kc q) -> p kc q", kc=8)[
                                        :, kc - 3:kc + 1, qc * 128:(qc + 1) * 128],
                                    in_=pp[2][:, 0:256].bitcast(BF16).rearrange(
                                        "p (kc q) -> p kc q", kc=4))
                            rec = P.op("vector", gev)
                            vv4 = P.inc(rec, "V", 1)
                            P.wait("tensor", "V", vv4)
                for kc in range(kc0, 8):
                    sglob = 4 * w - 4 + kc
                    def f(t, h=h, kc=kc, sglob=sglob, kc0=kc0):
                        return t.matmul(
                            out=pp[3][0:64, :],
                            lhsT=v65[:, sglob * (H8 * 64) + h * 64: sglob * (H8 * 64) + (h + 1) * 64],
                            rhs=PTsb[:, kc * SL:(kc + 1) * SL],
                            start=(kc == kc0), stop=(kc == 7))
                    rec = P.op("tensor", f)
                to = P.inc(rec, "T", 1)
                P.wait("vector", "T", to)
                if ho == 0:
                    def ge(v, hc=hc, qt0=qt0):
                        return v.tensor_copy(out=o_bf[0:64, hc * T + qt0: hc * T + qt0 + SL],
                                             in_=pp[3][0:64, :])
                    rec = P.op("vector", ge)
                    ve = P.inc(rec, "V", 1)
                    P.wait("tensor", "V", ve)
                else:
                    P.wait("vector", "DOB")
                    rec = P.op("vector", lambda v: v.tensor_copy(out=otmp[:],
                                                                 in_=pp[3][0:64, :]))
                    ve = P.inc(rec, "V", 1)
                    P.wait("gpsimd", "V", ve)
                    P.dma("gpsimd", o_bf[64:128, hc * T + qt0: hc * T + qt0 + SL],
                          otmp[:], sem="DOB")
                    P.wait("tensor", "V", ve)
        P.wait("tensor", "DOB")
        P.barrier(("tensor", "vector", "scalar", "gpsimd"))

    with contextlib.ExitStack() as s2b:
        sb = lambda n, sh, d=F32: s2b.enter_context(P.nc.sbuf_tensor(n, sh, d))
        wo_bf = sb(f"d_wo{l}", [128, 4 * 1024], BF16)
        P.dma("gpsimd", wo_bf[:].rearrange("p (c f) -> p c f", c=4),
              aps["wo"][l].rearrange("(c p) f -> p c f", p=128), sem="DG")
        P.wait("tensor", "DG")
        for fc in range(DC):
            for tt in range(NSL):
                t0 = tt * SL
                for ec in range(4):
                    def f(t, fc=fc, ec=ec, t0=t0):
                        return t.matmul(out=pp[4][:],
                                        lhsT=wo_bf[:, ec * 1024 + fc * 128: ec * 1024 + (fc + 1) * 128],
                                        rhs=o_bf[:, ec * T + t0: ec * T + t0 + SL],
                                        start=(ec == 0), stop=(ec == 3))
                    rec = P.op("tensor", f)
                tw = P.inc(rec, "T", 1)
                P.wait("vector", "T", tw)
                def fz(v, fc=fc, t0=t0):
                    return v.tensor_add(out=zb[:, fc * TW + 1 + t0: fc * TW + 1 + t0 + SL],
                                        in0=pp[4][:],
                                        in1=zT[:, fc * TW + 1 + t0: fc * TW + 1 + t0 + SL])
                P.op("vector", fz)
                def fr(v, fc=fc, t0=t0):
                    return v.tensor_add(out=zT[:, fc * TW + 1 + t0: fc * TW + 1 + t0 + SL],
                                        in0=pp[4][:],
                                        in1=zT[:, fc * TW + 1 + t0: fc * TW + 1 + t0 + SL])
                rec = P.op("vector", fr)
                vr = P.inc(rec, "V", 1)
                P.wait("tensor", "V", vr)
        P.barrier(("tensor", "vector", "scalar", "gpsimd"))
    if dbg is not None and l == DBG_LAYER:
        P.barrier()
        P.dma("sync", dbg["qf0"], qf[:], sem="DO")
        P.dma("sync", dbg["kf0"], kf[:], sem="DO")
        P.dma("sync", dbg["v650"], v65[:], sem="DO")
        P.dma("sync", dbg["obf0"], o_bf[:], sem="DO")
        P.dma("sync", dbg["zattn0"], zT[:], sem="DO")
        for e_ in ("sync", "tensor", "vector", "scalar", "gpsimd"):
            P.wait(e_, "DO")
    P.barrier()
    attn_scope.close()

    if getattr(P, "_phase", None) is not None:
        P.set_phase(f"d_ffn{l}")
    HIDW = [128] * 21 + [rag]
    with contextlib.ExitStack() as s3:
        sb = lambda n, sh, d=F32: s3.enter_context(P.nc.sbuf_tensor(n, sh, d))
        hn = sb(f"d_hn{l}", [128, DC * T], BF16)
        gcol = sb(f"d_gcol{l}", [128, 16])
        b1acol = sb(f"d_b1a{l}", [128, NHC])
        b1gcol = sb(f"d_b1g{l}", [128, NHC])
        b2col = sb(f"d_b2c{l}", [128, 8])
        lnscratch = contextlib.ExitStack()
        sbl = lambda n, sh, d=F32: lnscratch.enter_context(P.nc.sbuf_tensor(n, sh, d))
        gb8 = sbl(f"d_gb8{l}", [8, 128])
        b1s = sbl(f"d_b1s{l}", [21, 128])
        b1t = sbl(f"d_b1t{l}", [1, 128])
        rows = {k: sbl(f"d_f_lnrow_{k}{l}", [1, SL]) for k in ("negm", "r", "mr", "msq", "var")}
        xsq = sbl(f"d_f_xsq{l}", [128, SL], BF16)
        ta = sbl(f"d_f_ta{l}", [128, SL])
        P.dma("sync", gb8[:], aps["lng"][l].rearrange("(a j) -> a j", j=128), sem="DS")
        P.wait("tensor", "DS")
        rec = P.op("tensor", lambda t: t.transpose(out=pp[5][:, 0:8], in_=gb8[:],
                                                   identity=C["idn_f32"][0:8, 0:8]))
        t1v = P.inc(rec, "T", 1)
        P.wait("sync", "T", t1v)
        P.dma("sync", gb8[:], aps["lnb"][l].rearrange("(a j) -> a j", j=128), sem="DS")
        P.wait("tensor", "DS")
        rec = P.op("tensor", lambda t: t.transpose(out=pp[5][:, 8:16], in_=gb8[:],
                                                   identity=C["idn_f32"][0:8, 0:8]))
        tg = P.inc(rec, "T", 1)
        P.wait("vector", "T", tg)
        rec = P.op("vector", lambda v: v.tensor_copy(out=gcol[:], in_=pp[5][:, 0:16]))
        vg = P.inc(rec, "V", 1)
        P.wait("tensor", "V", vg)
        for (base, dstc) in ((0, b1acol), (HID, b1gcol)):
            P.wait("sync", "T")
            P.dma("sync", b1s[:], aps["b1"][l, base:base + 21 * 128].rearrange(
                "(a j) -> a j", j=128), sem="DS")
            P.dma("sync", b1t[:, 0:rag], aps["b1"][l, base + 21 * 128:base + HID].rearrange(
                "(a j) -> a j", a=1), sem="DS")
            P.wait("tensor", "DS")
            P.op("tensor", lambda t: t.transpose(out=pp[5][:, 0:21], in_=b1s[:],
                                                 identity=C["idn_f32"][0:21, 0:21]))
            rec = P.op("tensor", lambda t: t.transpose(
                out=pp[6][0:rag, 0:1], in_=b1t[:, 0:rag],
                identity=C["idn_f32"][0:1, 0:1]))
            tb1 = P.inc(rec, "T", 1)
            P.wait("vector", "T", tb1)
            def gv(v, dstc=dstc):
                return v.tensor_copy(out=dstc[:, 0:21], in_=pp[5][:, 0:21])
            P.op("vector", gv)
            def gv2(v, dstc=dstc):
                return v.tensor_copy(out=dstc[0:rag, 21:22], in_=pp[6][0:rag, 0:1])
            rec = P.op("vector", gv2)
            vb1 = P.inc(rec, "V", 1)
            P.wait("tensor", "V", vb1)
        P.wait("sync", "T")
        P.dma("sync", gb8[:], aps["b2"][l].rearrange("(a j) -> a j", j=128), sem="DS")
        P.wait("tensor", "DS")
        rec = P.op("tensor", lambda t: t.transpose(out=pp[5][:, 0:8], in_=gb8[:],
                                                   identity=C["idn_f32"][0:8, 0:8]))
        tb2 = P.inc(rec, "T", 1)
        P.wait("vector", "T", tb2)
        rec = P.op("vector", lambda v: v.tensor_copy(out=b2col[:], in_=pp[5][:, 0:8]))
        vb2 = P.inc(rec, "V", 1)
        P.wait("tensor", "V", vb2)

        for i in range(NSL):
            t0 = i * SL
            for c in range(DC):
                def f(t, c=c, t0=t0):
                    return t.matmul(out=pp[2][0:1, :], lhsT=C["onescol_bf"][:],
                                    rhs=zsl(c, t0, shifted=True),
                                    start=(c == 0), stop=(c == DC - 1))
                rec = P.op("tensor", f)
            ts1 = P.inc(rec, "T", 1)
            for c in range(DC):
                def f(s, c=c, t0=t0):
                    return s.activation(out=xsq[:], in_=zsl(c, t0, shifted=True),
                                        func=AF.Square)
                rec = P.op("scalar", f)
                sv = P.inc(rec, "S", 1)
                P.wait("tensor", "S", sv)
                def f2(t, c=c):
                    return t.matmul(out=pp[3][0:1, :], lhsT=C["onescol_bf"][:],
                                    rhs=xsq[:], start=(c == 0), stop=(c == DC - 1))
                rec2 = P.op("tensor", f2)
                ts2 = P.inc(rec2, "T", 1)
                P.wait("scalar", "T", ts2)
            P.wait("scalar", "T", ts1)
            rec = P.op("scalar", lambda s: s.activation(
                out=rows["negm"][:], in_=pp[2][0:1, :], func=AF.Copy, scale=-1.0 / D))
            P.inc(rec, "S", 1)
            P.selfsync("scalar")
            rec = P.op("scalar", lambda s: s.activation(
                out=rows["msq"][:], in_=rows["negm"][:], func=AF.Square))
            smsq = P.inc(rec, "S", 1)
            P.wait("vector", "S", smsq)
            P.wait("vector", "T", ts2)
            rec = P.op("vector", lambda v: v.scalar_tensor_tensor(
                out=rows["var"][:], in0=pp[3][0:1, :], scalar=1.0 / D,
                in1=rows["msq"][:], op0=ALU.mult, op1=ALU.subtract))
            vvar = P.inc(rec, "V", 1)
            P.wait("scalar", "V", vvar)
            rec = P.op("scalar", lambda s: s.activation(
                out=rows["var"][:], in_=rows["var"][:], func=AF.Sqrt, bias=C["epscol"][:]))
            sstd = P.inc(rec, "S", 1)
            P.wait("vector", "S", sstd)
            P.op("vector", lambda v: v.reciprocal(out=rows["r"][:], in_=rows["var"][:]))
            P.selfsync("vector")
            rec = P.op("vector", lambda v: v.tensor_mul(
                out=rows["mr"][:], in0=rows["negm"][:], in1=rows["r"][:]))
            vfin = P.inc(rec, "V", 1)
            if dbg is not None and "rows0x" in dbg and l == DBG_LAYER and i == 0:
                P.wait("sync", "V", vfin)
                P.dma("sync", dbg["rows0"][0:1, :], rows["negm"][:], sem="DO")
                P.dma("sync", dbg["rows0"][1:2, :], rows["r"][:], sem="DO")
                dro = P.counts["DO"]
                P.wait("vector", "DO", dro)
                P.wait("tensor", "DO", dro)
            P.wait("tensor", "V", vfin)
            P.op("tensor", lambda t: t.matmul(
                out=pp[2][:], lhsT=C["ones1_f32"][:], rhs=rows["r"][:],
                start=True, stop=True))
            rec = P.op("tensor", lambda t: t.matmul(
                out=pp[3][:], lhsT=C["ones1_f32"][:], rhs=rows["mr"][:],
                start=True, stop=True))
            tb = P.inc(rec, "T", 1)
            P.wait("vector", "T", tb)
            for c in range(DC):
                off = 0 if c >= 4 else 1
                def f(v, c=c, t0=t0, off=off):
                    return v.tensor_mul(
                        out=ta[:],
                        in0=zT[:, c * TW + off + t0: c * TW + off + t0 + SL],
                        in1=pp[2][:])
                P.op("vector", f)
                def f2(v):
                    return v.tensor_add(out=ta[:], in0=ta[:], in1=pp[3][:])
                rec = P.op("vector", f2)
                if dbg is not None and "u0x" in dbg and l == DBG_LAYER and i == 0 and c == 0:
                    vu = P.inc(rec, "V", 1)
                    P.wait("sync", "V", vu)
                    P.dma("sync", dbg["u0"], ta[:], sem="DO")
                    duo = P.counts["DO"]
                    P.wait("vector", "DO", duo)
                def f3(v, c=c, t0=t0):
                    return v.tensor_scalar(out=hn[:, c * T + t0: c * T + t0 + SL],
                                           in0=ta[:], scalar1=gcol[:, c:c + 1],
                                           scalar2=gcol[:, 8 + c:9 + c],
                                           op0=ALU.mult, op1=ALU.add)
                rec = P.op("vector", f3)
            vap = P.inc(rec, "V", 1)
            P.wait("tensor", "V", vap)
        P.barrier(("tensor", "vector", "scalar", "gpsimd"))
        lnscratch.close()

        w2bf = sb(f"d_w2bf{l}", [128, NHC * 1024], BF16)
        mm1scratch = contextlib.ExitStack()
        sbm = lambda n, sh, d=F32: mm1scratch.enter_context(P.nc.sbuf_tensor(n, sh, d))
        w1cb = [sbm(f"d_w1cb{l}_{i}", [128, 8 * 128], BF16) for i in range(4)]
        sg = sbm(f"d_sg{l}", [128, 2 * SL])
        hgt = [sbm(f"d_hgt{l}_{i}", [128, SL], BF16) for i in range(2)]
        for c in range(NHC):
            wdt = HIDW[c]
            P.dma("gpsimd", w2bf[0:wdt, c * 1024:(c + 1) * 1024],
                  aps["w2"][l, c * 128:c * 128 + wdt, :], sem="DG")
        wq2done = P.counts["DG"]
        P.wait("tensor", "DG", wq2done)

        w1_done = {}
        def dma_w1(c_, par):
            wdt_ = HIDW[c_]
            for (k, base) in ((0, 0), (1, HID)):
                inap = aps["w1"][l][:, base + c_ * 128: base + c_ * 128 + wdt_].rearrange(
                    "(dc p) j -> p dc j", p=128)
                v = P.dma("gpsimd", w1cb[2 * par + k][:, 0:8 * wdt_].rearrange(
                    "p (dc j) -> p dc j", dc=DC), inap, sem="DW1" + str(par))
            w1_done[c_] = ("DW1" + str(par), v)

        hgd = {}
        tmm_after = {}
        mm1v = {}
        mm1s = {}
        dma_w1(0, 0)
        for c in range(NHC):
            par = c % 2
            if c + 1 < NHC:
                if c >= 1:
                    P.wait("gpsimd", "T", tmm_after[c - 1])
                dma_w1(c + 1, (c + 1) % 2)
            wdt = HIDW[c]
            P.wait("tensor", w1_done[c][0], w1_done[c][1])
            for tt in range(NSL):
                t0 = tt * SL
                idx = c * NSL + tt
                par2 = idx % 2
                bk0, bk1 = (pp[0], pp[1]) if par2 == 0 else (pp[2], pp[3])
                if idx >= 2:
                    P.wait("tensor", "V", mm1v[idx - 2])
                    P.wait("tensor", "S", mm1s[idx - 2])
                for (psumt, k) in ((bk0, 0), (bk1, 1)):
                    for dc in range(DC):
                        def f(t, psumt=psumt, k=k, dc=dc, t0=t0, par=par, wdt=wdt):
                            return t.matmul(
                                out=psumt[0:wdt, :],
                                lhsT=w1cb[2 * par + k][:, dc * wdt:(dc + 1) * wdt],
                                rhs=hn[:, dc * T + t0: dc * T + t0 + SL],
                                start=(dc == 0), stop=(dc == DC - 1))
                        rec = P.op("tensor", f)
                tmm = P.inc(rec, "T", 1)
                P.wait("scalar", "T", tmm)
                if idx >= 2:
                    P.wait("scalar", "V", mm1v[idx - 2])
                def fg(s, c=c, wdt=wdt, bk1=bk1, par2=par2):
                    return s.activation(out=sg[0:wdt, par2 * SL:(par2 + 1) * SL],
                                        in_=bk1[0:wdt, :],
                                        func=AF.Gelu, bias=b1gcol[0:wdt, c:c + 1])
                rec = P.op("scalar", fg)
                mm1s[idx] = ssi = P.inc(rec, "S", 1)
                P.wait("vector", "S", ssi)
                if idx >= 2:
                    P.wait("vector", "DHG" + str(par2), hgd[idx - 2][1])
                def fh(v, c=c, wdt=wdt, par2=par2, bk0=bk0):
                    return v.scalar_tensor_tensor(
                        out=hgt[par2][0:wdt, :],
                        in0=bk0[0:wdt, :], scalar=b1acol[0:wdt, c:c + 1],
                        in1=sg[0:wdt, par2 * SL:(par2 + 1) * SL],
                        op0=ALU.add, op1=ALU.mult)
                rec = P.op("vector", fh)
                mm1v[idx] = vg_ = P.inc(rec, "V", 1)
                P.wait("gpsimd", "V", vg_)
                hgd[idx] = ("DHG" + str(par2), P.dma(
                    "gpsimd", aps["hgscr2"][c * 128:c * 128 + wdt, t0:t0 + SL],
                    hgt[par2][0:wdt, :], sem="DHG" + str(par2)))
            tmm_after[c] = P.counts["T"]
        P.wait("tensor", "V")
        P.wait("tensor", "S")
        P.wait("sync", "DHG0")
        P.wait("sync", "DHG1")
        mm1scratch.close()
        hgtt = sb(f"d_hgtt{l}", [128, NHC * SL], BF16)
        tprev = [None]
        for tt in range(NSL):
            t0 = tt * SL
            if tprev[0] is not None:
                P.wait("sync", "T", tprev[0])
            hgld = P.dma("sync", hgtt[:].rearrange("p (c t) -> p c t", c=NHC),
                         aps["hgscr2"][:, t0:t0 + SL].rearrange("(c p) t -> p c t", p=128),
                         sem="DHGR")
            P.wait("tensor", "DHGR", hgld)
            for c in range(NHC):
                wdt = HIDW[c]
                for fc in range(DC):
                    def f(t, c=c, fc=fc, wdt=wdt):
                        return t.matmul(
                            out=pp[fc][:],
                            lhsT=w2bf[0:wdt, c * 1024 + fc * 128: c * 1024 + (fc + 1) * 128],
                            rhs=hgtt[0:wdt, c * SL:(c + 1) * SL],
                            start=(c == 0), stop=(c == NHC - 1))
                    rec = P.op("tensor", f)
            tm2 = P.inc(rec, "T", 1)
            tprev[0] = tm2
            P.wait("vector", "T", tm2)
            for fc in range(DC):
                def fz(v, fc=fc, t0=t0):
                    return v.scalar_tensor_tensor(
                        out=zb[:, fc * TW + 1 + t0: fc * TW + 1 + t0 + SL],
                        in0=pp[fc][:], scalar=b2col[:, fc:fc + 1],
                        in1=zT[:, fc * TW + 1 + t0: fc * TW + 1 + t0 + SL],
                        op0=ALU.add, op1=ALU.add)
                P.op("vector", fz)
                def fo(v, fc=fc, t0=t0):
                    return v.scalar_tensor_tensor(
                        out=zT[:, fc * TW + 1 + t0: fc * TW + 1 + t0 + SL],
                        in0=pp[fc][:], scalar=b2col[:, fc:fc + 1],
                        in1=zT[:, fc * TW + 1 + t0: fc * TW + 1 + t0 + SL],
                        op0=ALU.add, op1=ALU.add)
                rec = P.op("vector", fo)
            vz = P.inc(rec, "V", 1)
            P.wait("tensor", "V", vz)
        P.barrier()
        if dbg is not None and l == 0:
            P.barrier()
            P.dma("sync", dbg["hn0"], hn[:], sem="DO")
            P.dma("sync", dbg["zl0"], zT[:], sem="DO")
            for e_ in ("sync", "tensor", "vector", "scalar", "gpsimd"):
                P.wait(e_, "DO")
    P.barrier()



def dft_consts():
    n = np.arange(128, dtype=np.float64)
    M = np.cos(2 * np.pi * np.outer(n, n) / 128.0).astype(np.float32)
    S = np.sin(2 * np.pi * np.outer(n, n) / 128.0).astype(np.float32)
    return M, (-S).astype(np.float32)


def build_launch_a(detect_races=True):
    P = Prog(detect_races=detect_races, name="la")
    x = P.dram_in("x", [T, D], F32)
    wqkv = P.dram_in("wqkv", [D, 3 * D], F32)
    wout = P.dram_in("wout", [D, D], F32)
    dftc = P.dram_in("dftc", [128, 128], F32)
    dftsn = P.dram_in("dftsn", [128, 128], F32)
    xfm = P.dram_out("xfm", [128, DC * T], F32)

    C = emit_common_consts(P)
    XT = P.sbuf("XT", [128, DC * T], F32)
    XN = P.sbuf("XN", [128, DC * T], BF16)
    pp = [P.psum(f"pp{i}", [128, 512], F32) for i in range(6)]
    for s in ("T", "V", "S", "G", "DS", "DO"):
        P.sem(s)
    P.barrier()

    emit_load_xt_from_tokmajor(P, x, XT, C, pp[:4])

    with contextlib.ExitStack() as lnscope:
        sb = lambda n, s, d=F32: lnscope.enter_context(P.nc.sbuf_tensor(n, s, d))
        rows = {k: sb(f"lnrow_{k}", [1, SL]) for k in ("negm", "r", "mr", "msq", "var")}
        xsq = [sb(f"lnxsq{i}", [128, SL], BF16) for i in range(DC)]
        ta = sb("lnta", [128, SL])
        emit_ln(P, XT, XN, C, pp[0], pp[1], pp[2], pp[3], xsq, ta, rows)

    emit_block_a(P, XT, XN, C, wqkv, wout, dftc, dftsn, pp)

    P.dma("sync", xfm, XT[:], sem="DO")
    P.wait("sync", "DO")
    return P


def build_launch_b(detect_races=True, dbg_on=True):
    P = Prog(detect_races=detect_races, name="lb")
    xfm_in = P.dram_in("xfm", [128, DC * T], F32)
    rw = P.dram_in("rw", [8, D], F32)
    rb = P.dram_in("rb", [8], F32)
    w1 = P.dram_in("w1", [8, 2 * 4096, D], F32)
    b1 = P.dram_in("b1", [8, 2 * 4096], F32)
    w2 = P.dram_in("w2", [8, D, 4096], F32)
    b2 = P.dram_in("b2", [8, D], F32)
    xfm2 = P.dram_out("xfm2", [128, DC * T], F32)

    C = emit_common_consts(P)
    XT = P.sbuf("XT", [128, DC * T], F32)
    XN = P.sbuf("XN", [128, DC * T], BF16)
    pp = [P.psum(f"pp{i}", [128, 512], F32) for i in range(8)]
    for s in ("T", "V", "S", "G", "DS", "DG", "DO"):
        P.sem(s)
    P.dma("sync", XT[:], xfm_in[:, :], sem="DS")
    P.wait("vector", "DS")
    P.barrier()

    with contextlib.ExitStack() as lnscope:
        sb = lambda n, s, d=F32: lnscope.enter_context(P.nc.sbuf_tensor(n, s, d))
        rows = {k: sb(f"lnrow_{k}", [1, SL]) for k in ("negm", "r", "mr", "msq", "var")}
        xsq = [sb(f"lnxsq{i}", [128, SL], BF16) for i in range(DC)]
        ta = sb("lnta", [128, SL])
        emit_ln(P, XT, XN, C, pp[0], pp[1], pp[2], pp[3], xsq, ta, rows)

    dbg = None if not dbg_on else {
           "weT": P.dram_out("dbg_weT", [8, T], F32),
           "logitsT": P.dram_out("dbg_logitsT", [8, T], F32),
           "W8": P.dram_out("dbg_W8", [128, 16 * 8], F32),
           "hg": P.dram_out("dbg_hg", [128, 32 * 1024], BF16),
           "Wb": P.dram_out("dbg_Wb", [128, T], F32)}
    emit_block_b(P, XT, XN, C, rw, rb, w1, b1, w2, b2, pp, dbg=dbg)

    P.dma("sync", xfm2, XT[:], sem="DO")
    P.wait("sync", "DO")
    return P


def build_launch_c(detect_races=True, dbg_on=True):
    P = Prog(detect_races=detect_races, name="lc")
    xfm_in = P.dram_in("xfm", [128, DC * T], F32)
    wav = P.dram_in("wav", [16, 64, 64], F32)
    wv = P.dram_in("wv", [D, D], F32)
    wvb = P.dram_in("wvb", [D], F32)
    wo = P.dram_in("wo", [D, D], F32)
    wob = P.dram_in("wob", [D], F32)
    xfm2 = P.dram_out("xfm2", [128, DC * T], F32)

    C = emit_common_consts(P)
    XT = P.sbuf("XT", [128, DC * T], F32)
    XN = P.sbuf("XN", [128, DC * T], BF16)
    pp = [P.psum(f"pp{i}", [128, 512], F32) for i in range(8)]
    for sm in ("T", "V", "S", "G", "DS", "DG", "DO", "DOB"):
        P.sem(sm)
    P.dma("sync", XT[:], xfm_in[:, :], sem="DS")
    P.wait("vector", "DS")
    P.barrier()

    with contextlib.ExitStack() as lnscope:
        sb = lambda n, sh, d=F32: lnscope.enter_context(P.nc.sbuf_tensor(n, sh, d))
        rows = {k: sb(f"lnrow_{k}", [1, SL]) for k in ("negm", "r", "mr", "msq", "var")}
        xsq = [sb(f"lnxsq{i}", [128, SL], BF16) for i in range(DC)]
        ta = sb("lnta", [128, SL])
        emit_ln(P, XT, XN, C, pp[0], pp[1], pp[2], pp[3], xsq, ta, rows)

    dbg = None if not dbg_on else {
           "obf": P.dram_out("dbg_obf", [128, DC * T], BF16),
           "vx65e": P.dram_out("dbg_vx65e", [128, 16 * 16 * 65], BF16),
           "woT": P.dram_out("dbg_woT", [128, DC * 1024], BF16),
           "xwT": P.dram_out("dbg_xwT", [128, T], BF16),
           "vx65": P.dram_out("dbg_vx65", [128, 16 * 16 * 65], BF16),
           "sml": P.dram_out("dbg_sml", [1, 256], F32)}
    emit_block_c(P, XT, XN, C, wav, wv, wvb, wo, wob, pp, dbg=dbg)

    P.dma("sync", xfm2, XT[:], sem="DO")
    P.wait("sync", "DO")
    return P


def d_rotary_consts():
    hi = np.arange(0, 64, 2, dtype=np.float64)
    inv_freq = 1.0 / 10000.0 ** (hi / 64.0)
    t = np.arange(T, dtype=np.float64)
    fr = np.concatenate([np.outer(t, inv_freq), np.outer(t, inv_freq)], axis=1)
    base = (hi + 0.4 * 64) / (1.4 * 64)
    base = np.concatenate([base, base])
    power = (t - T // 2) / 256.0
    scale = base[None, :] ** power[:, None]
    cos, sin = np.cos(fr), np.sin(fr)
    sgn = np.where(np.arange(64) < 32, -1.0, 1.0)
    csq = (cos * scale * 8.0).T
    ssq = (sin * scale * 8.0).T * sgn[:, None]
    csk = (cos / scale).T
    ssk = (sin / scale).T * sgn[:, None]
    tile = lambda a: np.concatenate([a, a], axis=0).astype(np.float32)
    return tile(csq), tile(ssq), tile(csk), tile(ssk)


def build_launch_d(detect_races=True, dbg_on=True):
    P = Prog(detect_races=detect_races, name="ld")
    xfm_in = P.dram_in("xfm", [128, DC * T], F32)
    aps = {
        "wqkv": P.dram_in("wqkv", [2, D, 1536], F32),
        "qs": P.dram_in("qs", [2, 64], F32),
        "ks": P.dram_in("ks", [2, 64], F32),
        "wo": P.dram_in("wo", [2, 512, D], F32),
        "lng": P.dram_in("lng", [2, D], F32),
        "lnb": P.dram_in("lnb", [2, D], F32),
        "w1": P.dram_in("w1", [2, D, 5460], F32),
        "b1": P.dram_in("b1", [2, 5460], F32),
        "w2": P.dram_in("w2", [2, 2730, D], F32),
        "b2": P.dram_in("b2", [2, D], F32),
        "csq": P.dram_in("csq", [128, T], F32),
        "ssq": P.dram_in("ssq", [128, T], F32),
        "csk": P.dram_in("csk", [128, T], F32),
        "ssk": P.dram_in("ssk", [128, T], F32),
        "hgscr2": P.dram_tmp("hgscr2", [22 * 128, T], BF16),
    }
    out = P.dram_out("out", [T, D], F32)

    C = emit_common_consts(P)
    pp = [P.psum(f"pp{i}", [128, 512], F32) for i in range(8)]
    for sm in ("T", "V", "S", "G", "DS", "DG", "DO", "DOB", "DW1", "DT2", "DC2"):
        P.sem(sm)
    P.barrier()
    dbg = None if not dbg_on else {
           "z0": P.dram_out("dbg_z0", [128, DC * TW], F32),
           "qf0": P.dram_out("dbg_qf0", [128, 4 * T], BF16),
           "kf0": P.dram_out("dbg_kf0", [128, 4 * T], BF16),
           "v650": P.dram_out("dbg_v650", [128, 16 * 8 * 64], BF16),
           "obf0": P.dram_out("dbg_obf0", [128, 4 * T], BF16),
           "zattn0": P.dram_out("dbg_zattn0", [128, DC * TW], F32),
           "hn0": P.dram_out("dbg_hn0", [128, DC * T], BF16),
           "zl0": P.dram_out("dbg_zl0", [128, DC * TW], F32),
           "rows0": P.dram_out("dbg_rows0", [2, SL], F32),
           "u0": P.dram_out("dbg_u0", [128, SL], F32),
           "zfin": P.dram_out("dbg_zfin", [128, DC * TW], F32),
           "zb1": P.dram_out("dbg_zb1", [128, DC * TW], BF16),
           "zb1f": P.dram_out("dbg_zb1f", [128, DC * TW], F32),
           "ps47": P.dram_out("dbg_ps47", [128, 4 * SL], BF16)}
    emit_block_d(P, xfm_in, out, C, aps, pp, dbg=dbg)
    return P


def build_launch_full(detect_races=True):
    P = Prog(detect_races=detect_races, name="lf")
    x = P.dram_in("x", [T, D], F32)
    a_wqkv = P.dram_in("i_a_wqkv", [D, 3 * D], F32)
    a_wout = P.dram_in("i_a_wout", [D, D], F32)
    dftc = P.dram_in("dftc", [128, 128], F32)
    dftsn = P.dram_in("dftsn", [128, 128], F32)
    b_rw = P.dram_in("i_b_rw", [8, D], F32)
    b_rb = P.dram_in("i_b_rb", [8], F32)
    b_w1 = P.dram_in("i_b_w1", [8, 2 * 4096, D], F32)
    b_b1 = P.dram_in("i_b_b1", [8, 2 * 4096], F32)
    b_w2 = P.dram_in("i_b_w2", [8, D, 4096], F32)
    b_b2 = P.dram_in("i_b_b2", [8, D], F32)
    c_wav = P.dram_in("i_c_wav", [16, 64, 64], F32)
    c_wv = P.dram_in("i_c_wv", [D, D], F32)
    c_wvb = P.dram_in("i_c_wvb", [D], F32)
    c_wo = P.dram_in("i_c_wo", [D, D], F32)
    c_wob = P.dram_in("i_c_wob", [D], F32)
    d_aps = {
        "wqkv": P.dram_in("i_d_wqkv", [2, D, 1536], F32),
        "qs": P.dram_in("i_d_qs", [2, 64], F32),
        "ks": P.dram_in("i_d_ks", [2, 64], F32),
        "wo": P.dram_in("i_d_wo", [2, 512, D], F32),
        "lng": P.dram_in("i_d_lng", [2, D], F32),
        "lnb": P.dram_in("i_d_lnb", [2, D], F32),
        "w1": P.dram_in("i_d_w1", [2, D, 5460], F32),
        "b1": P.dram_in("i_d_b1", [2, 5460], F32),
        "w2": P.dram_in("i_d_w2", [2, 2730, D], F32),
        "b2": P.dram_in("i_d_b2", [2, D], F32),
        "csq": P.dram_in("csq", [128, T], F32),
        "ssq": P.dram_in("ssq", [128, T], F32),
        "csk": P.dram_in("csk", [128, T], F32),
        "ssk": P.dram_in("ssk", [128, T], F32),
        "hgscr2": P.dram_tmp("hgscr2", [22 * 128, T], BF16),
    }
    xfm_cd = P.dram_tmp("xfm_cd", [128, DC * T], F32)
    out = P.dram_out("out", [T, D], F32)

    C = emit_common_consts(P)
    pp = [P.psum(f"pp{i}", [128, 512], F32) for i in range(8)]
    for sm in ("T", "V", "S", "G", "DS", "DG", "DO", "DOB"):
        P.sem(sm)
    P.barrier()

    def lnpass(tag, XT, XN):
        with contextlib.ExitStack() as lnscope:
            sb = lambda n, sh, d=F32: lnscope.enter_context(P.nc.sbuf_tensor(n, sh, d))
            rows = {k: sb(f"{tag}lnrow_{k}", [1, SL]) for k in ("negm", "r", "mr", "msq", "var")}
            xsq = [sb(f"{tag}lnxsq{i}", [128, SL], BF16) for i in range(DC)]
            ta = sb(f"{tag}lnta", [128, SL])
            emit_ln(P, XT, XN, C, pp[0], pp[1], pp[2], pp[3], xsq, ta, rows)
        P.barrier()

    with contextlib.ExitStack() as abc:
        sb = lambda n, sh, d=F32: abc.enter_context(P.nc.sbuf_tensor(n, sh, d))
        XT = sb("XT", [128, DC * T])
        XN = sb("XN", [128, DC * T], BF16)
        P.set_phase("load")
        emit_load_xt_from_tokmajor(P, x, XT, C, pp[:4])
        P.barrier()
        P.set_phase("ln_a")
        lnpass("a", XT, XN)
        P.push_scope()
        P.set_phase("blk_a")
        emit_block_a(P, XT, XN, C, a_wqkv, a_wout, dftc, dftsn, pp[:6])
        P.barrier()
        P.pop_scope()
        P.set_phase("ln_b")
        lnpass("b", XT, XN)
        P.push_scope()
        P.set_phase("blk_b")
        emit_block_b(P, XT, XN, C, b_rw, b_rb, b_w1, b_b1, b_w2, b_b2, pp)
        P.barrier()
        P.pop_scope()
        P.set_phase("ln_c")
        lnpass("c", XT, XN)
        P.push_scope()
        P.set_phase("blk_c")
        emit_block_c(P, XT, XN, C, c_wav, c_wv, c_wvb, c_wo, c_wob, pp)
        P.barrier()
        P.pop_scope()
        P.set_phase("handoff")
        P.dma("sync", xfm_cd, XT[:], sem="DO")
        for e_ in ENGINES:
            P.wait(e_, "DO")
    P.barrier()
    P.set_phase("blk_d")
    emit_block_d(P, xfm_cd, out, C, d_aps, pp, dbg=None)
    P.set_phase(None)
    return P


_PROGS = {}


def _get_prog(name):
    if name not in _PROGS:
        builders = {"a": build_launch_a,
                    "b": lambda: build_launch_b(dbg_on=False),
                    "c": lambda: build_launch_c(dbg_on=False),
                    "d": lambda: build_launch_d(dbg_on=False),
                    "full": build_launch_full}
        P = builders[name]()
        _PROGS[name] = (P, P.finalize())
    return _PROGS[name]


FUSED = True


def kernel(**inputs):
    f32 = lambda k: np.ascontiguousarray(np.asarray(inputs[k], dtype=np.float32))
    x = f32("L0_attn_x")
    dftc, dftsn = dft_consts()
    csq, ssq, csk, ssk = d_rotary_consts()
    cores = list(range(NCORES))

    if FUSED:
        _, ncf = _get_prog("full")
        in_f = [{"x": x[b],
                 "i_a_wqkv": f32("L0_attn_w_qkv"), "i_a_wout": f32("L0_attn_w_out"),
                 "dftc": dftc, "dftsn": dftsn,
                 "i_b_rw": f32("L0_ffn_router_w"), "i_b_rb": f32("L0_ffn_router_b"),
                 "i_b_w1": f32("L0_ffn_w1"), "i_b_b1": f32("L0_ffn_b1"),
                 "i_b_w2": f32("L0_ffn_w2"), "i_b_b2": f32("L0_ffn_b2"),
                 "i_c_wav": f32("L1_attn_w_att_val"),
                 "i_c_wv": f32("L1_attn_wv_w"), "i_c_wvb": f32("L1_attn_wv_b"),
                 "i_c_wo": f32("L1_attn_out_w"), "i_c_wob": f32("L1_attn_out_b"),
                 "i_d_wqkv": f32("L1_ffn_wqkv"),
                 "i_d_qs": f32("L1_ffn_q_scale"), "i_d_ks": f32("L1_ffn_k_scale"),
                 "i_d_wo": f32("L1_ffn_wo"),
                 "i_d_lng": f32("L1_ffn_ln_g"), "i_d_lnb": f32("L1_ffn_ln_b"),
                 "i_d_w1": f32("L1_ffn_w1"), "i_d_b1": f32("L1_ffn_b1"),
                 "i_d_w2": f32("L1_ffn_w2"), "i_d_b2": f32("L1_ffn_b2"),
                 "csq": csq, "ssq": ssq, "csk": csk, "ssk": ssk} for b in cores]
        res = run_bass_kernel_spmd(ncf, in_f, core_ids=cores)
        return np.stack([np.asarray(res.results[b]["out"]) for b in cores], axis=0)

    _, nca = _get_prog("a")
    in_a = [{"x": x[b], "wqkv": f32("L0_attn_w_qkv"), "wout": f32("L0_attn_w_out"),
             "dftc": dftc, "dftsn": dftsn} for b in cores]
    res = run_bass_kernel_spmd(nca, in_a, core_ids=cores)
    xfm = [np.asarray(res.results[b]["xfm"]) for b in cores]

    _, ncb = _get_prog("b")
    in_b = [{"xfm": xfm[b], "rw": f32("L0_ffn_router_w"), "rb": f32("L0_ffn_router_b"),
             "w1": f32("L0_ffn_w1"), "b1": f32("L0_ffn_b1"),
             "w2": f32("L0_ffn_w2"), "b2": f32("L0_ffn_b2")} for b in cores]
    res = run_bass_kernel_spmd(ncb, in_b, core_ids=cores)
    xfm = [np.asarray(res.results[b]["xfm2"]) for b in cores]

    _, ncc = _get_prog("c")
    in_c = [{"xfm": xfm[b], "wav": f32("L1_attn_w_att_val"),
             "wv": f32("L1_attn_wv_w"), "wvb": f32("L1_attn_wv_b"),
             "wo": f32("L1_attn_out_w"), "wob": f32("L1_attn_out_b")} for b in cores]
    res = run_bass_kernel_spmd(ncc, in_c, core_ids=cores)
    xfm = [np.asarray(res.results[b]["xfm2"]) for b in cores]

    _, ncd = _get_prog("d")
    in_d = [{"xfm": xfm[b], "wqkv": f32("L1_ffn_wqkv"),
             "qs": f32("L1_ffn_q_scale"), "ks": f32("L1_ffn_k_scale"),
             "wo": f32("L1_ffn_wo"), "lng": f32("L1_ffn_ln_g"), "lnb": f32("L1_ffn_ln_b"),
             "w1": f32("L1_ffn_w1"), "b1": f32("L1_ffn_b1"),
             "w2": f32("L1_ffn_w2"), "b2": f32("L1_ffn_b2"),
             "csq": csq, "ssq": ssq, "csk": csk, "ssk": ssk} for b in cores]
    res = run_bass_kernel_spmd(ncd, in_d, core_ids=cores)
    return np.stack([np.asarray(res.results[b]["out"]) for b in cores], axis=0)
